# version 07
# baseline (speedup 1.0000x reference)
import numpy as np
import ml_dtypes
import concourse.bass as bass
import concourse.mybir as mybir
from concourse.bass_core_rust import IndirectOffsetOnAxis

F32 = mybir.dt.float32
BF16 = mybir.dt.bfloat16
U32 = mybir.dt.uint32
AF = mybir.ActivationFunctionType
OP = mybir.AluOpType
AX = mybir.AxisListType

B = 2
S = 2048
H = 1024
E = 8
KSEL = 512
FF = 4096
DH = 64
T_OWN = B * S
EPS = 1e-5
QSCALE = 1.0 / 32.0
NEG = -1e30
ENGINES = ["sync", "scalar", "vector", "tensor", "gpsimd"]


class Prog:
    def __init__(self, stage=4, ncore=8):
        self.nc = bass.Bass("TRN2", target_bir_lowering=False)
        self.stage = stage
        self.ncore = ncore
        self.t_all = ncore * T_OWN
        self.sem = {}
        self.cnt = {}
        self.t = {}
        self.ops = {e: [] for e in ENGINES}
        self.watermark = {e: {} for e in ENGINES}

    def mksem(self, *names):
        for n in names:
            self.sem[n] = self.nc.alloc_semaphore(n)
            self.cnt[n] = 0

    def op(self, eng, fn, waits=(), inc=None, inc_n=1):
        wl = {}
        wm = self.watermark[eng]
        for ev in waits:
            if ev is None:
                continue
            s, v = ev
            if v is None or wm.get(s, 0) >= v:
                continue
            wl[s] = max(wl.get(s, 0), v)
        for s, v in wl.items():
            wm[s] = v
        evo = None
        if inc is not None:
            self.cnt[inc] += inc_n
            evo = (inc, self.cnt[inc])
        self.ops[eng].append((fn, sorted(wl.items()), inc, inc_n))
        return evo

    def emit(self):
        with self.nc.Block() as blk:
            for eng_name in ENGINES:
                oplist = self.ops[eng_name]
                if not oplist:
                    continue

                def make(oplist=oplist):
                    def run(eng):
                        for fn, waits, inc, inc_n in oplist:
                            for s, v in waits:
                                eng.wait_ge(self.sem[s], v)
                            ins = fn(eng)
                            if inc is not None:
                                ins.then_inc(self.sem[inc], inc_n)
                    return run
                getattr(blk, eng_name)(make())


def build(stage=4, ncore=8, launch="full"):
    c = Prog(stage, ncore)
    c.launch = launch
    nc = c.nc
    t = c.t

    def dram(name, shape, dt, kind="Internal"):
        t[name] = nc.dram_tensor(name, shape, dt, kind=kind)

    dram("x0", [T_OWN, H], F32, "ExternalInput")
    wlist = [("Wg", [E, H]), ("W1", [E, H, FF]), ("b1", [E, FF]),
             ("W2", [E, FF, H]), ("b2", [E, H])]
    if launch in ("full", "A"):
        wlist += [("Wq", [DH, H]), ("bq", [DH, 1]), ("Wk", [DH, H]), ("bk", [DH, 1]),
                  ("Wv", [DH, H]), ("bv", [DH, 1])]
    for n, sh in wlist:
        dram(n, sh, F32, "ExternalInput")
    if launch == "A":
        dram("x1o", [T_OWN, H], F32, "ExternalOutput")
        dram("kvo", [3 * DH * T_OWN], BF16, "ExternalOutput")
    else:
        dram("y", [T_OWN, H], F32, "ExternalOutput")
    if launch in ("B", "B3", "Bdbg"):
        dram("kTf_in", [128, (c.t_all // 256) * 128], BF16, "ExternalInput")
        dram("vf_in", [128, (c.t_all // 128) * DH], BF16, "ExternalInput")
        dram("qT_in", [DH, T_OWN], BF16, "ExternalInput")
    if launch in ("full", "B", "Bdbg"):
        dram("xr2", [T_OWN, H], F32)
    if launch == "full":
        dram("kvin", [2 * DH * T_OWN], BF16)
        dram("kvout", [c.ncore, 2 * DH * T_OWN], BF16)
    dram("xr", [T_OWN, H], F32)
    dram("lnx", [T_OWN, H], F32)
    dram("idxd", [B, E, KSEL], U32)
    dram("gd", [B, E, KSEL], F32)
    t["ident"] = nc.inline_tensor(np.eye(128, dtype=np.float32), "identc")

    if launch == "Bdbg":
        dram("d_ex", [128, B * 4 * H], F32, "ExternalOutput")
        dram("d_h1", [128, B * 32 * 512], U32 if False else BF16, "ExternalOutput")
        dram("d_ext", [128, B * 8 * 512], BF16, "ExternalOutput")
    if c.stage < 4 or launch == "Bdbg":
        dram("d_lnx", [T_OWN, H], F32, "ExternalOutput")
        dram("d_sm", [B, E, S], F32, "ExternalOutput")
        dram("d_idx", [B, E, KSEL], U32, "ExternalOutput")
        dram("d_g", [B, E, KSEL], F32, "ExternalOutput")

    def sb(name, shape, dt):
        t[name] = nc.alloc_sbuf_tensor(name, shape, dt)

    has_attw = launch in ("full", "A")
    has_q = launch in ("full", "B", "B3", "Bdbg")
    sb("ident_sb", [128, 128], F32)
    sb("ones_sb", [1, 128], BF16)
    sb("WgT", [128, 8, E], F32)
    if has_attw:
        sb("WqT", [128, 8, DH], BF16)
        sb("WkT", [128, 8, DH], BF16)
        sb("WvT", [128, 8, DH], BF16)
        sb("bq32", [DH, 1], F32)
        sb("bk_sb", [DH, 1], F32)
        sb("bv_f", [1, DH], F32)
        sb("bv_sb", [1, DH], BF16)
        sb("kT_own", [DH, T_OWN], BF16)
    sb("b2row", [1, 2, H], BF16)
    sb("b1all", [128, E, 32], F32)
    sb("xt", [128, 4, H], F32)
    sb("sq", [128, H], F32)
    sb("stat", [128, 4, 8], F32)
    sb("epsc", [128, 1], F32)
    sb("dcol", [128, 2], F32)
    sb("lnT", [128, 8, 512], BF16)
    sb("lnT32", [128, 8, 128], F32)
    sb("smtok", [128, 4, E], F32)
    sb("idxcol", [128, B, 4], U32)
    sb("ccol", [128, B, 4], F32)
    sb("contrib", [128, 4, 512], F32)

    base = (nc.sbuf_base + 31) & ~31

    def at(name, shape, dt, off):
        t[name] = nc.alloc_sbuf_tensor_at(name, shape, dt, offset=base + off)

    at("wq_f", [DH, H], F32, 0)
    at("wk_f", [DH, H], F32, 4096)
    at("wv_f", [DH, H], F32, 8192)
    at("wg_f", [E, H], F32, 12288)
    at("h1sb", [128, B, 32, 512], BF16, 0)
    kt_bytes = (c.t_all // 256) * 128 * 2
    if has_q:
        at("kT_full", [128, (c.t_all // 256) * 128], BF16, 0)
        at("v_aug", [128, c.t_all // 128, DH + 1], BF16, kt_bytes)
    rA = max(65536, (kt_bytes + (c.t_all // 128) * (DH + 1) * 2) if has_q else 0, 16384)
    rA = (rA + 31) & ~31
    at("ex_raw", [128, B, 4, H], F32, rA)
    at("exT", [128, B, 8, 512], BF16, rA + 32768)
    at("qT_sb", [128, T_OWN], BF16, rA)
    if has_attw:
        at("v_own", [128, T_OWN // 128, DH], BF16, rA + 8192)
    if has_q:
        at("E_sb", [128, 2, 512], BF16, rA + 12288)
        at("yT", [128, 512], F32, rA + 14336)
        at("ytmp", [128, 2, DH + 1], F32, rA + 16384)
        at("ytok", [128, 4, DH], F32, rA + 16960)
    rB = rA + 49152
    at("smT", [E, S], F32, rB)
    at("smW", [E, S], F32, rB + 8192)
    at("Gsb", [E, KSEL], F32, rB + 16384)
    at("Isb", [E, KSEL], U32, rB + 18432)
    at("W1sb", [128, 2, 8, 512], BF16, rB)
    at("W2sb", [128, 2, 512], BF16, rB + 16384)
    need = base + rB + 20480
    assert need <= nc.sbuf_top, (need, nc.sbuf_top, nc.sbuf_base)

    for i in range(8):
        t[f"p{i}"] = nc.alloc_psum_tensor(f"p{i}", [128, 512], F32)

    c.mksem("stp", "b1s", "lnx", "tkw", "dsc", "w1s0", "w1s1", "w2s0", "w2s1",
            "b2s", "sg", "scc", "kvl", "ycw", "dbg", "sT", "sV", "sA",
            *[f"xt{i}" for i in range(4)], *[f"wb{i}" for i in range(4)],
            *[f"sb{i}" for i in range(4)])

    st = {"xt_war": [[] for _ in range(4)], "xt_bar": []}
    emit_setup(c, st)
    if launch in ("full", "A"):
        emit_ffn(c, st, 0)
        emit_attn_kv(c, st)
    if launch == "A":
        emit_out_a(c, st)
    if launch in ("full", "B", "B3", "Bdbg"):
        emit_attn_q(c, st)
        emit_copy_y(c, st)
        if launch != "B3":
            emit_ffn(c, st, 1)
    emit_final(c, st)
    c.emit()
    return c


def emit_setup(c, st):
    t = c.t
    evs = {}
    has_attw = c.launch in ("full", "A")
    c.op("sync", lambda s: s.dma_start(out=t["xr"][:, :], in_=t["x0"][:, :]),
         inc="stp", inc_n=16)
    wl = [("Wg", "wg_f")]
    if has_attw:
        wl = [("Wq", "wq_f"), ("Wk", "wk_f"), ("Wv", "wv_f")] + wl
    for w, f in wl:
        evs[f] = c.op("sync", lambda s, a=w, b=f: s.dma_start(
            out=t[b][:, :], in_=t[a][:, :]), inc="stp", inc_n=16)
    if has_attw:
        evs["bq"] = c.op("sync", lambda s: s.dma_start(
            out=t["bq32"][:, :], in_=t["bq"][:, :]), inc="stp", inc_n=16)
        c.op("sync", lambda s: s.dma_start(out=t["bk_sb"][:, :], in_=t["bk"][:, :]),
             inc="stp", inc_n=16)
        evs["bv"] = c.op("sync", lambda s: s.dma_start(
            out=t["bv_f"][:1, :], in_=t["bv"][:, :].rearrange("a o -> o a")),
            inc="stp", inc_n=16)
    evs["id"] = c.op("sync", lambda s: s.dma_start(
        out=t["ident_sb"][:, :], in_=t["ident"][:, :]), inc="stp", inc_n=16)
    for e in range(E):
        for fc in range(32):
            st["b1_ld"] = c.op("scalar", lambda s, e=e, fc=fc: s.dma_start(
                out=t["b1all"][:, e, fc:fc + 1],
                in_=t["b1"][e, fc * 128:(fc + 1) * 128].rearrange("(p o) -> p o", o=1)),
                inc="b1s", inc_n=16)
    stp_full = ("stp", c.cnt["stp"])
    st["stp_full"] = stp_full
    st["xt_bar"] = [stp_full]
    st["b1_ld"] = ("b1s", c.cnt["b1s"])
    if has_attw:
        st["bq_ok"] = c.op("vector", lambda v: v.tensor_single_scalar(
            t["bq32"][:, :], t["bq32"][:, :], QSCALE, OP.mult),
            waits=[stp_full], inc="sV")
        st["bv_ok"] = c.op("vector", lambda v: v.tensor_copy(
            t["bv_sb"][:1, :], t["bv_f"][:1, :]), waits=[stp_full], inc="sV")
    st["ones_ok"] = c.op("vector", lambda v: v.memset(t["ones_sb"][:1, :], 1.0), inc="sV")
    st["eps_ok"] = c.op("vector", lambda v: v.memset(t["epsc"][:, :], EPS), inc="sV")
    if "yT" in t:
        st["yT_ok"] = c.op("vector", lambda v: v.memset(t["yT"][:, :], 0.0), inc="sV")

    par = [0]
    pfree = [None, None]
    wT = []
    wtl = [("wg_f", "WgT", E)]
    if has_attw:
        wtl = [("wq_f", "WqT", DH), ("wk_f", "WkT", DH), ("wv_f", "WvT", DH)] + wtl
    for mat, dstn, p in wtl:
        np_in = DH if mat != "wg_f" else E
        for hc in range(8):
            pp = par[0]
            tr = c.op("tensor", lambda te, m=mat, hc=hc, pp=pp, q=np_in: te.transpose(
                t[f"p{6 + pp}"][:, :q], t[m][:, hc * 128:(hc + 1) * 128],
                t["ident_sb"][:q, :q]),
                waits=[stp_full, pfree[pp]], inc="sT")
            pfree[pp] = c.op("scalar", lambda s, d=dstn, hc=hc, pp=pp, q=p:
                             s.copy(t[d][:, hc, :], t[f"p{6 + pp}"][:, :q]),
                             waits=[tr], inc="sA")
            wT.append(pfree[pp])
            par[0] ^= 1
    st["wT_done"] = wT[-1]
    st["p67_free"] = pfree
    st["scat_last"] = {0: None, 1: None}
    st["gate_war"] = None
    st["smT_tr_war"] = None
    st["smT_war"] = None
    st["lnx_war"] = None


def emit_ln_group(c, st, src_name, b, grp, want_lnx, gate):
    t = c.t
    out = {}
    row0 = b * S + grp * 512
    ld, red, sqe = [], [], []
    for tl in range(4):
        w = list(st["xt_war"][tl]) + list(st["xt_bar"])
        ld.append(c.op("sync", lambda s, tl=tl: s.dma_start(
            out=t["xt"][:, tl, :],
            in_=t[src_name][row0 + tl * 128: row0 + (tl + 1) * 128, :]),
            waits=w, inc=f"xt{tl}", inc_n=16))
    for tl in range(4):
        red.append(c.op("vector", lambda v, tl=tl: v.reduce_sum(
            t["stat"][:, tl, 0:1], t["xt"][:, tl, :], axis=AX.X),
            waits=[ld[tl]], inc="sV"))
    for tl in range(4):
        sqe.append(c.op("scalar", lambda s, tl=tl: s.activation(
            t["sq"][:, :], t["xt"][:, tl, :], AF.Square,
            accum_out=t["stat"][:, tl, 1:2]),
            waits=[ld[tl], st.get("sq_war")], inc="sA"))
        st["sq_war"] = sqe[-1]
    var_ev = []
    for tl in range(4):
        m = c.op("vector", lambda v, tl=tl: v.tensor_single_scalar(
            t["stat"][:, tl, 2:3], t["stat"][:, tl, 0:1], 1.0 / H, OP.mult),
            waits=[red[tl]], inc="sV")
        msq = c.op("vector", lambda v, tl=tl: v.tensor_mul(
            t["stat"][:, tl, 3:4], t["stat"][:, tl, 2:3], t["stat"][:, tl, 2:3]),
            waits=[m], inc="sV")
        var_ev.append(c.op("vector", lambda v, tl=tl: v.scalar_tensor_tensor(
            t["stat"][:, tl, 4:5], t["stat"][:, tl, 1:2], 1.0 / H, t["stat"][:, tl, 3:4],
            op0=OP.mult, op1=OP.subtract), waits=[msq, sqe[tl]], inc="sV"))
    sd_ev = [c.op("scalar", lambda s, tl=tl: s.activation(
        t["stat"][:, tl, 5:6], t["stat"][:, tl, 4:5], AF.Sqrt, bias=t["epsc"][:, 0:1]),
        waits=[var_ev[tl], st["eps_ok"]], inc="sA") for tl in range(4)]
    ap_ev = []
    for tl in range(4):
        rc = c.op("vector", lambda v, tl=tl: v.reciprocal(
            t["stat"][:, tl, 6:7], t["stat"][:, tl, 5:6]), waits=[sd_ev[tl]], inc="sV")
        ap_ev.append(c.op("vector", lambda v, tl=tl: v.tensor_scalar(
            t["xt"][:, tl, :], t["xt"][:, tl, :], t["stat"][:, tl, 2:3],
            t["stat"][:, tl, 6:7], op0=OP.subtract, op1=OP.mult),
            waits=[rc], inc="sV"))
    out["apply"] = ap_ev
    lnx_w = [None] * 4
    if want_lnx:
        for tl in range(4):
            lnx_w[tl] = c.op("sync", lambda s, tl=tl: s.dma_start(
                out=t["lnx"][row0 + tl * 128: row0 + (tl + 1) * 128, :],
                in_=t["xt"][:, tl, :]),
                waits=[ap_ev[tl], st["lnx_war"]], inc=f"xt{tl}", inc_n=16)
        out["lnx_w"] = lnx_w
    evac = {}
    tr_last = {}
    pfree = list(st["p67_free"])
    par = 0
    for tl in range(4):
        for hc in range(8):
            tr = c.op("tensor", lambda te, tl=tl, hc=hc, pp=par: te.transpose(
                t[f"p{6 + pp}"][:, :128], t["xt"][:, tl, hc * 128:(hc + 1) * 128],
                t["ident_sb"][:, :]),
                waits=[ap_ev[tl], pfree[par], st["wT_done"],
                       st.get(("w2evac", 6 + par))], inc="sT")
            ev1 = c.op("scalar", lambda s, tl=tl, hc=hc, pp=par: s.copy(
                t["lnT"][:, hc, tl * 128:(tl + 1) * 128], t[f"p{6 + pp}"][:, :128]),
                waits=[tr, st.get("lnT_war")], inc="sA")
            if gate:
                ev1 = c.op("scalar", lambda s, tl=tl, hc=hc, pp=par: s.copy(
                    t["lnT32"][:, hc, :], t[f"p{6 + pp}"][:, :128]),
                    waits=[tr, st.get("gate_mm_war")], inc="sA")
            pfree[par] = ev1
            evac[(tl, hc)] = ev1
            tr_last[tl] = tr
            par ^= 1
        if gate:
            tile_in_b = grp * 4 + tl
            gm = None
            for hc in range(8):
                gm = c.op("tensor", lambda te, tl=tl, hc=hc: te.matmul(
                    t["p4"][:, 0:E], t["lnT32"][:, hc, :],
                    t["WgT"][:, hc, :], start=(hc == 0), stop=(hc == 7)),
                    waits=[evac[(tl, hc)], st.get("gate_war")],
                    inc="sT" if hc == 7 else None)
            st["gate_mm_war"] = gm
            ex = c.op("scalar", lambda s, tl=tl: s.activation(
                t["smtok"][:, tl, :], t["p4"][:, 0:E], AF.Exp,
                accum_out=t["stat"][:, tl, 7:8]), waits=[gm], inc="sA")
            st["gate_war"] = ex
            rcg = c.op("vector", lambda v, tl=tl: v.reciprocal(
                t["stat"][:, tl, 3:4], t["stat"][:, tl, 7:8]), waits=[ex], inc="sV")
            dv = c.op("vector", lambda v, tl=tl: v.tensor_single_scalar(
                t["smtok"][:, tl, :], t["smtok"][:, tl, :], t["stat"][:, tl, 3:4],
                OP.mult), waits=[rcg], inc="sV")
            str_ = c.op("tensor", lambda te, tl=tl: te.transpose(
                t["p5"][0:E, 0:128], t["smtok"][:, tl, :], t["ident_sb"][:, :]),
                waits=[dv, st.get("smT_tr_war"), st.get(("w2evac", 5))], inc="sT")
            smev = c.op("scalar", lambda s, tl=tl, tb=tile_in_b: s.copy(
                t["smT"][:, tb * 128:(tb + 1) * 128], t["p5"][0:E, :128]),
                waits=[str_, st.get("smT_war"), st.get("smT_war2")], inc="sA")
            st["smT_tr_war"] = smev
            out["smT_last"] = smev
    st["p67_free"] = pfree
    out["evac"] = evac
    for tl in range(4):
        st["xt_war"][tl] = [tr_last[tl], lnx_w[tl]]
    st["xt_bar"] = []
    return out


def emit_topk(c, st, b, call):
    t = c.t
    cp = c.op("vector", lambda v: v.tensor_copy(t["smW"][:, :], t["smT"][:, :]),
              waits=[st["smT_last_ev"]] + st.get("topk_war", []), inc="sV")
    st["smT_war"] = cp
    prev = cp
    for it in range(KSEL // 8):
        sl = slice(it * 8, it * 8 + 8)
        mx = c.op("vector", lambda v, sl=sl: v.max(out=t["Gsb"][:, sl], in_=t["smW"][:, :]),
                  waits=[prev], inc="sV")
        mi = c.op("vector", lambda v, sl=sl: v.max_index(
            t["Isb"][:, sl], t["Gsb"][:, sl], t["smW"][:, :]), waits=[mx], inc="sV")
        prev = c.op("vector", lambda v, sl=sl: v.match_replace(
            out=t["smW"][:, :], in_to_replace=t["Gsb"][:, sl], in_values=t["smW"][:, :],
            imm_value=NEG), waits=[mi], inc="sV")
    gs = c.op("vector", lambda v: v.tensor_single_scalar(
        t["Gsb"][:, :], t["Gsb"][:, :], 0.5, OP.mult), waits=[prev], inc="sV")
    iw = c.op("sync", lambda s: s.dma_start(out=t["idxd"][b, :, :], in_=t["Isb"][:, :]),
              waits=[prev], inc="tkw", inc_n=16)
    gw = c.op("sync", lambda s: s.dma_start(out=t["gd"][b, :, :], in_=t["Gsb"][:, :]),
              waits=[gs], inc="tkw", inc_n=16)
    st["topk_war"] = [iw, gw]
    if c.stage < 4 and call == 0:
        c.op("sync", lambda s: s.dma_start(
            out=t["d_sm"][b, :, :], in_=t["smT"][:, :]),
            waits=[prev], inc="dbg", inc_n=16)
        c.op("sync", lambda s: s.dma_start(out=t["d_idx"][b, :, :], in_=t["Isb"][:, :]),
             waits=[prev], inc="dbg", inc_n=16)
        c.op("sync", lambda s: s.dma_start(out=t["d_g"][b, :, :], in_=t["Gsb"][:, :]),
             waits=[gs], inc="dbg", inc_n=16)
        st["smT_war2"] = ("dbg", c.cnt["dbg"])
    return iw, gw


def emit_ffn(c, st, call):
    t = c.t
    scd = t["xr" if call == 0 else "xr2"]

    idx_ready = {}
    for b in range(B):
        for grp in range(4):
            ev = emit_ln_group(c, st, "xr", b, grp, want_lnx=True, gate=True)
            st["smT_last_ev"] = ev["smT_last"]
        idx_ready[b] = emit_topk(c, st, b, call)
    tkw_full = ("tkw", c.cnt["tkw"])
    lnx_all = [(f"xt{i}", c.cnt[f"xt{i}"]) for i in range(4)]
    st["lnx_war"] = None

    if c.stage < 2:
        st["ffn_done"] = [tkw_full]
        return

    gather_prev = [None, None]
    w1_free = [None, None]
    w2_free = [None, None]
    geluevs = {}
    ex_war = st.get("exT_war")
    scat_last = st["scat_last"]
    contrib_free = st.get("contrib_free", [None] * 4)
    for e in range(E):
        scat_prev = {0: scat_last[0], 1: scat_last[1]}
        scat_all = [(f"sb{i}", c.cnt[f"sb{i}"]) for i in range(4)]
        b2l = c.op("gpsimd", lambda g, e=e: g.dma_start(
            out=t["b2row"][:1, e % 2, :], in_=t["b2"][e:e + 1, :]),
            waits=[st.get(("b2free", e % 2))], inc="b2s", inc_n=16)
        icol, gcol = {}, {}
        for b in range(B):
            for sc in range(4):
                icol[(b, sc)] = c.op("scalar", lambda s, b=b, sc=sc, e=e: s.dma_start(
                    out=t["idxcol"][:, b, sc:sc + 1],
                    in_=t["idxd"][b, e, sc * 128:(sc + 1) * 128].rearrange("(p o) -> p o", o=1)),
                    waits=[tkw_full] + scat_all, inc="dsc", inc_n=16)
                gcol[(b, sc)] = c.op("scalar", lambda s, b=b, sc=sc, e=e: s.dma_start(
                    out=t["ccol"][:, b, sc:sc + 1],
                    in_=t["gd"][b, e, sc * 128:(sc + 1) * 128].rearrange("(p o) -> p o", o=1)),
                    waits=[tkw_full] + scat_all, inc="dsc", inc_n=16)
        dsc_full = ("dsc", c.cnt["dsc"])
        gat = {}
        prev_sg = ("sg", c.cnt["sg"])
        for b in range(B):
            for sc in range(4):
                w = [dsc_full, prev_sg, ex_war, st["wT_done"]] + lnx_all
                gat[(b, sc)] = c.op("gpsimd", lambda g, b=b, sc=sc: g.indirect_dma_start(
                    out=t["ex_raw"][:, b, sc, :], out_offset=None,
                    in_=t["lnx"][:, :],
                    in_offset=IndirectOffsetOnAxis(t["idxcol"][:, b, sc:sc + 1], 0),
                    element_offset=b * S * H),
                    waits=w, inc="sg", inc_n=16)
        sg_full = ("sg", c.cnt["sg"])
        pfree = list(st["p67_free"])
        par = 0
        exT_evac = {}
        for b in range(B):
            for sc in range(4):
                for hc in range(8):
                    tr = c.op("tensor", lambda te, b=b, sc=sc, hc=hc, pp=par: te.transpose(
                        t[f"p{6 + pp}"][:, :128], t["ex_raw"][:, b, sc, hc * 128:(hc + 1) * 128],
                        t["ident_sb"][:, :]),
                        waits=[sg_full, pfree[par], st.get(("w2evac", 6 + par))],
                        inc="sT")
                    pfree[par] = c.op("scalar", lambda s, b=b, sc=sc, hc=hc, pp=par: s.copy(
                        t["exT"][:, b, hc, sc * 128:(sc + 1) * 128], t[f"p{6 + pp}"][:, :128]),
                        waits=[tr], inc="sA")
                    exT_evac[(b, sc, hc)] = pfree[par]
                    par ^= 1
        st["p67_free"] = pfree
        last_w1 = None
        for fcg in range(8):
            wfirst = ([tkw_full, ("dbg", c.cnt["dbg"])] if e == 0 else [])
            wl = c.op("gpsimd", lambda g, fcg=fcg, e=e: g.dma_start(
                out=t["W1sb"][:, fcg % 2, :, :],
                in_=t["W1"][e, :, fcg * 512:(fcg + 1) * 512].rearrange(
                    "(hc p) f -> p hc f", p=128)),
                waits=[w1_free[fcg % 2]] + wfirst, inc=f"w1s{fcg % 2}", inc_n=16)
            for fcl in range(4):
                fc = fcg * 4 + fcl
                for b in range(B):
                    bank = 2 * (fc % 2) + b
                    if fc < 2:
                        bfree = st.get(("w2evac", bank))
                    else:
                        bfree = geluevs[(call, e, fc - 2, b)]
                    mm = None
                    for hc in range(8):
                        w = ([wl, bfree] if hc == 0 else []) + [exT_evac[(b, 3, hc)]]
                        mm = c.op("tensor",
                                  lambda te, fcg=fcg, fcl=fcl, b=b, hc=hc, bank=bank:
                                  te.matmul(t[f"p{bank}"][:, :],
                                            t["W1sb"][:, fcg % 2, hc, fcl * 128:(fcl + 1) * 128],
                                            t["exT"][:, b, hc, :],
                                            start=(hc == 0), stop=(hc == 7)),
                                  waits=w, inc="sT" if hc == 7 else None)
                    last_w1 = mm
                    geluevs[(call, e, fc, b)] = c.op(
                        "scalar", lambda s, fc=fc, b=b, bank=bank, e=e: s.activation(
                            t["h1sb"][:, b, fc, :], t[f"p{bank}"][:, :], AF.Gelu_apprx_tanh,
                            bias=t["b1all"][:, e, fc:fc + 1]),
                        waits=[mm, st["b1_ld"], st.get("h1_war")], inc="sA")
            w1_free[fcg % 2] = last_w1
        ex_war = last_w1
        for hh in range(2):
            mmstop = {}
            for fc in range(32):
                wl2 = c.op("gpsimd", lambda g, hh=hh, fc=fc, e=e: g.dma_start(
                    out=t["W2sb"][:, fc % 2, :],
                    in_=t["W2"][e, fc * 128:(fc + 1) * 128, hh * 512:(hh + 1) * 512]),
                    waits=[w2_free[fc % 2]], inc=f"w2s{fc % 2}", inc_n=16)
                lastmm = None
                for b in range(B):
                    for sc in range(4):
                        bank = b * 4 + sc
                        if fc == 0:
                            if hh == 0 and bank < 4:
                                bfree = geluevs[(call, e, 30 + (bank >> 1), bank & 1)]
                            else:
                                bfree = st.get(("w2evac", bank))
                            c.op("tensor", lambda te, bank=bank, hh=hh, e=e: te.matmul(
                                t[f"p{bank}"][:, :], t["ones_sb"][:1, :],
                                t["b2row"][:1, e % 2, hh * 512:(hh + 1) * 512],
                                start=True, stop=False),
                                waits=[bfree, b2l, st["ones_ok"],
                                       geluevs[(call, e, 31, b)]])
                        lastmm = c.op("tensor",
                                      lambda te, b=b, sc=sc, fc=fc, bank=bank:
                                      te.matmul(t[f"p{bank}"][:, :],
                                                t["h1sb"][:, b, fc, sc * 128:(sc + 1) * 128],
                                                t["W2sb"][:, fc % 2, :],
                                                start=False, stop=(fc == 31)),
                                      waits=[wl2],
                                      inc="sT" if (fc == 31 or (b == B - 1 and sc == 3))
                                      else None)
                        if fc == 31:
                            mmstop[(b, sc)] = lastmm
                w2_free[fc % 2] = lastmm
            for b in range(B):
                for sc in range(4):
                    bank = b * 4 + sc
                    buf = (b * 2 + hh * 4 + sc) % 4
                    ev = c.op("vector", lambda v, b=b, sc=sc, bank=bank, buf=buf:
                              v.tensor_single_scalar(t["contrib"][:, buf, :],
                                                     t[f"p{bank}"][:, :],
                                                     t["ccol"][:, b, sc:sc + 1], OP.mult),
                              waits=[mmstop[(b, sc)], dsc_full, contrib_free[buf]],
                              inc="sV")
                    st[("w2evac", bank)] = ev
                    scat = c.op("gpsimd", lambda g, b=b, sc=sc, hh=hh, buf=buf, scd=scd:
                                g.indirect_dma_start(
                                    out=scd[:, :],
                                    out_offset=IndirectOffsetOnAxis(
                                        t["idxcol"][:, b, sc:sc + 1], 0),
                                    in_=t["contrib"][:, buf, :], in_offset=None,
                                    element_offset=b * S * H + hh * 512,
                                    compute_op=OP.add),
                                waits=[ev] + scat_all + [st.get("ydone")],
                                inc=f"sb{buf}", inc_n=16)
                    contrib_free[buf] = scat
                    scat_last[b] = scat
            st[("b2free", e % 2)] = mmstop[(1, 3)]
    st["exT_war"] = ex_war
    st["contrib_free"] = contrib_free
    st["h1_war"] = None
    st["ffn_done"] = [(f"sb{i}", c.cnt[f"sb{i}"]) for i in range(4)]


def emit_attn_kv(c, st):
    t = c.t
    st["xt_bar"] = list(st["ffn_done"])
    st["p0_att"] = st.get(("w2evac", 0))
    st["p1_att"] = st.get(("w2evac", 1))
    vfree = [st.get(("w2evac", 2)), st.get(("w2evac", 3))]
    kq_evac, vown_ev = [], []
    mm = ve = None
    for b in range(B):
        for grp in range(4):
            ev = emit_ln_group(c, st, "xr", b, grp, want_lnx=False, gate=False)
            evac = ev["evac"]
            col0 = b * S + grp * 512
            for name, bank, wtn in [("k", 0, "WkT"), ("q", 1, "WqT")]:
                mm = None
                for hc in range(8):
                    w = [evac[(3, hc)]] + ([st[f"p{bank}_att"]] if hc == 0 else [])
                    mm = c.op("tensor", lambda te, hc=hc, bank=bank, wtn=wtn: te.matmul(
                        t[f"p{bank}"][0:DH, :], t[wtn][:, hc, :], t["lnT"][:, hc, :],
                        start=(hc == 0), stop=(hc == 7)),
                        waits=w, inc="sT" if hc == 7 else None)
                if name == "k":
                    evk = c.op("scalar", lambda s, col0=col0: s.activation(
                        t["kT_own"][:, col0:col0 + 512], t["p0"][0:DH, :], AF.Identity,
                        bias=t["bk_sb"][:, 0:1]), waits=[mm], inc="sA")
                    st["p0_att"] = evk
                    kq_evac.append(evk)
                else:
                    evq = c.op("scalar", lambda s, col0=col0: s.activation(
                        t["qT_sb"][0:DH, col0:col0 + 512], t["p1"][0:DH, :], AF.Identity,
                        bias=t["bq32"][:, 0:1], scale=QSCALE),
                        waits=[mm, st["bq_ok"]], inc="sA")
                    evq = c.op("scalar", lambda s, col0=col0: s.activation(
                        t["qT_sb"][DH:2 * DH, col0:col0 + 512], t["p1"][0:DH, :],
                        AF.Identity, bias=t["bq32"][:, 0:1], scale=QSCALE),
                        waits=[mm, st["bq_ok"]], inc="sA")
                    st["p1_att"] = evq
                    kq_evac.append(evq)
            for tl in range(4):
                tile = col0 // 128 + tl
                par = tl % 2
                c.op("tensor", lambda te, par=par: te.matmul(
                    t[f"p{2 + par}"][:, 0:DH], t["ones_sb"][:1, :], t["bv_sb"][:1, :],
                    start=True, stop=False),
                    waits=[evac[(tl, 7)], vfree[par], st["bv_ok"], st["ones_ok"]])
                mm = None
                for hc in range(8):
                    mm = c.op("tensor", lambda te, tl=tl, hc=hc, par=par: te.matmul(
                        t[f"p{2 + par}"][:, 0:DH], t["lnT"][:, hc, tl * 128:(tl + 1) * 128],
                        t["WvT"][:, hc, :], start=False, stop=(hc == 7)),
                        inc="sT" if hc == 7 else None)
                ve = c.op("scalar", lambda s, tile=tile, par=par: s.copy(
                    t["v_own"][:, tile, :], t[f"p{2 + par}"][:, 0:DH]),
                    waits=[mm], inc="sA")
                vfree[par] = ve
                vown_ev.append(ve)
            st["lnT_war"] = mm
    st["kq_evac"] = kq_evac
    st["vown_ev"] = vown_ev
    st["vfree"] = vfree


def emit_out_a(c, st):
    t = c.t
    c.op("sync", lambda s: s.dma_start(out=t["x1o"][:, :], in_=t["xr"][:, :]),
         waits=list(st["ffn_done"]), inc="ycw", inc_n=16)
    c.op("sync", lambda s: s.dma_start(
        out=t["kvo"][0:DH * T_OWN].rearrange("(p f) -> p f", p=DH),
        in_=t["kT_own"][:, :]), waits=st["kq_evac"], inc="ycw", inc_n=16)
    c.op("sync", lambda s: s.dma_start(
        out=t["kvo"][DH * T_OWN:2 * DH * T_OWN].rearrange(
            "(tile p f) -> p tile f", p=128, f=DH),
        in_=t["v_own"][:, :, :]), waits=st["vown_ev"], inc="ycw", inc_n=16)
    c.op("sync", lambda s: s.dma_start(
        out=t["kvo"][2 * DH * T_OWN:].rearrange("(p f) -> p f", p=DH),
        in_=t["qT_sb"][0:DH, :]), waits=st["kq_evac"], inc="ycw", inc_n=16)


def emit_attn_q(c, st):
    t = c.t
    NCH = c.t_all // 128
    kv_ok = []
    if c.launch in ("B", "B3", "Bdbg"):
        kv_ok.append(c.op("sync", lambda s: s.dma_start(
            out=t["kT_full"][:, :], in_=t["kTf_in"][:, :]),
            waits=[st["stp_full"], st["wT_done"]], inc="kvl", inc_n=16))
        kv_ok.append(c.op("sync", lambda s: s.dma_start(
            out=t["v_aug"][:, :, 0:DH],
            in_=t["vf_in"][:, :].rearrange("p (c f) -> p c f", f=DH)),
            waits=[st["stp_full"], st["wT_done"]], inc="kvl", inc_n=16))
        for half in range(2):
            kv_ok.append(c.op("sync", lambda s, half=half: s.dma_start(
                out=t["qT_sb"][half * DH:(half + 1) * DH, :], in_=t["qT_in"][:, :]),
                waits=[st["stp_full"]], inc="kvl", inc_n=16))
        vfree = [None, None]
        yfree0 = None
        st["ytr_free"] = [None, None]
    else:
        e1 = c.op("sync", lambda s: s.dma_start(
            out=t["kvin"][0:DH * T_OWN].rearrange("(p f) -> p f", p=DH),
            in_=t["kT_own"][:, :]), waits=st["kq_evac"], inc="kvl", inc_n=16)
        e2 = c.op("sync", lambda s: s.dma_start(
            out=t["kvin"][DH * T_OWN:].rearrange("(tile p f) -> p tile f", p=128, f=DH),
            in_=t["v_own"][:, :, :]), waits=st["vown_ev"], inc="kvl", inc_n=16)
        cc = c.op("gpsimd", lambda g: g.collective_compute(
            "AllGather", OP.bypass, replica_groups=[list(range(c.ncore))],
            ins=[t["kvin"].ap().opt()], outs=[t["kvout"].ap().opt()]),
            waits=[("kvl", c.cnt["kvl"])], inc="scc", inc_n=16)
        for r in range(c.ncore):
            for half in range(2):
                kv_ok.append(c.op("sync", lambda s, r=r, half=half: s.dma_start(
                    out=t["kT_full"][half * DH:(half + 1) * DH,
                                     r * (T_OWN // 2):(r + 1) * (T_OWN // 2)],
                    in_=t["kvout"][r, 0:DH * T_OWN].rearrange(
                        "(p cp two f) -> two p cp f", p=DH, two=2, f=128)[half]),
                    waits=[cc] + st["ffn_done"], inc="kvl", inc_n=16))
            kv_ok.append(c.op("sync", lambda s, r=r: s.dma_start(
                out=t["v_aug"][:, r * 32:(r + 1) * 32, 0:DH],
                in_=t["kvout"][r, DH * T_OWN:].rearrange(
                    "(tile p f) -> p tile f", p=128, f=DH)),
                waits=[cc], inc="kvl", inc_n=16))
        vfree = st["vfree"]
        yfree0 = vfree[0]
        st["ytr_free"] = [vfree[1], None]
    kvl_full = ("kvl", c.cnt["kvl"])
    vaug_ones = c.op("vector", lambda v: v.memset(t["v_aug"][:, :, DH:DH + 1], 1.0),
                     waits=[kvl_full], inc="sV")
    kv_ok = [kvl_full, vaug_ones]

    yfree = yfree0
    exp_ev = {}
    ycp = None
    for g in range(T_OWN // 512):
        ymm = None
        for ch in range(NCH):
            par = ch % 2
            w = (kv_ok if ch < 2 else []) + [exp_ev.get((g, ch - 2))]
            smm = c.op("tensor", lambda te, ch=ch, g=g, par=par: te.matmul(
                t[f"p{par}"][:, :],
                t["kT_full"][(ch % 2) * DH:(ch % 2) * DH + DH,
                             (ch // 2) * 128:(ch // 2) * 128 + 128],
                t["qT_sb"][(ch % 2) * DH:(ch % 2) * DH + DH, g * 512:(g + 1) * 512],
                start=True, stop=True),
                waits=w, inc="sT")
            exp_ev[(g, ch)] = c.op("scalar", lambda s, ch=ch, par=par: s.activation(
                t["E_sb"][:, par, :], t[f"p{par}"][:, :], AF.Exp),
                waits=[smm, ymm], inc="sA")
            ymm = c.op("tensor", lambda te, ch=ch, par=par: te.matmul(
                t["p2"][0:DH + 1, :], t["v_aug"][:, ch, :], t["E_sb"][:, par, :],
                start=(ch == 0), stop=(ch == NCH - 1)),
                waits=[exp_ev[(g, ch)]] + ([yfree, st["yT_ok"]] if ch == 0 else []),
                inc="sT")
        ycp = c.op("scalar", lambda s: s.copy(t["yT"][0:DH + 1, :], t["p2"][0:DH + 1, :]),
                   waits=[ymm], inc="sA")
        yfree = ycp
        for tl in range(4):
            par = tl % 2
            tr = c.op("tensor", lambda te, tl=tl, par=par: te.transpose(
                t[f"p{3 + par}"][:, :128], t["yT"][:, tl * 128:(tl + 1) * 128],
                t["ident_sb"][:, :]), waits=[ycp, st["ytr_free"][par]], inc="sT")
            cpy = c.op("vector", lambda v, tl=tl, par=par: v.tensor_copy(
                t["ytmp"][:, par, :], t[f"p{3 + par}"][:, 0:DH + 1]), waits=[tr], inc="sV")
            st["ytr_free"][par] = cpy
            rcy = c.op("vector", lambda v, par=par: v.reciprocal(
                t["dcol"][:, par:par + 1], t["ytmp"][:, par, DH:DH + 1]),
                waits=[cpy], inc="sV")
            dv = c.op("vector", lambda v, tl=tl, par=par: v.tensor_single_scalar(
                t["ytok"][:, tl, :], t["ytmp"][:, par, 0:DH], t["dcol"][:, par:par + 1],
                OP.mult), waits=[rcy], inc="sV")
            tile = g * 4 + tl
            row0 = tile * 128
            ld = c.op("sync", lambda s, tl=tl, row0=row0: s.dma_start(
                out=t["xt"][:, tl, :], in_=t["xr"][row0:row0 + 128, :]),
                waits=st["xt_war"][tl] + st["xt_bar"], inc=f"xt{tl}", inc_n=16)
            add = None
            for j in range(16):
                add = c.op("vector", lambda v, tl=tl, j=j: v.tensor_add(
                    t["xt"][:, tl, j * DH:(j + 1) * DH],
                    t["xt"][:, tl, j * DH:(j + 1) * DH], t["ytok"][:, tl, :]),
                    waits=[ld, dv] if j == 0 else [],
                    inc="sV" if j == 15 else None)
            wb = c.op("sync", lambda s, tl=tl, row0=row0: s.dma_start(
                out=t["xr"][row0:row0 + 128, :], in_=t["xt"][:, tl, :]),
                waits=[add], inc=f"wb{tl}", inc_n=16)
            st["xt_war"][tl] = [wb]
    st["xt_bar"] = []
    st["att_done"] = [(f"wb{i}", c.cnt[f"wb{i}"]) for i in range(4)]
    st[("w2evac", 0)] = exp_ev[(T_OWN // 512 - 1, NCH - 2)]
    st[("w2evac", 1)] = exp_ev[(T_OWN // 512 - 1, NCH - 1)]
    st[("w2evac", 2)] = ycp
    st[("w2evac", 3)] = st["ytr_free"][0]
    st[("w2evac", 4)] = st["ytr_free"][1]
    st["gate_war"] = st["ytr_free"][1]
    st["h1_war"] = ("sT", c.cnt["sT"])


def emit_copy_y(c, st):
    t = c.t
    dst = "y" if c.launch == "B3" else "xr2"
    ev = c.op("sync", lambda s, dst=dst: s.dma_start(
        out=t[dst][:, :], in_=t["xr"][:, :]),
        waits=st["att_done"], inc="ycw", inc_n=16)
    st["ydone"] = ev
    st["xt_bar"] = [ev]


def emit_final(c, st):
    t = c.t
    if c.launch == "Bdbg":
        fw = list(st.get("ffn_done", []))
        c.op("sync", lambda s: s.dma_start(
            out=t["d_ex"][:, :], in_=t["ex_raw"].reshape([128, B * 4 * H])[:, :]),
            waits=fw, inc="dbg", inc_n=16)
        c.op("sync", lambda s: s.dma_start(
            out=t["d_h1"][:, :], in_=t["h1sb"].reshape([128, B * 32 * 512])[:, :]),
            waits=fw, inc="dbg", inc_n=16)
        c.op("sync", lambda s: s.dma_start(
            out=t["d_ext"][:, :], in_=t["exT"].reshape([128, B * 8 * 512])[:, :]),
            waits=fw, inc="dbg", inc_n=16)
        c.op("sync", lambda s: s.dma_start(out=t["d_idx"][:, :, :], in_=t["idxd"][:, :, :]),
             waits=[("tkw", c.cnt["tkw"])], inc="dbg", inc_n=16)
        c.op("sync", lambda s: s.dma_start(out=t["d_g"][:, :, :], in_=t["gd"][:, :, :]),
             waits=[("tkw", c.cnt["tkw"])], inc="dbg", inc_n=16)
        c.op("sync", lambda s: s.dma_start(out=t["d_lnx"][:, :], in_=t["lnx"][:, :]),
             waits=[("sg", c.cnt["sg"])] + [(f"xt{i}", c.cnt[f"xt{i}"]) for i in range(4)],
             inc="dbg", inc_n=16)
        c.op("sync", lambda s: s.dma_start(out=t["d_sm"][0, :, :], in_=t["smT"][:, :]),
             waits=[("sA", c.cnt["sA"])], inc="dbg", inc_n=16)
    if c.launch in ("full", "B", "Bdbg"):
        c.op("sync", lambda s: s.dma_start(out=t["y"][:, :], in_=t["xr2"][:, :]),
             waits=list(st.get("ffn_done", [])), inc="ycw", inc_n=16)
    if c.stage < 4:
        w = list(st.get("ffn_done", [])) + list(st.get("att_done", []))
        c.op("sync", lambda s: s.dma_start(out=t["y"][:, :], in_=t["xr"][:, :]),
             waits=[x for x in w if x], inc="ycw", inc_n=16)
        if "d_lnx" in t:
            c.op("sync", lambda s: s.dma_start(out=t["d_lnx"][:, :], in_=t["lnx"][:, :]),
                 waits=[("sg", c.cnt["sg"])] + [(f"xt{i}", c.cnt[f"xt{i}"]) for i in range(4)],
                 inc="dbg", inc_n=16)
    allsems = ["stp", "b1s", "lnx", "tkw", "dsc", "w1s0", "w1s1", "w2s0", "w2s1",
               "b2s", "sg", "scc", "kvl", "ycw", "dbg",
               "xt0", "xt1", "xt2", "xt3", "wb0", "wb1", "wb2", "wb3",
               "sb0", "sb1", "sb2", "sb3"]
    fin = [(sname, c.cnt[sname]) for sname in allsems if c.cnt[sname] > 0]
    c.op("sync", lambda s: s.nop(), waits=fin)
    c.op("gpsimd", lambda g: g.nop(), waits=fin)


_CACHE = {}


def _relayout_kv(kvo_list, ncore):
    t_all = ncore * T_OWN
    kT = [k[0:DH * T_OWN].reshape(DH, T_OWN) for k in kvo_list]
    vv = [k[DH * T_OWN:2 * DH * T_OWN].reshape(T_OWN, DH) for k in kvo_list]
    qT = [k[2 * DH * T_OWN:].reshape(DH, T_OWN) for k in kvo_list]
    kTg = np.concatenate(kT, axis=1)
    vg = np.concatenate(vv, axis=0)
    nch = t_all // 128
    kTf = np.zeros((128, t_all // 2), dtype=kTg.dtype)
    for ch in range(nch):
        half, colblk = ch % 2, ch // 2
        kTf[half * DH:(half + 1) * DH, colblk * 128:(colblk + 1) * 128] = \
            kTg[:, ch * 128:(ch + 1) * 128]
    vf = np.ascontiguousarray(
        vg.reshape(nch, 128, DH).transpose(1, 0, 2).reshape(128, nch * DH))
    return kTf, vf, qT


def kernel(L0_attn_embeddings, L0_attn_Wq, L0_attn_bq, L0_attn_Wk, L0_attn_bk,
           L0_attn_Wv, L0_attn_bv, L0_ffn_Wg, L0_ffn_W1, L0_ffn_b1, L0_ffn_W2,
           L0_ffn_b2):
    from concourse.bass_utils import run_bass_kernel_spmd
    ncore = 8
    x = np.ascontiguousarray(np.asarray(L0_attn_embeddings, dtype=np.float32))
    if "A" not in _CACHE:
        _CACHE["A"] = build(stage=4, ncore=ncore, launch="A")
        _CACHE["B"] = build(stage=4, ncore=ncore, launch="B")
    cA, cB = _CACHE["A"], _CACHE["B"]
    common = dict(
        Wg=np.ascontiguousarray(L0_ffn_Wg, dtype=np.float32),
        W1=np.ascontiguousarray(L0_ffn_W1, dtype=np.float32),
        b1=np.ascontiguousarray(L0_ffn_b1, dtype=np.float32),
        W2=np.ascontiguousarray(L0_ffn_W2, dtype=np.float32),
        b2=np.ascontiguousarray(L0_ffn_b2, dtype=np.float32),
    )
    commonA = dict(common,
                   Wq=np.ascontiguousarray(L0_attn_Wq, dtype=np.float32),
                   bq=np.ascontiguousarray(np.asarray(L0_attn_bq, np.float32).reshape(DH, 1)),
                   Wk=np.ascontiguousarray(L0_attn_Wk, dtype=np.float32),
                   bk=np.ascontiguousarray(np.asarray(L0_attn_bk, np.float32).reshape(DH, 1)),
                   Wv=np.ascontiguousarray(L0_attn_Wv, dtype=np.float32),
                   bv=np.ascontiguousarray(np.asarray(L0_attn_bv, np.float32).reshape(DH, 1)))
    in_maps = [dict(commonA, x0=np.ascontiguousarray(
        x[2 * i:2 * i + 2].reshape(T_OWN, H))) for i in range(ncore)]
    resA = run_bass_kernel_spmd(cA.nc, in_maps, core_ids=list(range(ncore)))
    kvo_list = [np.asarray(r["kvo"]).ravel() for r in resA.results]
    kTf, vf, qTl = _relayout_kv(kvo_list, ncore)
    in_mapsB = [dict(common,
                     x0=np.ascontiguousarray(
                         np.asarray(resA.results[i]["x1o"], np.float32).reshape(T_OWN, H)),
                     kTf_in=np.ascontiguousarray(kTf),
                     vf_in=np.ascontiguousarray(vf),
                     qT_in=np.ascontiguousarray(qTl[i]))
                for i in range(ncore)]
    resB = run_bass_kernel_spmd(cB.nc, in_mapsB, core_ids=list(range(ncore)))
    out = np.concatenate(
        [np.asarray(r["y"], dtype=np.float32).reshape(B, S, H) for r in resB.results],
        axis=0)
    return out


# version 08
# speedup vs baseline: 1.2275x; 1.2275x over previous
import numpy as np
import ml_dtypes
import concourse.bass as bass
import concourse.mybir as mybir
from concourse.bass_core_rust import IndirectOffsetOnAxis

F32 = mybir.dt.float32
BF16 = mybir.dt.bfloat16
U32 = mybir.dt.uint32
AF = mybir.ActivationFunctionType
OP = mybir.AluOpType
AX = mybir.AxisListType

B = 2
S = 2048
H = 1024
E = 8
KSEL = 512
FF = 4096
DH = 64
T_OWN = B * S
EPS = 1e-5
QSCALE = 1.0 / 32.0
NEG = -1e30
ENGINES = ["sync", "scalar", "vector", "tensor", "gpsimd"]


class Prog:
    def __init__(self, stage=4, ncore=8):
        self.nc = bass.Bass("TRN2", target_bir_lowering=False)
        self.stage = stage
        self.ncore = ncore
        self.t_all = ncore * T_OWN
        self.sem = {}
        self.cnt = {}
        self.t = {}
        self.ops = {e: [] for e in ENGINES}
        self.watermark = {e: {} for e in ENGINES}

    def mksem(self, *names):
        for n in names:
            self.sem[n] = self.nc.alloc_semaphore(n)
            self.cnt[n] = 0

    def op(self, eng, fn, waits=(), inc=None, inc_n=1):
        wl = {}
        wm = self.watermark[eng]
        for ev in waits:
            if ev is None:
                continue
            s, v = ev
            if v is None or wm.get(s, 0) >= v:
                continue
            wl[s] = max(wl.get(s, 0), v)
        for s, v in wl.items():
            wm[s] = v
        evo = None
        if inc is not None:
            self.cnt[inc] += inc_n
            evo = (inc, self.cnt[inc])
        self.ops[eng].append((fn, sorted(wl.items()), inc, inc_n))
        return evo

    def emit(self):
        with self.nc.Block() as blk:
            for eng_name in ENGINES:
                oplist = self.ops[eng_name]
                if not oplist:
                    continue

                def make(oplist=oplist):
                    def run(eng):
                        for fn, waits, inc, inc_n in oplist:
                            for s, v in waits:
                                eng.wait_ge(self.sem[s], v)
                            ins = fn(eng)
                            if inc is not None:
                                ins.then_inc(self.sem[inc], inc_n)
                    return run
                getattr(blk, eng_name)(make())


def build(stage=4, ncore=8, launch="full"):
    c = Prog(stage, ncore)
    c.launch = launch
    nc = c.nc
    t = c.t

    def dram(name, shape, dt, kind="Internal"):
        t[name] = nc.dram_tensor(name, shape, dt, kind=kind)

    dram("x0", [T_OWN, H], F32, "ExternalInput")
    wlist = [("Wg", [E, H]), ("W1", [E, H, FF]), ("b1", [E, FF]),
             ("W2", [E, FF, H]), ("b2", [E, H])]
    if launch in ("full", "A"):
        wlist += [("Wq", [DH, H]), ("bq", [DH, 1]), ("Wk", [DH, H]), ("bk", [DH, 1]),
                  ("Wv", [DH, H]), ("bv", [DH, 1])]
    for n, sh in wlist:
        dram(n, sh, F32, "ExternalInput")
    if launch == "A":
        dram("x1o", [T_OWN, H], F32, "ExternalOutput")
        dram("kvo", [3 * DH * T_OWN], BF16, "ExternalOutput")
    else:
        dram("y", [T_OWN, H], F32, "ExternalOutput")
    if launch in ("B", "B3", "Bdbg"):
        dram("kTf_in", [128, (c.t_all // 256) * 128], BF16, "ExternalInput")
        dram("vf_in", [128, (c.t_all // 128) * DH], BF16, "ExternalInput")
        dram("qT_in", [DH, T_OWN], BF16, "ExternalInput")
    if launch in ("full", "B", "Bdbg"):
        dram("xr2", [T_OWN, H], F32)
    if launch == "full":
        dram("kvin", [2 * DH * T_OWN], BF16)
        dram("kvout", [c.ncore, 2 * DH * T_OWN], BF16)
    dram("xr", [T_OWN, H], F32)
    dram("lnx", [T_OWN, H], F32)
    dram("idxd", [B, E, KSEL], U32)
    dram("gd", [B, E, KSEL], F32)
    t["ident"] = nc.inline_tensor(np.eye(128, dtype=np.float32), "identc")

    if launch == "Bdbg":
        dram("d_ex", [128, B * 4 * H], F32, "ExternalOutput")
        dram("d_h1", [128, B * 32 * 512], U32 if False else BF16, "ExternalOutput")
        dram("d_ext", [128, B * 8 * 512], BF16, "ExternalOutput")
    if c.stage < 4 or launch == "Bdbg":
        dram("d_lnx", [T_OWN, H], F32, "ExternalOutput")
        dram("d_sm", [B, E, S], F32, "ExternalOutput")
        dram("d_idx", [B, E, KSEL], U32, "ExternalOutput")
        dram("d_g", [B, E, KSEL], F32, "ExternalOutput")

    def sb(name, shape, dt):
        t[name] = nc.alloc_sbuf_tensor(name, shape, dt)

    has_attw = launch in ("full", "A")
    has_q = launch in ("full", "B", "B3", "Bdbg")
    sb("ident_sb", [128, 128], F32)
    sb("ones_sb", [1, 128], BF16)
    sb("WgT", [128, 8, E], F32)
    if has_attw:
        sb("WqT", [128, 8, DH], BF16)
        sb("WkT", [128, 8, DH], BF16)
        sb("WvT", [128, 8, DH], BF16)
        sb("bq32", [DH, 1], F32)
        sb("bk_sb", [DH, 1], F32)
        sb("bv_f", [1, DH], F32)
        sb("bv_sb", [1, DH], BF16)
        sb("kT_own", [DH, T_OWN], BF16)
    sb("b2row", [1, 2, H], BF16)
    sb("b1all", [128, E, 32], F32)
    sb("xt", [128, 4, H], F32)
    sb("sq", [128, H], F32)
    sb("stat", [128, 4, 8], F32)
    sb("epsc", [128, 1], F32)
    sb("dcol", [128, 2], F32)
    sb("lnT", [128, 8, 512], BF16)
    sb("lnT32", [128, 8, 128], F32)
    sb("smtok", [128, 4, E], F32)
    sb("idxcol", [128, B, 4], U32)
    sb("ccol", [128, B, 4], F32)
    sb("contrib", [128, 4, 512], F32)

    base = (nc.sbuf_base + 31) & ~31

    def at(name, shape, dt, off):
        t[name] = nc.alloc_sbuf_tensor_at(name, shape, dt, offset=base + off)

    at("wq_f", [DH, H], F32, 0)
    at("wk_f", [DH, H], F32, 4096)
    at("wv_f", [DH, H], F32, 8192)
    at("wg_f", [E, H], F32, 12288)
    at("h1sb", [128, B, 32, 512], BF16, 0)
    kt_bytes = (c.t_all // 256) * 128 * 2
    if has_q:
        at("kT_full", [128, (c.t_all // 256) * 128], BF16, 0)
        at("v_aug", [128, c.t_all // 128, DH + 1], BF16, kt_bytes)
    rA = max(65536, (kt_bytes + (c.t_all // 128) * (DH + 1) * 2) if has_q else 0, 16384)
    rA = (rA + 31) & ~31
    at("ex_raw", [128, B, 4, H], F32, rA)
    at("exT", [128, B, 8, 512], BF16, rA + 32768)
    at("qT_sb", [128, T_OWN], BF16, rA)
    if has_attw:
        at("v_own", [128, T_OWN // 128, DH], BF16, rA + 8192)
    if has_q:
        at("E_sb", [128, 2, 512], BF16, rA + 12288)
        at("yT", [128, 512], F32, rA + 14336)
        at("ytmp", [128, 2, DH + 1], F32, rA + 16384)
        at("ytok", [128, 4, DH], F32, rA + 16960)
    rB = rA + 49152
    at("smT", [40, S], F32, rB)
    at("smW", [40, S], F32, rB + 8192)
    at("Gsb", [40, KSEL], F32, rB + 16384)
    at("Isb", [40, KSEL], U32, rB + 18432)
    at("W2sb", [128, 2, 512], BF16, rB + 16384)
    at("W1sb", [128, 2, 8, 512], BF16, rB + 20480)
    need = base + rB + 36864
    assert need <= nc.sbuf_top, (need, nc.sbuf_top, nc.sbuf_base)

    for i in range(8):
        t[f"p{i}"] = nc.alloc_psum_tensor(f"p{i}", [128, 512], F32)

    c.mksem("stp", "b1s", "lnx", "tkw", "dsc", "w1s0", "w1s1", "w2s0", "w2s1",
            "b2s", "sg", "scc", "kvl", "ycw", "dbg", "sT", "sV", "sA",
            *[f"xt{i}" for i in range(4)], *[f"wb{i}" for i in range(4)],
            *[f"sb{i}" for i in range(4)])

    st = {"xt_war": [[] for _ in range(4)], "xt_bar": []}
    emit_setup(c, st)
    if launch in ("full", "A"):
        emit_ffn(c, st, 0)
        emit_attn_kv(c, st)
    if launch == "A":
        emit_out_a(c, st)
    if launch in ("full", "B", "B3", "Bdbg"):
        if launch == "B3":
            emit_attn_q(c, st)
            emit_copy_y(c, st)
        else:
            def ln_cb(g, c=c, st=st):
                ev = emit_ln_group(c, st, "xr", g // 4, g % 4,
                                   want_lnx=True, gate=True)
                st["smT_last_ev"] = ev["smT_last"]
            emit_attn_q(c, st, ln_cb=ln_cb)
            emit_copy_y(c, st)
            st["skip_ln"] = True
            emit_ffn(c, st, 1)
    emit_final(c, st)
    c.emit()
    return c


def emit_setup(c, st):
    t = c.t
    evs = {}
    has_attw = c.launch in ("full", "A")
    c.op("sync", lambda s: s.dma_start(out=t["xr"][:, :], in_=t["x0"][:, :]),
         inc="stp", inc_n=16)
    wl = [("Wg", "wg_f")]
    if has_attw:
        wl = [("Wq", "wq_f"), ("Wk", "wk_f"), ("Wv", "wv_f")] + wl
    for w, f in wl:
        evs[f] = c.op("sync", lambda s, a=w, b=f: s.dma_start(
            out=t[b][:, :], in_=t[a][:, :]), inc="stp", inc_n=16)
    if has_attw:
        evs["bq"] = c.op("sync", lambda s: s.dma_start(
            out=t["bq32"][:, :], in_=t["bq"][:, :]), inc="stp", inc_n=16)
        c.op("sync", lambda s: s.dma_start(out=t["bk_sb"][:, :], in_=t["bk"][:, :]),
             inc="stp", inc_n=16)
        evs["bv"] = c.op("sync", lambda s: s.dma_start(
            out=t["bv_f"][:1, :], in_=t["bv"][:, :].rearrange("a o -> o a")),
            inc="stp", inc_n=16)
    evs["id"] = c.op("sync", lambda s: s.dma_start(
        out=t["ident_sb"][:, :], in_=t["ident"][:, :]), inc="stp", inc_n=16)
    for e in range(E):
        for fc in range(32):
            st["b1_ld"] = c.op("scalar", lambda s, e=e, fc=fc: s.dma_start(
                out=t["b1all"][:, e, fc:fc + 1],
                in_=t["b1"][e, fc * 128:(fc + 1) * 128].rearrange("(p o) -> p o", o=1)),
                inc="b1s", inc_n=16)
    stp_full = ("stp", c.cnt["stp"])
    st["stp_full"] = stp_full
    st["xt_bar"] = [stp_full]
    st["b1_ld"] = ("b1s", c.cnt["b1s"])
    if has_attw:
        st["bq_ok"] = c.op("vector", lambda v: v.tensor_single_scalar(
            t["bq32"][:, :], t["bq32"][:, :], QSCALE, OP.mult),
            waits=[stp_full], inc="sV")
        st["bv_ok"] = c.op("vector", lambda v: v.tensor_copy(
            t["bv_sb"][:1, :], t["bv_f"][:1, :]), waits=[stp_full], inc="sV")
    st["ones_ok"] = c.op("vector", lambda v: v.memset(t["ones_sb"][:1, :], 1.0), inc="sV")
    st["eps_ok"] = c.op("vector", lambda v: v.memset(t["epsc"][:, :], EPS), inc="sV")
    if "yT" in t:
        st["yT_ok"] = c.op("vector", lambda v: v.memset(t["yT"][:, :], 0.0), inc="sV")
    st["smT_init"] = c.op("vector", lambda v: v.memset(t["smT"][:, :], 0.0), inc="sV")

    par = [0]
    pfree = [None, None]
    wT = []
    wtl = [("wg_f", "WgT", E)]
    if has_attw:
        wtl = [("wq_f", "WqT", DH), ("wk_f", "WkT", DH), ("wv_f", "WvT", DH)] + wtl
    for mat, dstn, p in wtl:
        np_in = DH if mat != "wg_f" else E
        for hc in range(8):
            pp = par[0]
            tr = c.op("tensor", lambda te, m=mat, hc=hc, pp=pp, q=np_in: te.transpose(
                t[f"p{6 + pp}"][:, :q], t[m][:, hc * 128:(hc + 1) * 128],
                t["ident_sb"][:q, :q]),
                waits=[stp_full, pfree[pp]], inc="sT")
            pfree[pp] = c.op("scalar", lambda s, d=dstn, hc=hc, pp=pp, q=p:
                             s.copy(t[d][:, hc, :], t[f"p{6 + pp}"][:, :q]),
                             waits=[tr], inc="sA")
            wT.append(pfree[pp])
            par[0] ^= 1
    st["wT_done"] = wT[-1]
    st["p67_free"] = pfree
    st["scat_last"] = {0: None, 1: None}
    st["gate_war"] = None
    st["smT_tr_war"] = None
    st["smT_war"] = None
    st["lnx_war"] = None


def emit_ln_group(c, st, src_name, b, grp, want_lnx, gate):
    t = c.t
    out = {}
    row0 = b * S + grp * 512
    ld, red, sqe = [], [], []
    for tl in range(4):
        w = list(st["xt_war"][tl]) + list(st["xt_bar"])
        ld.append(c.op("sync", lambda s, tl=tl: s.dma_start(
            out=t["xt"][:, tl, :],
            in_=t[src_name][row0 + tl * 128: row0 + (tl + 1) * 128, :]),
            waits=w, inc=f"xt{tl}", inc_n=16))
    for tl in range(4):
        red.append(c.op("vector", lambda v, tl=tl: v.reduce_sum(
            t["stat"][:, tl, 0:1], t["xt"][:, tl, :], axis=AX.X),
            waits=[ld[tl]], inc="sV"))
    for tl in range(4):
        sqe.append(c.op("scalar", lambda s, tl=tl: s.activation(
            t["sq"][:, :], t["xt"][:, tl, :], AF.Square,
            accum_out=t["stat"][:, tl, 1:2]),
            waits=[ld[tl], st.get("sq_war")], inc="sA"))
        st["sq_war"] = sqe[-1]
    var_ev = []
    for tl in range(4):
        m = c.op("vector", lambda v, tl=tl: v.tensor_single_scalar(
            t["stat"][:, tl, 2:3], t["stat"][:, tl, 0:1], 1.0 / H, OP.mult),
            waits=[red[tl]], inc="sV")
        msq = c.op("vector", lambda v, tl=tl: v.tensor_mul(
            t["stat"][:, tl, 3:4], t["stat"][:, tl, 2:3], t["stat"][:, tl, 2:3]),
            waits=[m], inc="sV")
        var_ev.append(c.op("vector", lambda v, tl=tl: v.scalar_tensor_tensor(
            t["stat"][:, tl, 4:5], t["stat"][:, tl, 1:2], 1.0 / H, t["stat"][:, tl, 3:4],
            op0=OP.mult, op1=OP.subtract), waits=[msq, sqe[tl]], inc="sV"))
    sd_ev = [c.op("scalar", lambda s, tl=tl: s.activation(
        t["stat"][:, tl, 5:6], t["stat"][:, tl, 4:5], AF.Sqrt, bias=t["epsc"][:, 0:1]),
        waits=[var_ev[tl], st["eps_ok"]], inc="sA") for tl in range(4)]
    ap_ev = []
    for tl in range(4):
        rc = c.op("vector", lambda v, tl=tl: v.reciprocal(
            t["stat"][:, tl, 6:7], t["stat"][:, tl, 5:6]), waits=[sd_ev[tl]], inc="sV")
        ap_ev.append(c.op("vector", lambda v, tl=tl: v.tensor_scalar(
            t["xt"][:, tl, :], t["xt"][:, tl, :], t["stat"][:, tl, 2:3],
            t["stat"][:, tl, 6:7], op0=OP.subtract, op1=OP.mult),
            waits=[rc], inc="sV"))
    out["apply"] = ap_ev
    lnx_w = [None] * 4
    if want_lnx:
        for tl in range(4):
            lnx_w[tl] = c.op("sync", lambda s, tl=tl: s.dma_start(
                out=t["lnx"][row0 + tl * 128: row0 + (tl + 1) * 128, :],
                in_=t["xt"][:, tl, :]),
                waits=[ap_ev[tl], st["lnx_war"]], inc=f"xt{tl}", inc_n=16)
        out["lnx_w"] = lnx_w
    evac = {}
    tr_last = {}
    pfree = list(st["p67_free"])
    par = 0
    for tl in range(4):
        for hc in range(8):
            tr = c.op("tensor", lambda te, tl=tl, hc=hc, pp=par: te.transpose(
                t[f"p{6 + pp}"][:, :128], t["xt"][:, tl, hc * 128:(hc + 1) * 128],
                t["ident_sb"][:, :]),
                waits=[ap_ev[tl], pfree[par], st["wT_done"],
                       st.get(("w2evac", 6 + par))], inc="sT")
            ev1 = c.op("scalar", lambda s, tl=tl, hc=hc, pp=par: s.copy(
                t["lnT"][:, hc, tl * 128:(tl + 1) * 128], t[f"p{6 + pp}"][:, :128]),
                waits=[tr, st.get("lnT_war")], inc="sA")
            if gate:
                ev1 = c.op("scalar", lambda s, tl=tl, hc=hc, pp=par: s.copy(
                    t["lnT32"][:, hc, :], t[f"p{6 + pp}"][:, :128]),
                    waits=[tr, st.get("gate_mm_war")], inc="sA")
            pfree[par] = ev1
            evac[(tl, hc)] = ev1
            tr_last[tl] = tr
            par ^= 1
        if gate:
            tile_in_b = grp * 4 + tl
            gm = None
            for hc in range(8):
                gm = c.op("tensor", lambda te, tl=tl, hc=hc: te.matmul(
                    t["p4"][:, 0:E], t["lnT32"][:, hc, :],
                    t["WgT"][:, hc, :], start=(hc == 0), stop=(hc == 7)),
                    waits=[evac[(tl, hc)], st.get("gate_war"), st.get("gate_war2")],
                    inc="sT" if hc == 7 else None)
            st["gate_mm_war"] = gm
            ex = c.op("scalar", lambda s, tl=tl: s.activation(
                t["smtok"][:, tl, :], t["p4"][:, 0:E], AF.Exp,
                accum_out=t["stat"][:, tl, 7:8]), waits=[gm], inc="sA")
            st["gate_war"] = ex
            rcg = c.op("vector", lambda v, tl=tl: v.reciprocal(
                t["stat"][:, tl, 3:4], t["stat"][:, tl, 7:8]), waits=[ex], inc="sV")
            dv = c.op("vector", lambda v, tl=tl: v.tensor_single_scalar(
                t["smtok"][:, tl, :], t["smtok"][:, tl, :], t["stat"][:, tl, 3:4],
                OP.mult), waits=[rcg], inc="sV")
            str_ = c.op("tensor", lambda te, tl=tl: te.transpose(
                t["p5"][0:E, 0:128], t["smtok"][:, tl, :], t["ident_sb"][:, :]),
                waits=[dv, st.get("smT_tr_war"), st.get(("w2evac", 5))], inc="sT")
            smev = c.op("scalar", lambda s, tl=tl, tb=tile_in_b, b=b: s.copy(
                t["smT"][b * 32:b * 32 + E, tb * 128:(tb + 1) * 128],
                t["p5"][0:E, :128]),
                waits=[str_, st.get("smT_war"), st.get("smT_war2"),
                       st["smT_init"]], inc="sA")
            st["smT_tr_war"] = smev
            out["smT_last"] = smev
    st["p67_free"] = pfree
    out["evac"] = evac
    for tl in range(4):
        st["xt_war"][tl] = [tr_last[tl], lnx_w[tl]]
    st["xt_bar"] = []
    return out


def emit_topk(c, st, call):
    t = c.t
    cp = c.op("vector", lambda v: v.tensor_copy(t["smW"][:, :], t["smT"][:, :]),
              waits=[st["smT_last_ev"]] + st.get("topk_war", []), inc="sV")
    st["smT_war"] = cp
    prev = cp
    for it in range(KSEL // 8):
        sl = slice(it * 8, it * 8 + 8)
        mx = c.op("vector", lambda v, sl=sl: v.max(out=t["Gsb"][:, sl], in_=t["smW"][:, :]),
                  waits=[prev], inc="sV")
        mi = c.op("vector", lambda v, sl=sl: v.max_index(
            t["Isb"][:, sl], t["Gsb"][:, sl], t["smW"][:, :]), waits=[mx], inc="sV")
        prev = c.op("vector", lambda v, sl=sl: v.match_replace(
            out=t["smW"][:, :], in_to_replace=t["Gsb"][:, sl], in_values=t["smW"][:, :],
            imm_value=NEG), waits=[mi], inc="sV")
    gs = c.op("vector", lambda v: v.tensor_single_scalar(
        t["Gsb"][:, :], t["Gsb"][:, :], 0.5, OP.mult), waits=[prev], inc="sV")
    for b in range(B):
        iw = c.op("sync", lambda s, b=b: s.dma_start(
            out=t["idxd"][b, :, :], in_=t["Isb"][b * 32:b * 32 + E, :]),
            waits=[prev], inc="tkw", inc_n=16)
        gw = c.op("sync", lambda s, b=b: s.dma_start(
            out=t["gd"][b, :, :], in_=t["Gsb"][b * 32:b * 32 + E, :]),
            waits=[gs], inc="tkw", inc_n=16)
    st["topk_war"] = [iw, gw]
    return iw, gw


def emit_ffn(c, st, call):
    t = c.t
    scd = t["xr" if call == 0 else "xr2"]

    if not st.pop("skip_ln", False):
        for b in range(B):
            for grp in range(4):
                ev = emit_ln_group(c, st, "xr", b, grp, want_lnx=True, gate=True)
                st["smT_last_ev"] = ev["smT_last"]
    emit_topk(c, st, call)
    tkw_full = ("tkw", c.cnt["tkw"])
    lnx_all = [(f"xt{i}", c.cnt[f"xt{i}"]) for i in range(4)]
    st["lnx_war"] = None

    if c.stage < 2:
        st["ffn_done"] = [tkw_full]
        return

    gather_prev = [None, None]
    w1_free = [None, None]
    w2_free = [None, None]
    geluevs = {}
    ex_war = st.get("exT_war")
    scat_last = st["scat_last"]
    contrib_free = st.get("contrib_free", [None] * 4)
    for e in range(E):
        scat_prev = {0: scat_last[0], 1: scat_last[1]}
        scat_all = [(f"sb{i}", c.cnt[f"sb{i}"]) for i in range(4)]
        b2l = c.op("gpsimd", lambda g, e=e: g.dma_start(
            out=t["b2row"][:1, e % 2, :], in_=t["b2"][e:e + 1, :]),
            waits=[st.get(("b2free", e % 2))], inc="b2s", inc_n=16)
        icol, gcol = {}, {}
        for b in range(B):
            for sc in range(4):
                icol[(b, sc)] = c.op("scalar", lambda s, b=b, sc=sc, e=e: s.dma_start(
                    out=t["idxcol"][:, b, sc:sc + 1],
                    in_=t["idxd"][b, e, sc * 128:(sc + 1) * 128].rearrange("(p o) -> p o", o=1)),
                    waits=[tkw_full] + scat_all, inc="dsc", inc_n=16)
                gcol[(b, sc)] = c.op("scalar", lambda s, b=b, sc=sc, e=e: s.dma_start(
                    out=t["ccol"][:, b, sc:sc + 1],
                    in_=t["gd"][b, e, sc * 128:(sc + 1) * 128].rearrange("(p o) -> p o", o=1)),
                    waits=[tkw_full] + scat_all, inc="dsc", inc_n=16)
        dsc_full = ("dsc", c.cnt["dsc"])
        gat = {}
        prev_sg = ("sg", c.cnt["sg"])
        for b in range(B):
            for sc in range(4):
                w = [dsc_full, prev_sg, ex_war, st["wT_done"]] + lnx_all
                gat[(b, sc)] = c.op("gpsimd", lambda g, b=b, sc=sc: g.indirect_dma_start(
                    out=t["ex_raw"][:, b, sc, :], out_offset=None,
                    in_=t["lnx"][:, :],
                    in_offset=IndirectOffsetOnAxis(t["idxcol"][:, b, sc:sc + 1], 0),
                    element_offset=b * S * H),
                    waits=w, inc="sg", inc_n=16)
        sg_full = ("sg", c.cnt["sg"])
        pfree = list(st["p67_free"])
        par = 0
        exT_evac = {}
        for b in range(B):
            for sc in range(4):
                for hc in range(8):
                    tr = c.op("tensor", lambda te, b=b, sc=sc, hc=hc, pp=par: te.transpose(
                        t[f"p{6 + pp}"][:, :128], t["ex_raw"][:, b, sc, hc * 128:(hc + 1) * 128],
                        t["ident_sb"][:, :]),
                        waits=[sg_full, pfree[par], st.get(("w2evac", 6 + par))],
                        inc="sT")
                    pfree[par] = c.op("scalar", lambda s, b=b, sc=sc, hc=hc, pp=par: s.copy(
                        t["exT"][:, b, hc, sc * 128:(sc + 1) * 128], t[f"p{6 + pp}"][:, :128]),
                        waits=[tr], inc="sA")
                    exT_evac[(b, sc, hc)] = pfree[par]
                    par ^= 1
        st["p67_free"] = pfree
        last_w1 = None
        for fcg in range(8):
            wfirst = []
            wl = c.op("gpsimd", lambda g, fcg=fcg, e=e: g.dma_start(
                out=t["W1sb"][:, fcg % 2, :, :],
                in_=t["W1"][e, :, fcg * 512:(fcg + 1) * 512].rearrange(
                    "(hc p) f -> p hc f", p=128)),
                waits=[w1_free[fcg % 2]] + wfirst, inc=f"w1s{fcg % 2}", inc_n=16)
            for fcl in range(4):
                fc = fcg * 4 + fcl
                for b in range(B):
                    bank = 2 * (fc % 2) + b
                    if fc < 2:
                        bfree = st.get(("w2evac", bank))
                    else:
                        bfree = geluevs[(call, e, fc - 2, b)]
                    mm = None
                    for hc in range(8):
                        w = ([wl, bfree] if hc == 0 else []) + [exT_evac[(b, 3, hc)]]
                        mm = c.op("tensor",
                                  lambda te, fcg=fcg, fcl=fcl, b=b, hc=hc, bank=bank:
                                  te.matmul(t[f"p{bank}"][:, :],
                                            t["W1sb"][:, fcg % 2, hc, fcl * 128:(fcl + 1) * 128],
                                            t["exT"][:, b, hc, :],
                                            start=(hc == 0), stop=(hc == 7)),
                                  waits=w, inc="sT" if hc == 7 else None)
                    last_w1 = mm
                    geluevs[(call, e, fc, b)] = c.op(
                        "scalar", lambda s, fc=fc, b=b, bank=bank, e=e: s.activation(
                            t["h1sb"][:, b, fc, :], t[f"p{bank}"][:, :], AF.Gelu_apprx_tanh,
                            bias=t["b1all"][:, e, fc:fc + 1]),
                        waits=[mm, st["b1_ld"], st.get("h1_war")], inc="sA")
            w1_free[fcg % 2] = last_w1
        ex_war = last_w1
        for hh in range(2):
            mmstop = {}
            for fc in range(32):
                wl2 = c.op("gpsimd", lambda g, hh=hh, fc=fc, e=e: g.dma_start(
                    out=t["W2sb"][:, fc % 2, :],
                    in_=t["W2"][e, fc * 128:(fc + 1) * 128, hh * 512:(hh + 1) * 512]),
                    waits=[w2_free[fc % 2]], inc=f"w2s{fc % 2}", inc_n=16)
                lastmm = None
                for b in range(B):
                    for sc in range(4):
                        bank = b * 4 + sc
                        if fc == 0:
                            if hh == 0 and bank < 4:
                                bfree = geluevs[(call, e, 30 + (bank >> 1), bank & 1)]
                            else:
                                bfree = st.get(("w2evac", bank))
                            c.op("tensor", lambda te, bank=bank, hh=hh, e=e: te.matmul(
                                t[f"p{bank}"][:, :], t["ones_sb"][:1, :],
                                t["b2row"][:1, e % 2, hh * 512:(hh + 1) * 512],
                                start=True, stop=False),
                                waits=[bfree, b2l, st["ones_ok"],
                                       geluevs[(call, e, 31, b)]])
                        lastmm = c.op("tensor",
                                      lambda te, b=b, sc=sc, fc=fc, bank=bank:
                                      te.matmul(t[f"p{bank}"][:, :],
                                                t["h1sb"][:, b, fc, sc * 128:(sc + 1) * 128],
                                                t["W2sb"][:, fc % 2, :],
                                                start=False, stop=(fc == 31)),
                                      waits=[wl2],
                                      inc="sT" if (fc == 31 or (b == B - 1 and sc == 3))
                                      else None)
                        if fc == 31:
                            mmstop[(b, sc)] = lastmm
                w2_free[fc % 2] = lastmm
            for b in range(B):
                for sc in range(4):
                    bank = b * 4 + sc
                    buf = (b * 2 + hh * 4 + sc) % 4
                    ev = c.op("vector", lambda v, b=b, sc=sc, bank=bank, buf=buf:
                              v.tensor_single_scalar(t["contrib"][:, buf, :],
                                                     t[f"p{bank}"][:, :],
                                                     t["ccol"][:, b, sc:sc + 1], OP.mult),
                              waits=[mmstop[(b, sc)], dsc_full, contrib_free[buf]],
                              inc="sV")
                    st[("w2evac", bank)] = ev
                    scat = c.op("gpsimd", lambda g, b=b, sc=sc, hh=hh, buf=buf, scd=scd:
                                g.indirect_dma_start(
                                    out=scd[:, :],
                                    out_offset=IndirectOffsetOnAxis(
                                        t["idxcol"][:, b, sc:sc + 1], 0),
                                    in_=t["contrib"][:, buf, :], in_offset=None,
                                    element_offset=b * S * H + hh * 512,
                                    compute_op=OP.add),
                                waits=[ev] + scat_all + [st.get("ydone")],
                                inc=f"sb{buf}", inc_n=16)
                    contrib_free[buf] = scat
                    scat_last[b] = scat
            st[("b2free", e % 2)] = mmstop[(1, 3)]
    st["exT_war"] = ex_war
    st["contrib_free"] = contrib_free
    st["h1_war"] = None
    st["ffn_done"] = [(f"sb{i}", c.cnt[f"sb{i}"]) for i in range(4)]


def emit_attn_kv(c, st):
    t = c.t
    st["xt_bar"] = list(st["ffn_done"])
    st["p0_att"] = st.get(("w2evac", 0))
    st["p1_att"] = st.get(("w2evac", 1))
    vfree = [st.get(("w2evac", 2)), st.get(("w2evac", 3))]
    kq_evac, vown_ev = [], []
    mm = ve = None
    for b in range(B):
        for grp in range(4):
            ev = emit_ln_group(c, st, "xr", b, grp, want_lnx=False, gate=False)
            evac = ev["evac"]
            col0 = b * S + grp * 512
            for name, bank, wtn in [("k", 0, "WkT"), ("q", 1, "WqT")]:
                mm = None
                for hc in range(8):
                    w = [evac[(3, hc)]] + ([st[f"p{bank}_att"]] if hc == 0 else [])
                    mm = c.op("tensor", lambda te, hc=hc, bank=bank, wtn=wtn: te.matmul(
                        t[f"p{bank}"][0:DH, :], t[wtn][:, hc, :], t["lnT"][:, hc, :],
                        start=(hc == 0), stop=(hc == 7)),
                        waits=w, inc="sT" if hc == 7 else None)
                if name == "k":
                    evk = c.op("scalar", lambda s, col0=col0: s.activation(
                        t["kT_own"][:, col0:col0 + 512], t["p0"][0:DH, :], AF.Identity,
                        bias=t["bk_sb"][:, 0:1]), waits=[mm], inc="sA")
                    st["p0_att"] = evk
                    kq_evac.append(evk)
                else:
                    evq = c.op("scalar", lambda s, col0=col0: s.activation(
                        t["qT_sb"][0:DH, col0:col0 + 512], t["p1"][0:DH, :], AF.Identity,
                        bias=t["bq32"][:, 0:1], scale=QSCALE),
                        waits=[mm, st["bq_ok"]], inc="sA")
                    evq = c.op("scalar", lambda s, col0=col0: s.activation(
                        t["qT_sb"][DH:2 * DH, col0:col0 + 512], t["p1"][0:DH, :],
                        AF.Identity, bias=t["bq32"][:, 0:1], scale=QSCALE),
                        waits=[mm, st["bq_ok"]], inc="sA")
                    st["p1_att"] = evq
                    kq_evac.append(evq)
            for tl in range(4):
                tile = col0 // 128 + tl
                par = tl % 2
                c.op("tensor", lambda te, par=par: te.matmul(
                    t[f"p{2 + par}"][:, 0:DH], t["ones_sb"][:1, :], t["bv_sb"][:1, :],
                    start=True, stop=False),
                    waits=[evac[(tl, 7)], vfree[par], st["bv_ok"], st["ones_ok"]])
                mm = None
                for hc in range(8):
                    mm = c.op("tensor", lambda te, tl=tl, hc=hc, par=par: te.matmul(
                        t[f"p{2 + par}"][:, 0:DH], t["lnT"][:, hc, tl * 128:(tl + 1) * 128],
                        t["WvT"][:, hc, :], start=False, stop=(hc == 7)),
                        inc="sT" if hc == 7 else None)
                ve = c.op("scalar", lambda s, tile=tile, par=par: s.copy(
                    t["v_own"][:, tile, :], t[f"p{2 + par}"][:, 0:DH]),
                    waits=[mm], inc="sA")
                vfree[par] = ve
                vown_ev.append(ve)
            st["lnT_war"] = mm
    st["kq_evac"] = kq_evac
    st["vown_ev"] = vown_ev
    st["vfree"] = vfree


def emit_out_a(c, st):
    t = c.t
    c.op("sync", lambda s: s.dma_start(out=t["x1o"][:, :], in_=t["xr"][:, :]),
         waits=list(st["ffn_done"]), inc="ycw", inc_n=16)
    c.op("sync", lambda s: s.dma_start(
        out=t["kvo"][0:DH * T_OWN].rearrange("(p f) -> p f", p=DH),
        in_=t["kT_own"][:, :]), waits=st["kq_evac"], inc="ycw", inc_n=16)
    c.op("sync", lambda s: s.dma_start(
        out=t["kvo"][DH * T_OWN:2 * DH * T_OWN].rearrange(
            "(tile p f) -> p tile f", p=128, f=DH),
        in_=t["v_own"][:, :, :]), waits=st["vown_ev"], inc="ycw", inc_n=16)
    c.op("sync", lambda s: s.dma_start(
        out=t["kvo"][2 * DH * T_OWN:].rearrange("(p f) -> p f", p=DH),
        in_=t["qT_sb"][0:DH, :]), waits=st["kq_evac"], inc="ycw", inc_n=16)


def emit_attn_q(c, st, ln_cb=None):
    t = c.t
    NCH = c.t_all // 128
    kv_ok = []
    if c.launch in ("B", "B3", "Bdbg"):
        kv_ok.append(c.op("sync", lambda s: s.dma_start(
            out=t["kT_full"][:, :], in_=t["kTf_in"][:, :]),
            waits=[st["stp_full"], st["wT_done"]], inc="kvl", inc_n=16))
        kv_ok.append(c.op("sync", lambda s: s.dma_start(
            out=t["v_aug"][:, :, 0:DH],
            in_=t["vf_in"][:, :].rearrange("p (c f) -> p c f", f=DH)),
            waits=[st["stp_full"], st["wT_done"]], inc="kvl", inc_n=16))
        for half in range(2):
            kv_ok.append(c.op("sync", lambda s, half=half: s.dma_start(
                out=t["qT_sb"][half * DH:(half + 1) * DH, :], in_=t["qT_in"][:, :]),
                waits=[st["stp_full"]], inc="kvl", inc_n=16))
        vfree = [None, None]
        yfree0 = None
        st["ytr_free"] = [None, None]
    else:
        e1 = c.op("sync", lambda s: s.dma_start(
            out=t["kvin"][0:DH * T_OWN].rearrange("(p f) -> p f", p=DH),
            in_=t["kT_own"][:, :]), waits=st["kq_evac"], inc="kvl", inc_n=16)
        e2 = c.op("sync", lambda s: s.dma_start(
            out=t["kvin"][DH * T_OWN:].rearrange("(tile p f) -> p tile f", p=128, f=DH),
            in_=t["v_own"][:, :, :]), waits=st["vown_ev"], inc="kvl", inc_n=16)
        cc = c.op("gpsimd", lambda g: g.collective_compute(
            "AllGather", OP.bypass, replica_groups=[list(range(c.ncore))],
            ins=[t["kvin"].ap().opt()], outs=[t["kvout"].ap().opt()]),
            waits=[("kvl", c.cnt["kvl"])], inc="scc", inc_n=16)
        for r in range(c.ncore):
            for half in range(2):
                kv_ok.append(c.op("sync", lambda s, r=r, half=half: s.dma_start(
                    out=t["kT_full"][half * DH:(half + 1) * DH,
                                     r * (T_OWN // 2):(r + 1) * (T_OWN // 2)],
                    in_=t["kvout"][r, 0:DH * T_OWN].rearrange(
                        "(p cp two f) -> two p cp f", p=DH, two=2, f=128)[half]),
                    waits=[cc] + st["ffn_done"], inc="kvl", inc_n=16))
            kv_ok.append(c.op("sync", lambda s, r=r: s.dma_start(
                out=t["v_aug"][:, r * 32:(r + 1) * 32, 0:DH],
                in_=t["kvout"][r, DH * T_OWN:].rearrange(
                    "(tile p f) -> p tile f", p=128, f=DH)),
                waits=[cc], inc="kvl", inc_n=16))
        vfree = st["vfree"]
        yfree0 = vfree[0]
        st["ytr_free"] = [vfree[1], None]
    kvl_full = ("kvl", c.cnt["kvl"])
    vaug_ones = c.op("vector", lambda v: v.memset(t["v_aug"][:, :, DH:DH + 1], 1.0),
                     waits=[kvl_full], inc="sV")
    kv_ok = [kvl_full, vaug_ones]

    yfree = yfree0
    exp_ev = {}
    smm_ev = {}
    ymm_ev = {}
    ycp = None
    for g in range(T_OWN // 512):
        for ch in range(NCH + 1):
            par = ch % 2
            if ch < NCH:
                w = (kv_ok if ch < 2 else []) + [exp_ev.get((g, ch - 2))]
                smm_ev[(g, ch)] = c.op("tensor", lambda te, ch=ch, g=g, par=par: te.matmul(
                    t[f"p{par}"][:, :],
                    t["kT_full"][(ch % 2) * DH:(ch % 2) * DH + DH,
                                 (ch // 2) * 128:(ch // 2) * 128 + 128],
                    t["qT_sb"][(ch % 2) * DH:(ch % 2) * DH + DH,
                               g * 512:(g + 1) * 512],
                    start=True, stop=True),
                    waits=w, inc="sT")
            if ch >= 1:
                cy = ch - 1
                ymm_ev[(g, cy)] = c.op("tensor", lambda te, cy=cy: te.matmul(
                    t["p2"][0:DH + 1, :], t["v_aug"][:, cy, :],
                    t["E_sb"][:, cy % 2, :],
                    start=(cy == 0), stop=(cy == NCH - 1)),
                    waits=[exp_ev[(g, cy)]] +
                          ([yfree, st.get("yT_ok")] if cy == 0 else []),
                    inc="sT")
            if ch < NCH:
                exp_ev[(g, ch)] = c.op("scalar", lambda s, ch=ch, par=par: s.activation(
                    t["E_sb"][:, par, :], t[f"p{par}"][:, :], AF.Exp),
                    waits=[smm_ev[(g, ch)], ymm_ev.get((g, ch - 2))], inc="sA")
        ycp = c.op("scalar", lambda s: s.copy(t["yT"][0:DH + 1, :], t["p2"][0:DH + 1, :]),
                   waits=[ymm_ev[(g, NCH - 1)]], inc="sA")
        yfree = ycp
        for tl in range(4):
            par = tl % 2
            tr = c.op("tensor", lambda te, tl=tl, par=par: te.transpose(
                t[f"p{3 + par}"][:, :128], t["yT"][:, tl * 128:(tl + 1) * 128],
                t["ident_sb"][:, :]),
                waits=[ycp, st["ytr_free"][par],
                       (st.get("gate_war") if par == 1 else None)], inc="sT")
            cpy = c.op("vector", lambda v, tl=tl, par=par: v.tensor_copy(
                t["ytmp"][:, par, :], t[f"p{3 + par}"][:, 0:DH + 1]), waits=[tr], inc="sV")
            st["ytr_free"][par] = cpy
            rcy = c.op("vector", lambda v, par=par: v.reciprocal(
                t["dcol"][:, par:par + 1], t["ytmp"][:, par, DH:DH + 1]),
                waits=[cpy], inc="sV")
            dv = c.op("vector", lambda v, tl=tl, par=par: v.tensor_single_scalar(
                t["ytok"][:, tl, :], t["ytmp"][:, par, 0:DH], t["dcol"][:, par:par + 1],
                OP.mult), waits=[rcy], inc="sV")
            tile = g * 4 + tl
            row0 = tile * 128
            ld = c.op("sync", lambda s, tl=tl, row0=row0: s.dma_start(
                out=t["xt"][:, tl, :], in_=t["xr"][row0:row0 + 128, :]),
                waits=st["xt_war"][tl] + st["xt_bar"], inc=f"xt{tl}", inc_n=16)
            add = None
            for j in range(16):
                add = c.op("vector", lambda v, tl=tl, j=j: v.tensor_add(
                    t["xt"][:, tl, j * DH:(j + 1) * DH],
                    t["xt"][:, tl, j * DH:(j + 1) * DH], t["ytok"][:, tl, :]),
                    waits=[ld, dv] if j == 0 else [],
                    inc="sV" if j == 15 else None)
            wb = c.op("sync", lambda s, tl=tl, row0=row0: s.dma_start(
                out=t["xr"][row0:row0 + 128, :], in_=t["xt"][:, tl, :]),
                waits=[add], inc=f"wb{tl}", inc_n=16)
            st["xt_war"][tl] = [wb]
        if ln_cb is not None:
            st["gate_war2"] = st["ytr_free"][1]
            ln_cb(g)
    st["xt_bar"] = []
    st["att_done"] = [(f"wb{i}", c.cnt[f"wb{i}"]) for i in range(4)]
    st[("w2evac", 0)] = exp_ev[(T_OWN // 512 - 1, NCH - 2)]
    st[("w2evac", 1)] = exp_ev[(T_OWN // 512 - 1, NCH - 1)]
    st[("w2evac", 2)] = ycp
    st[("w2evac", 3)] = st["ytr_free"][0]
    st[("w2evac", 4)] = st["ytr_free"][1]
    st["gate_war"] = st["ytr_free"][1]
    st["h1_war"] = ("sT", c.cnt["sT"])


def emit_copy_y(c, st):
    t = c.t
    dst = "y" if c.launch == "B3" else "xr2"
    ev = c.op("sync", lambda s, dst=dst: s.dma_start(
        out=t[dst][:, :], in_=t["xr"][:, :]),
        waits=st["att_done"], inc="ycw", inc_n=16)
    st["ydone"] = ev


def emit_final(c, st):
    t = c.t
    if c.launch == "Bdbg":
        fw = list(st.get("ffn_done", []))
        c.op("sync", lambda s: s.dma_start(
            out=t["d_ex"][:, :], in_=t["ex_raw"].reshape([128, B * 4 * H])[:, :]),
            waits=fw, inc="dbg", inc_n=16)
        c.op("sync", lambda s: s.dma_start(
            out=t["d_h1"][:, :], in_=t["h1sb"].reshape([128, B * 32 * 512])[:, :]),
            waits=fw, inc="dbg", inc_n=16)
        c.op("sync", lambda s: s.dma_start(
            out=t["d_ext"][:, :], in_=t["exT"].reshape([128, B * 8 * 512])[:, :]),
            waits=fw, inc="dbg", inc_n=16)
        c.op("sync", lambda s: s.dma_start(out=t["d_idx"][:, :, :], in_=t["idxd"][:, :, :]),
             waits=[("tkw", c.cnt["tkw"])], inc="dbg", inc_n=16)
        c.op("sync", lambda s: s.dma_start(out=t["d_g"][:, :, :], in_=t["gd"][:, :, :]),
             waits=[("tkw", c.cnt["tkw"])], inc="dbg", inc_n=16)
        c.op("sync", lambda s: s.dma_start(out=t["d_lnx"][:, :], in_=t["lnx"][:, :]),
             waits=[("sg", c.cnt["sg"])] + [(f"xt{i}", c.cnt[f"xt{i}"]) for i in range(4)],
             inc="dbg", inc_n=16)
        c.op("sync", lambda s: s.dma_start(out=t["d_sm"][0, :, :], in_=t["smT"][:, :]),
             waits=[("sA", c.cnt["sA"])], inc="dbg", inc_n=16)
    if c.launch in ("full", "B", "Bdbg"):
        c.op("sync", lambda s: s.dma_start(out=t["y"][:, :], in_=t["xr2"][:, :]),
             waits=list(st.get("ffn_done", [])), inc="ycw", inc_n=16)
    if c.stage < 4:
        w = list(st.get("ffn_done", [])) + list(st.get("att_done", []))
        c.op("sync", lambda s: s.dma_start(out=t["y"][:, :], in_=t["xr"][:, :]),
             waits=[x for x in w if x], inc="ycw", inc_n=16)
        if "d_lnx" in t:
            c.op("sync", lambda s: s.dma_start(out=t["d_lnx"][:, :], in_=t["lnx"][:, :]),
                 waits=[("sg", c.cnt["sg"])] + [(f"xt{i}", c.cnt[f"xt{i}"]) for i in range(4)],
                 inc="dbg", inc_n=16)
    allsems = ["stp", "b1s", "lnx", "tkw", "dsc", "w1s0", "w1s1", "w2s0", "w2s1",
               "b2s", "sg", "scc", "kvl", "ycw", "dbg",
               "xt0", "xt1", "xt2", "xt3", "wb0", "wb1", "wb2", "wb3",
               "sb0", "sb1", "sb2", "sb3"]
    fin = [(sname, c.cnt[sname]) for sname in allsems if c.cnt[sname] > 0]
    c.op("sync", lambda s: s.nop(), waits=fin)
    c.op("gpsimd", lambda g: g.nop(), waits=fin)


_CACHE = {}


def _relayout_kv(kvo_list, ncore):
    t_all = ncore * T_OWN
    kT = [k[0:DH * T_OWN].reshape(DH, T_OWN) for k in kvo_list]
    vv = [k[DH * T_OWN:2 * DH * T_OWN].reshape(T_OWN, DH) for k in kvo_list]
    qT = [k[2 * DH * T_OWN:].reshape(DH, T_OWN) for k in kvo_list]
    kTg = np.concatenate(kT, axis=1)
    vg = np.concatenate(vv, axis=0)
    nch = t_all // 128
    kTf = np.zeros((128, t_all // 2), dtype=kTg.dtype)
    for ch in range(nch):
        half, colblk = ch % 2, ch // 2
        kTf[half * DH:(half + 1) * DH, colblk * 128:(colblk + 1) * 128] = \
            kTg[:, ch * 128:(ch + 1) * 128]
    vf = np.ascontiguousarray(
        vg.reshape(nch, 128, DH).transpose(1, 0, 2).reshape(128, nch * DH))
    return kTf, vf, qT


def kernel(L0_attn_embeddings, L0_attn_Wq, L0_attn_bq, L0_attn_Wk, L0_attn_bk,
           L0_attn_Wv, L0_attn_bv, L0_ffn_Wg, L0_ffn_W1, L0_ffn_b1, L0_ffn_W2,
           L0_ffn_b2):
    from concourse.bass_utils import run_bass_kernel_spmd
    ncore = 8
    x = np.ascontiguousarray(np.asarray(L0_attn_embeddings, dtype=np.float32))
    if "A" not in _CACHE:
        _CACHE["A"] = build(stage=4, ncore=ncore, launch="A")
        _CACHE["B"] = build(stage=4, ncore=ncore, launch="B")
    cA, cB = _CACHE["A"], _CACHE["B"]
    common = dict(
        Wg=np.ascontiguousarray(L0_ffn_Wg, dtype=np.float32),
        W1=np.ascontiguousarray(L0_ffn_W1, dtype=np.float32),
        b1=np.ascontiguousarray(L0_ffn_b1, dtype=np.float32),
        W2=np.ascontiguousarray(L0_ffn_W2, dtype=np.float32),
        b2=np.ascontiguousarray(L0_ffn_b2, dtype=np.float32),
    )
    commonA = dict(common,
                   Wq=np.ascontiguousarray(L0_attn_Wq, dtype=np.float32),
                   bq=np.ascontiguousarray(np.asarray(L0_attn_bq, np.float32).reshape(DH, 1)),
                   Wk=np.ascontiguousarray(L0_attn_Wk, dtype=np.float32),
                   bk=np.ascontiguousarray(np.asarray(L0_attn_bk, np.float32).reshape(DH, 1)),
                   Wv=np.ascontiguousarray(L0_attn_Wv, dtype=np.float32),
                   bv=np.ascontiguousarray(np.asarray(L0_attn_bv, np.float32).reshape(DH, 1)))
    in_maps = [dict(commonA, x0=np.ascontiguousarray(
        x[2 * i:2 * i + 2].reshape(T_OWN, H))) for i in range(ncore)]
    resA = run_bass_kernel_spmd(cA.nc, in_maps, core_ids=list(range(ncore)))
    kvo_list = [np.asarray(r["kvo"]).ravel() for r in resA.results]
    kTf, vf, qTl = _relayout_kv(kvo_list, ncore)
    in_mapsB = [dict(common,
                     x0=np.ascontiguousarray(
                         np.asarray(resA.results[i]["x1o"], np.float32).reshape(T_OWN, H)),
                     kTf_in=np.ascontiguousarray(kTf),
                     vf_in=np.ascontiguousarray(vf),
                     qT_in=np.ascontiguousarray(qTl[i]))
                for i in range(ncore)]
    resB = run_bass_kernel_spmd(cB.nc, in_mapsB, core_ids=list(range(ncore)))
    out = np.concatenate(
        [np.asarray(r["y"], dtype=np.float32).reshape(B, S, H) for r in resB.results],
        axis=0)
    return out


# version 12
# speedup vs baseline: 1.4422x; 1.1749x over previous
import numpy as np
import ml_dtypes
import concourse.bass as bass
import concourse.mybir as mybir
from concourse.bass_core_rust import IndirectOffsetOnAxis

F32 = mybir.dt.float32
BF16 = mybir.dt.bfloat16
U32 = mybir.dt.uint32
AF = mybir.ActivationFunctionType
OP = mybir.AluOpType
AX = mybir.AxisListType

B = 2
S = 2048
H = 1024
E = 8
KSEL = 512
FF = 4096
DH = 64
T_OWN = B * S
EPS = 1e-5
QSCALE = 1.0 / 32.0
NEG = -1e30
ENGINES = ["sync", "scalar", "vector", "tensor", "gpsimd"]


class Prog:
    def __init__(self, stage=4, ncore=8):
        self.nc = bass.Bass("TRN2", target_bir_lowering=False)
        self.stage = stage
        self.ncore = ncore
        self.t_all = ncore * T_OWN
        self.sem = {}
        self.cnt = {}
        self.t = {}
        self.ops = {e: [] for e in ENGINES}
        self.watermark = {e: {} for e in ENGINES}

    def mksem(self, *names):
        for n in names:
            self.sem[n] = self.nc.alloc_semaphore(n)
            self.cnt[n] = 0

    def op(self, eng, fn, waits=(), inc=None, inc_n=1):
        wl = {}
        wm = self.watermark[eng]
        for ev in waits:
            if ev is None:
                continue
            s, v = ev
            if v is None or wm.get(s, 0) >= v:
                continue
            wl[s] = max(wl.get(s, 0), v)
        for s, v in wl.items():
            wm[s] = v
        evo = None
        if inc is not None:
            self.cnt[inc] += inc_n
            evo = (inc, self.cnt[inc])
        self.ops[eng].append((fn, sorted(wl.items()), inc, inc_n))
        return evo

    def emit(self):
        with self.nc.Block() as blk:
            for eng_name in ENGINES:
                oplist = self.ops[eng_name]
                if not oplist:
                    continue

                def make(oplist=oplist):
                    def run(eng):
                        for fn, waits, inc, inc_n in oplist:
                            for s, v in waits:
                                eng.wait_ge(self.sem[s], v)
                            ins = fn(eng)
                            if inc is not None:
                                ins.then_inc(self.sem[inc], inc_n)
                    return run
                getattr(blk, eng_name)(make())


def build(stage=4, ncore=8, launch="full"):
    c = Prog(stage, ncore)
    c.launch = launch
    nc = c.nc
    t = c.t

    def dram(name, shape, dt, kind="Internal"):
        t[name] = nc.dram_tensor(name, shape, dt, kind=kind)

    dram("x0", [T_OWN, H], F32, "ExternalInput")
    wlist = [("Wg", [E, H]), ("W1", [E, H, FF]), ("b1", [E, FF]),
             ("W2", [E, FF, H]), ("b2", [E, H])]
    if launch in ("full", "A"):
        wlist += [("Wq", [DH, H]), ("bq", [DH, 1]), ("Wk", [DH, H]), ("bk", [DH, 1]),
                  ("Wv", [DH, H]), ("bv", [DH, 1])]
    for n, sh in wlist:
        dram(n, sh, F32, "ExternalInput")
    if launch == "A":
        dram("x1o", [T_OWN, H], F32, "ExternalOutput")
        dram("kvo", [3 * DH * T_OWN], BF16, "ExternalOutput")
    else:
        dram("y", [T_OWN, H], F32, "ExternalOutput")
    if launch in ("B", "B3", "Bdbg"):
        dram("kTf_in", [128, (c.t_all // 256) * 128], BF16, "ExternalInput")
        dram("vf_in", [128, (c.t_all // 128) * DH], BF16, "ExternalInput")
        dram("qT_in", [DH, T_OWN], BF16, "ExternalInput")
    if launch in ("full", "B", "Bdbg"):
        dram("xr2", [T_OWN, H], F32)
    if launch == "full":
        dram("kvin", [2 * DH * T_OWN], BF16)
        dram("kvout", [c.ncore, 2 * DH * T_OWN], BF16)
    dram("xr", [T_OWN, H], F32)
    dram("lnx", [T_OWN, H], F32)
    dram("idxd", [B, E, KSEL], U32)
    dram("gd", [B, E, KSEL], F32)
    t["ident"] = nc.inline_tensor(np.eye(128, dtype=np.float32), "identc")

    if launch == "Bdbg":
        dram("d_ex", [128, B * 4 * H], F32, "ExternalOutput")
        dram("d_h1", [128, B * 32 * 512], U32 if False else BF16, "ExternalOutput")
        dram("d_ext", [128, B * 8 * 512], BF16, "ExternalOutput")
    if c.stage < 4 or launch == "Bdbg":
        dram("d_lnx", [T_OWN, H], F32, "ExternalOutput")
        dram("d_sm", [B, E, S], F32, "ExternalOutput")
        dram("d_idx", [B, E, KSEL], U32, "ExternalOutput")
        dram("d_g", [B, E, KSEL], F32, "ExternalOutput")

    def sb(name, shape, dt):
        t[name] = nc.alloc_sbuf_tensor(name, shape, dt)

    has_attw = launch in ("full", "A")
    has_q = launch in ("full", "B", "B3", "Bdbg")
    sb("ident_sb", [128, 128], F32)
    sb("ones_sb", [1, 128], BF16)
    sb("WgT", [128, 8, E], F32)
    if has_attw:
        sb("WqT", [128, 8, DH], BF16)
        sb("WkT", [128, 8, DH], BF16)
        sb("WvT", [128, 8, DH], BF16)
        sb("bq32", [DH, 1], F32)
        sb("bk_sb", [DH, 1], F32)
        sb("bv_f", [1, DH], F32)
        sb("bv_sb", [1, DH], BF16)
        sb("kT_own", [DH, T_OWN], BF16)
    sb("b2row", [1, 2, H], BF16)
    sb("b1all", [128, E, 32], F32)
    sb("xt", [128, 4, H], F32)
    sb("sq", [128, H], F32)
    sb("stat", [128, 4, 8], F32)
    sb("epsc", [128, 1], F32)
    sb("dcol", [128, 2], F32)
    sb("lnT", [128, 8, 512], BF16)
    sb("lnT32", [128, 8, 128], F32)
    sb("smtok", [128, 4, E], F32)
    sb("idxcol", [128, 2, B, 4], U32)
    sb("ccol", [128, 2, B, 4], F32)
    sb("contrib", [128, 4, 512], F32)

    base = (nc.sbuf_base + 31) & ~31

    def at(name, shape, dt, off):
        t[name] = nc.alloc_sbuf_tensor_at(name, shape, dt, offset=base + off)

    at("wq_f", [DH, H], F32, 0)
    at("wk_f", [DH, H], F32, 4096)
    at("wv_f", [DH, H], F32, 8192)
    at("wg_f", [E, H], F32, 12288)
    at("h1sb", [128, B, 32, 512], BF16, 0)
    kt_bytes = (c.t_all // 256) * 128 * 2
    if has_q:
        at("kT_full", [128, (c.t_all // 256) * 128], BF16, 0)
        at("v_aug", [128, c.t_all // 128, DH + 1], BF16, kt_bytes)
    rA = max(65536, (kt_bytes + (c.t_all // 128) * (DH + 1) * 2) if has_q else 0, 16384)
    rA = (rA + 31) & ~31
    at("ex_raw", [128, B, 4, H], F32, rA)
    at("exT", [128, B, 8, 512], BF16, rA + 32768)
    at("qT_sb", [128, T_OWN], BF16, rA)
    if has_attw:
        at("v_own", [128, T_OWN // 128, DH], BF16, rA + 8192)
    if has_q:
        at("E_sb", [128, 2, 512], BF16, rA + 12288)
        at("yT", [128, 512], F32, rA + 14336)
        at("ytmp", [128, 2, DH + 1], F32, rA + 16384)
        at("ytok", [128, 4, DH], F32, rA + 16960)
    rB = rA + 49152
    at("smT", [128, S], F32, rB)
    at("W1sb", [128, 3, 8, 512], BF16, rB)
    at("smW", [128, S], F32, rB + 24576)
    at("W2sb", [128, 8, 512], BF16, rB + 24576)
    at("Gsb", [128, KSEL], F32, rB + 32768)
    at("Isb", [128, KSEL], U32, rB + 34816)
    need = base + rB + 36864
    assert need <= nc.sbuf_top, (need, nc.sbuf_top, nc.sbuf_base)

    for i in range(8):
        t[f"p{i}"] = nc.alloc_psum_tensor(f"p{i}", [128, 512], F32)

    c.mksem("stp", "b1s", "lnx", "tkw", "dsc",
            *[f"w1s{i}" for i in range(3)], *[f"w2s{i}" for i in range(8)],
            "b2s", "sg", "scc", "kvl", "ycw", "dbg", "sT", "sV", "sA",
            *[f"xt{i}" for i in range(4)], *[f"wb{i}" for i in range(4)],
            *[f"sb{i}" for i in range(4)])

    st = {"xt_war": [[] for _ in range(4)], "xt_bar": []}
    emit_setup(c, st)
    if launch in ("full", "A"):
        emit_ffn(c, st, 0)
        emit_attn_kv(c, st)
    if launch == "A":
        emit_out_a(c, st)
    if launch in ("full", "B", "B3", "Bdbg"):
        if launch == "B3":
            emit_attn_q(c, st)
            emit_copy_y(c, st)
        else:
            def ln_cb(g, c=c, st=st):
                ev = emit_ln_group(c, st, "xr", g // 4, g % 4,
                                   want_lnx=True, gate=True)
                st["smT_last_ev"] = ev["smT_last"]
            emit_attn_q(c, st, ln_cb=ln_cb)
            emit_copy_y(c, st)
            st["skip_ln"] = True
            emit_ffn(c, st, 1)
    emit_final(c, st)
    c.emit()
    return c


def emit_setup(c, st):
    t = c.t
    evs = {}
    has_attw = c.launch in ("full", "A")
    c.op("sync", lambda s: s.dma_start(out=t["xr"][:, :], in_=t["x0"][:, :]),
         inc="stp", inc_n=16)
    wl = [("Wg", "wg_f")]
    if has_attw:
        wl = [("Wq", "wq_f"), ("Wk", "wk_f"), ("Wv", "wv_f")] + wl
    for w, f in wl:
        evs[f] = c.op("sync", lambda s, a=w, b=f: s.dma_start(
            out=t[b][:, :], in_=t[a][:, :]), inc="stp", inc_n=16)
    if has_attw:
        evs["bq"] = c.op("sync", lambda s: s.dma_start(
            out=t["bq32"][:, :], in_=t["bq"][:, :]), inc="stp", inc_n=16)
        c.op("sync", lambda s: s.dma_start(out=t["bk_sb"][:, :], in_=t["bk"][:, :]),
             inc="stp", inc_n=16)
        evs["bv"] = c.op("sync", lambda s: s.dma_start(
            out=t["bv_f"][:1, :], in_=t["bv"][:, :].rearrange("a o -> o a")),
            inc="stp", inc_n=16)
    evs["id"] = c.op("sync", lambda s: s.dma_start(
        out=t["ident_sb"][:, :], in_=t["ident"][:, :]), inc="stp", inc_n=16)
    for e in range(E):
        for fc in range(32):
            st["b1_ld"] = c.op("scalar", lambda s, e=e, fc=fc: s.dma_start(
                out=t["b1all"][:, e, fc:fc + 1],
                in_=t["b1"][e, fc * 128:(fc + 1) * 128].rearrange("(p o) -> p o", o=1)),
                inc="b1s", inc_n=16)
    stp_full = ("stp", c.cnt["stp"])
    st["stp_full"] = stp_full
    st["xt_bar"] = [stp_full]
    st["b1_ld"] = ("b1s", c.cnt["b1s"])
    if has_attw:
        st["bq_ok"] = c.op("vector", lambda v: v.tensor_single_scalar(
            t["bq32"][:, :], t["bq32"][:, :], QSCALE, OP.mult),
            waits=[stp_full], inc="sV")
        st["bv_ok"] = c.op("vector", lambda v: v.tensor_copy(
            t["bv_sb"][:1, :], t["bv_f"][:1, :]), waits=[stp_full], inc="sV")
    st["ones_ok"] = c.op("vector", lambda v: v.memset(t["ones_sb"][:1, :], 1.0), inc="sV")
    st["eps_ok"] = c.op("vector", lambda v: v.memset(t["epsc"][:, :], EPS), inc="sV")
    if "yT" in t:
        st["yT_ok"] = c.op("vector", lambda v: v.memset(t["yT"][:, :], 0.0), inc="sV")
    st["smT_init"] = c.op("vector", lambda v: v.memset(t["smT"][:, :], 0.0), inc="sV")

    st["p67_free"] = [None, None]
    st["p67i"] = 0
    pfree = st["p67_free"]
    wT = []
    wtl = [("wg_f", "WgT", E)]
    if has_attw:
        wtl = [("wq_f", "WqT", DH), ("wk_f", "WkT", DH), ("wv_f", "WvT", DH)] + wtl
    for mat, dstn, p in wtl:
        np_in = DH if mat != "wg_f" else E
        for hc in range(8):
            sl = st["p67i"] % 2
            st["p67i"] += 1
            tr = c.op("tensor", lambda te, m=mat, hc=hc, sl=sl, q=np_in: te.transpose(
                t[f"p{6 + sl}"][:, :q], t[m][:, hc * 128:(hc + 1) * 128],
                t["ident_sb"][:q, :q]),
                waits=[stp_full, pfree[sl]], inc="sT")
            pfree[sl] = c.op("scalar", lambda s, d=dstn, hc=hc, sl=sl, q=p:
                             s.copy(t[d][:, hc, :], t[f"p{6 + sl}"][:, :q]),
                             waits=[tr], inc="sA")
            wT.append(pfree[sl])
    st["wT_done"] = wT[-1]
    st["scat_last"] = {0: None, 1: None}
    st["gate_war"] = None
    st["smT_tr_war"] = None
    st["smT_war"] = None
    st["lnx_war"] = None


def emit_ln_group(c, st, src_name, b, grp, want_lnx, gate):
    t = c.t
    out = {}
    row0 = b * S + grp * 512
    ld, red, sqe = [], [], []
    for tl in range(4):
        w = list(st["xt_war"][tl]) + list(st["xt_bar"])
        ld.append(c.op("sync", lambda s, tl=tl: s.dma_start(
            out=t["xt"][:, tl, :],
            in_=t[src_name][row0 + tl * 128: row0 + (tl + 1) * 128, :]),
            waits=w, inc=f"xt{tl}", inc_n=16))
    for tl in range(4):
        red.append(c.op("vector", lambda v, tl=tl: v.reduce_sum(
            t["stat"][:, tl, 0:1], t["xt"][:, tl, :], axis=AX.X),
            waits=[ld[tl]], inc="sV"))
    for tl in range(4):
        sqe.append(c.op("scalar", lambda s, tl=tl: s.activation(
            t["sq"][:, :], t["xt"][:, tl, :], AF.Square,
            accum_out=t["stat"][:, tl, 1:2]),
            waits=[ld[tl], st.get("sq_war")], inc="sA"))
        st["sq_war"] = sqe[-1]
    var_ev = []
    for tl in range(4):
        m = c.op("vector", lambda v, tl=tl: v.tensor_single_scalar(
            t["stat"][:, tl, 2:3], t["stat"][:, tl, 0:1], 1.0 / H, OP.mult),
            waits=[red[tl]], inc="sV")
        msq = c.op("vector", lambda v, tl=tl: v.tensor_mul(
            t["stat"][:, tl, 3:4], t["stat"][:, tl, 2:3], t["stat"][:, tl, 2:3]),
            waits=[m], inc="sV")
        var_ev.append(c.op("vector", lambda v, tl=tl: v.scalar_tensor_tensor(
            t["stat"][:, tl, 4:5], t["stat"][:, tl, 1:2], 1.0 / H, t["stat"][:, tl, 3:4],
            op0=OP.mult, op1=OP.subtract), waits=[msq, sqe[tl]], inc="sV"))
    sd_ev = [c.op("scalar", lambda s, tl=tl: s.activation(
        t["stat"][:, tl, 5:6], t["stat"][:, tl, 4:5], AF.Sqrt, bias=t["epsc"][:, 0:1]),
        waits=[var_ev[tl], st["eps_ok"]], inc="sA") for tl in range(4)]
    ap_ev = []
    for tl in range(4):
        rc = c.op("vector", lambda v, tl=tl: v.reciprocal(
            t["stat"][:, tl, 6:7], t["stat"][:, tl, 5:6]), waits=[sd_ev[tl]], inc="sV")
        ap_ev.append(c.op("vector", lambda v, tl=tl: v.tensor_scalar(
            t["xt"][:, tl, :], t["xt"][:, tl, :], t["stat"][:, tl, 2:3],
            t["stat"][:, tl, 6:7], op0=OP.subtract, op1=OP.mult),
            waits=[rc], inc="sV"))
    out["apply"] = ap_ev
    lnx_w = [None] * 4
    if want_lnx:
        for tl in range(4):
            lnx_w[tl] = c.op("sync", lambda s, tl=tl: s.dma_start(
                out=t["lnx"][row0 + tl * 128: row0 + (tl + 1) * 128, :],
                in_=t["xt"][:, tl, :]),
                waits=[ap_ev[tl], st["lnx_war"]], inc=f"xt{tl}", inc_n=16)
        out["lnx_w"] = lnx_w
    evac = {}
    tr_last = {}
    pfree = st["p67_free"]
    for tl in range(4):
        for hc in range(8):
            sl = st["p67i"] % 2
            st["p67i"] += 1
            tr = c.op("tensor", lambda te, tl=tl, hc=hc, sl=sl: te.transpose(
                t[f"p{6 + sl}"][:, :128], t["xt"][:, tl, hc * 128:(hc + 1) * 128],
                t["ident_sb"][:, :]),
                waits=[ap_ev[tl], pfree[sl], st["wT_done"],
                       st.get(("w2evac", 6 + sl))], inc="sT")
            ev1 = c.op("scalar", lambda s, tl=tl, hc=hc, sl=sl: s.copy(
                t["lnT"][:, hc, tl * 128:(tl + 1) * 128], t[f"p{6 + sl}"][:, :128]),
                waits=[tr, st.get("lnT_war")], inc="sA")
            if gate:
                ev1 = c.op("scalar", lambda s, tl=tl, hc=hc, sl=sl: s.copy(
                    t["lnT32"][:, hc, :], t[f"p{6 + sl}"][:, :128]),
                    waits=[tr, st.get("gate_mm_war")], inc="sA")
            pfree[sl] = ev1
            evac[(tl, hc)] = ev1
            tr_last[tl] = tr
        if gate:
            tile_in_b = grp * 4 + tl
            gm = None
            for hc in range(8):
                gm = c.op("tensor", lambda te, tl=tl, hc=hc: te.matmul(
                    t["p4"][:, 0:E], t["lnT32"][:, hc, :],
                    t["WgT"][:, hc, :], start=(hc == 0), stop=(hc == 7)),
                    waits=[evac[(tl, hc)], st.get("gate_war"), st.get("gate_war2")],
                    inc="sT" if hc == 7 else None)
            st["gate_mm_war"] = gm
            ex = c.op("scalar", lambda s, tl=tl: s.activation(
                t["smtok"][:, tl, :], t["p4"][:, 0:E], AF.Exp,
                accum_out=t["stat"][:, tl, 7:8]), waits=[gm], inc="sA")
            st["gate_war"] = ex
            rcg = c.op("vector", lambda v, tl=tl: v.reciprocal(
                t["stat"][:, tl, 3:4], t["stat"][:, tl, 7:8]), waits=[ex], inc="sV")
            dv = c.op("vector", lambda v, tl=tl: v.tensor_single_scalar(
                t["smtok"][:, tl, :], t["smtok"][:, tl, :], t["stat"][:, tl, 3:4],
                OP.mult), waits=[rcg], inc="sV")
            str_ = c.op("tensor", lambda te, tl=tl: te.transpose(
                t["p5"][0:E, 0:128], t["smtok"][:, tl, :],
                t["ident_sb"][:, :]),
                waits=[dv, st.get("smT_tr_war"), st.get("p5_exp_war"),
                       st.get(("w2evac", 5))], inc="sT")
            smev = c.op("scalar", lambda s, tl=tl, tb=tile_in_b, b=b: s.copy(
                t["smT"][b * 32:b * 32 + E, tb * 128:(tb + 1) * 128],
                t["p5"][0:E, 0:128]),
                waits=[str_, st.get("smT_war"), st.get("smT_war2"),
                       st["smT_init"]], inc="sA")
            st["smT_tr_war"] = smev
            out["smT_last"] = smev
    st["p67_free"] = pfree
    out["evac"] = evac
    for tl in range(4):
        st["xt_war"][tl] = [tr_last[tl], lnx_w[tl]]
    st["xt_bar"] = []
    return out


def emit_topk(c, st, call):
    t = c.t
    cp = c.op("vector", lambda v: v.tensor_copy(t["smW"][:, :], t["smT"][:, :]),
              waits=[st["smT_last_ev"], st.get("smW_wl2")] + st.get("topk_war", []),
              inc="sV")
    st["smT_war"] = cp
    st["smT_free"] = cp
    prev = cp
    for it in range(KSEL // 8):
        sl = slice(it * 8, it * 8 + 8)
        mx = c.op("vector", lambda v, sl=sl: v.max(out=t["Gsb"][:, sl], in_=t["smW"][:, :]),
                  waits=[prev], inc="sV")
        mi = c.op("vector", lambda v, sl=sl: v.max_index(
            t["Isb"][:, sl], t["Gsb"][:, sl], t["smW"][:, :]), waits=[mx], inc="sV")
        prev = c.op("vector", lambda v, sl=sl: v.match_replace(
            out=t["smW"][:, :], in_to_replace=t["Gsb"][:, sl], in_values=t["smW"][:, :],
            imm_value=NEG), waits=[mi], inc="sV")
    st["smW_free"] = prev
    gs = c.op("vector", lambda v: v.tensor_single_scalar(
        t["Gsb"][:, :], t["Gsb"][:, :], 0.5, OP.mult), waits=[prev], inc="sV")
    for b in range(B):
        iw = c.op("sync", lambda s, b=b: s.dma_start(
            out=t["idxd"][b, :, :], in_=t["Isb"][b * 32:b * 32 + E, :]),
            waits=[prev], inc="tkw", inc_n=16)
        gw = c.op("sync", lambda s, b=b: s.dma_start(
            out=t["gd"][b, :, :], in_=t["Gsb"][b * 32:b * 32 + E, :]),
            waits=[gs], inc="tkw", inc_n=16)
    st["topk_war"] = [iw, gw]
    return iw, gw


def emit_ffn(c, st, call):
    t = c.t
    scd = t["xr" if call == 0 else "xr2"]

    if not st.pop("skip_ln", False):
        for b in range(B):
            for grp in range(4):
                ev = emit_ln_group(c, st, "xr", b, grp, want_lnx=True, gate=True)
                st["smT_last_ev"] = ev["smT_last"]
    emit_topk(c, st, call)
    tkw_full = ("tkw", c.cnt["tkw"])
    lnx_all = [(f"xt{i}", c.cnt[f"xt{i}"]) for i in range(4)]

    if c.stage < 2:
        st["ffn_done"] = [tkw_full]
        return

    w1_free = st.setdefault("w1_free4", [None] * 3)
    w2_free = st.setdefault("w2_free8", [None] * 8)
    geluevs = {}
    scat_last = st["scat_last"]
    contrib_free = st.get("contrib_free", [None] * 4)

    def w1_load(e, fcg):
        slot = fcg % 3
        la = c.op("gpsimd", lambda g, e=e, fcg=fcg, slot=slot: g.dma_start(
            out=t["W1sb"][:, slot, 0:4, :],
            in_=t["W1"][e, :, fcg * 512:(fcg + 1) * 512].rearrange(
                "(hc p) f -> p hc f", p=128)[:, 0:4, :]),
            waits=[w1_free[slot], st.get("smT_free")], inc=f"w1s{slot}", inc_n=16)
        lb = c.op("gpsimd", lambda g, e=e, fcg=fcg, slot=slot: g.dma_start(
            out=t["W1sb"][:, slot, 4:8, :],
            in_=t["W1"][e, :, fcg * 512:(fcg + 1) * 512].rearrange(
                "(hc p) f -> p hc f", p=128)[:, 4:8, :]),
            inc=f"w1s{slot}", inc_n=16)
        st[("w1ld", e, fcg)] = (la, lb)

    def w2_load(e, j):
        slot, hh, fc = j % 8, j // 32, j % 32
        st[("w2ld", e, j)] = st["last_w2ld"] = c.op(
            "gpsimd", lambda g, e=e, hh=hh, fc=fc, slot=slot: g.dma_start(
                out=t["W2sb"][:, slot, :],
                in_=t["W2"][e, fc * 128:(fc + 1) * 128, hh * 512:(hh + 1) * 512]),
            waits=[w2_free[slot], st.get("smW_free")], inc=f"w2s{slot}", inc_n=16)

    def idx_loads(e):
        par = e % 2
        for b in range(B):
            for sc in range(4):
                c.op("scalar", lambda s, b=b, sc=sc, e=e, par=par: s.dma_start(
                    out=t["idxcol"][:, par, b, sc:sc + 1],
                    in_=t["idxd"][b, e, sc * 128:(sc + 1) * 128]
                    .rearrange("(p o) -> p o", o=1)),
                    waits=[tkw_full] + st.get(("idx_free", par), []),
                    inc="dsc", inc_n=16)
                c.op("scalar", lambda s, b=b, sc=sc, e=e, par=par: s.dma_start(
                    out=t["ccol"][:, par, b, sc:sc + 1],
                    in_=t["gd"][b, e, sc * 128:(sc + 1) * 128]
                    .rearrange("(p o) -> p o", o=1)),
                    waits=[tkw_full] + st.get(("idx_free", par), []),
                    inc="dsc", inc_n=16)
        st[("dscf", e)] = ("dsc", c.cnt["dsc"])

    def gathers(e):
        par = e % 2
        prev_sg = ("sg", c.cnt["sg"])
        for b in range(B):
            for sc in range(4):
                w = [st[("dscf", e)], prev_sg, st.get("exT_tr_last"),
                     st["wT_done"]] + lnx_all
                c.op("gpsimd", lambda g, b=b, sc=sc, par=par: g.indirect_dma_start(
                    out=t["ex_raw"][:, b, sc, :], out_offset=None,
                    in_=t["lnx"][:, :],
                    in_offset=IndirectOffsetOnAxis(t["idxcol"][:, par, b, sc:sc + 1], 0),
                    element_offset=b * S * H),
                    waits=w, inc="sg", inc_n=16)
        st[("sgf", e)] = ("sg", c.cnt["sg"])

    def b2_load(e):
        st[("b2ld", e)] = c.op("gpsimd", lambda g, e=e: g.dma_start(
            out=t["b2row"][:1, e % 2, :], in_=t["b2"][e:e + 1, :]),
            waits=[st.get(("b2free", e % 2))], inc="b2s", inc_n=16)

    for fcg in range(3):
        w1_load(0, fcg)
    b2_load(0)
    idx_loads(0)
    gathers(0)
    for j in range(4):
        w2_load(0, j)

    for e in range(E):
        scat_all = [(f"sb{i}", c.cnt[f"sb{i}"]) for i in range(4)]
        par = e % 2
        sg_full = st[("sgf", e)]
        dsc_full = st[("dscf", e)]
        b2l = st[("b2ld", e)]
        pfree = st["p67_free"]
        exT_evac = {}
        tr = None
        for b in range(B):
            for sc in range(4):
                for hc in range(8):
                    sl = st["p67i"] % 2
                    st["p67i"] += 1
                    tr = c.op("tensor", lambda te, b=b, sc=sc, hc=hc, sl=sl: te.transpose(
                        t[f"p{6 + sl}"][:, :128],
                        t["ex_raw"][:, b, sc, hc * 128:(hc + 1) * 128],
                        t["ident_sb"][:, :]),
                        waits=[sg_full, pfree[sl], st.get(("w2evac", 6 + sl))],
                        inc="sT")
                    pfree[sl] = c.op("scalar", lambda s, b=b, sc=sc, hc=hc, sl=sl: s.copy(
                        t["exT"][:, b, hc, sc * 128:(sc + 1) * 128],
                        t[f"p{6 + sl}"][:, :128]),
                        waits=[tr], inc="sA")
                    exT_evac[(b, sc, hc)] = pfree[sl]
        st["exT_tr_last"] = tr
        last_w1 = None
        for fcg in range(8):
            if ("w1ld", e, fcg) not in st:
                w1_load(e, fcg)
            la, lb = st.pop(("w1ld", e, fcg))
            slot = fcg % 3
            for fcl in range(4):
                fc = fcg * 4 + fcl
                for b in range(B):
                    bank = 2 * (fc % 2) + b
                    if fc < 2:
                        bfree = st.get(("w2evac", bank))
                    else:
                        bfree = geluevs[(call, e, fc - 2, b)]
                    mm = None
                    for hc in range(8):
                        w = [exT_evac[(b, 3, hc)]]
                        if hc == 0:
                            w += [la, lb, bfree]
                        mm = c.op("tensor",
                                  lambda te, fcl=fcl, b=b, hc=hc, bank=bank, slot=slot:
                                  te.matmul(t[f"p{bank}"][:, :],
                                            t["W1sb"][:, slot, hc, fcl * 128:(fcl + 1) * 128],
                                            t["exT"][:, b, hc, :],
                                            start=(hc == 0), stop=(hc == 7)),
                                  waits=w, inc="sT" if hc == 7 else None)
                    last_w1 = mm
                    geluevs[(call, e, fc, b)] = c.op(
                        "scalar", lambda s, fc=fc, b=b, bank=bank, e=e: s.activation(
                            t["h1sb"][:, b, fc, :], t[f"p{bank}"][:, :], AF.Gelu_apprx_tanh,
                            bias=t["b1all"][:, e, fc:fc + 1]),
                        waits=[mm, st["b1_ld"], st.get("h1_war")], inc="sA")
            w1_free[slot] = last_w1
        for hh in range(2):
            mmstop = {}
            for fc in range(32):
                j = hh * 32 + fc
                if ("w2ld", e, j) not in st:
                    w2_load(e, j)
                wl2 = st.pop(("w2ld", e, j))
                slot = j % 8
                lastmm = None
                for b in range(B):
                    for sc in range(4):
                        bank = b * 4 + sc
                        if fc == 0:
                            if hh == 0 and bank < 4:
                                bfree = geluevs[(call, e, 30 + (bank >> 1), bank & 1)]
                            else:
                                bfree = st.get(("w2evac", bank))
                            c.op("tensor", lambda te, bank=bank, hh=hh, e=e: te.matmul(
                                t[f"p{bank}"][:, :], t["ones_sb"][:1, :],
                                t["b2row"][:1, e % 2, hh * 512:(hh + 1) * 512],
                                start=True, stop=False),
                                waits=[bfree, b2l, st["ones_ok"],
                                       geluevs[(call, e, 31, b)]])
                        lastmm = c.op("tensor",
                                      lambda te, b=b, sc=sc, fc=fc, bank=bank, slot=slot:
                                      te.matmul(t[f"p{bank}"][:, :],
                                                t["h1sb"][:, b, fc, sc * 128:(sc + 1) * 128],
                                                t["W2sb"][:, slot, :],
                                                start=False, stop=(fc == 31)),
                                      waits=[wl2],
                                      inc="sT" if (fc == 31 or (b == B - 1 and sc == 3))
                                      else None)
                        if fc == 31:
                            mmstop[(b, sc)] = lastmm
                w2_free[slot] = lastmm
            if hh == 0:
                if e + 1 < E:
                    b2_load(e + 1)
                    idx_loads(e + 1)
                    gathers(e + 1)
                    for fcg in range(3):
                        w1_load(e + 1, fcg)
                for j2 in range(32, 36):
                    w2_load(e, j2)
            scat_evs = st.setdefault(("scat_evs", par), [])
            if hh == 0:
                scat_evs.clear()
            for b in range(B):
                for sc in range(4):
                    bank = b * 4 + sc
                    buf = (b * 2 + hh * 4 + sc) % 4
                    ev = c.op("vector", lambda v, b=b, sc=sc, bank=bank, buf=buf, par=par:
                              v.tensor_single_scalar(t["contrib"][:, buf, :],
                                                     t[f"p{bank}"][:, :],
                                                     t["ccol"][:, par, b, sc:sc + 1],
                                                     OP.mult),
                              waits=[mmstop[(b, sc)], dsc_full, contrib_free[buf]],
                              inc="sV")
                    st[("w2evac", bank)] = ev
                    scat = c.op("gpsimd", lambda g, b=b, sc=sc, hh=hh, buf=buf, scd=scd,
                                par=par:
                                g.indirect_dma_start(
                                    out=scd[:, :],
                                    out_offset=IndirectOffsetOnAxis(
                                        t["idxcol"][:, par, b, sc:sc + 1], 0),
                                    in_=t["contrib"][:, buf, :], in_offset=None,
                                    element_offset=b * S * H + hh * 512,
                                    compute_op=OP.add),
                                waits=[ev] + scat_all + [st.get("ydone")],
                                inc=f"sb{buf}", inc_n=16)
                    contrib_free[buf] = scat
                    scat_last[b] = scat
                    scat_evs.append(scat)
            st[("b2free", e % 2)] = mmstop[(1, 3)]
        st[("idx_free", par)] = list(st[("scat_evs", par)])
    st["smW_wl2"] = st.get("last_w2ld")
    st["contrib_free"] = contrib_free
    st["h1_war"] = None
    st["ffn_done"] = [(f"sb{i}", c.cnt[f"sb{i}"]) for i in range(4)]


def emit_attn_kv(c, st):
    t = c.t
    st["xt_bar"] = list(st["ffn_done"])
    st["p0_att"] = st.get(("w2evac", 0))
    st["p1_att"] = st.get(("w2evac", 1))
    vfree = [st.get(("w2evac", 2)), st.get(("w2evac", 3))]
    kq_evac, vown_ev = [], []
    mm = ve = None
    for b in range(B):
        for grp in range(4):
            ev = emit_ln_group(c, st, "xr", b, grp, want_lnx=False, gate=False)
            evac = ev["evac"]
            col0 = b * S + grp * 512
            for name, bank, wtn in [("k", 0, "WkT"), ("q", 1, "WqT")]:
                mm = None
                for hc in range(8):
                    w = [evac[(3, hc)]] + ([st[f"p{bank}_att"]] if hc == 0 else [])
                    mm = c.op("tensor", lambda te, hc=hc, bank=bank, wtn=wtn: te.matmul(
                        t[f"p{bank}"][0:DH, :], t[wtn][:, hc, :], t["lnT"][:, hc, :],
                        start=(hc == 0), stop=(hc == 7)),
                        waits=w, inc="sT" if hc == 7 else None)
                if name == "k":
                    evk = c.op("scalar", lambda s, col0=col0: s.activation(
                        t["kT_own"][:, col0:col0 + 512], t["p0"][0:DH, :], AF.Identity,
                        bias=t["bk_sb"][:, 0:1]), waits=[mm], inc="sA")
                    st["p0_att"] = evk
                    kq_evac.append(evk)
                else:
                    evq = c.op("scalar", lambda s, col0=col0: s.activation(
                        t["qT_sb"][0:DH, col0:col0 + 512], t["p1"][0:DH, :], AF.Identity,
                        bias=t["bq32"][:, 0:1], scale=QSCALE),
                        waits=[mm, st["bq_ok"]], inc="sA")
                    evq = c.op("scalar", lambda s, col0=col0: s.activation(
                        t["qT_sb"][DH:2 * DH, col0:col0 + 512], t["p1"][0:DH, :],
                        AF.Identity, bias=t["bq32"][:, 0:1], scale=QSCALE),
                        waits=[mm, st["bq_ok"]], inc="sA")
                    st["p1_att"] = evq
                    kq_evac.append(evq)
            for tl in range(4):
                tile = col0 // 128 + tl
                par = tl % 2
                c.op("tensor", lambda te, par=par: te.matmul(
                    t[f"p{2 + par}"][:, 0:DH], t["ones_sb"][:1, :], t["bv_sb"][:1, :],
                    start=True, stop=False),
                    waits=[evac[(tl, 7)], vfree[par], st["bv_ok"], st["ones_ok"]])
                mm = None
                for hc in range(8):
                    mm = c.op("tensor", lambda te, tl=tl, hc=hc, par=par: te.matmul(
                        t[f"p{2 + par}"][:, 0:DH], t["lnT"][:, hc, tl * 128:(tl + 1) * 128],
                        t["WvT"][:, hc, :], start=False, stop=(hc == 7)),
                        inc="sT" if hc == 7 else None)
                ve = c.op("scalar", lambda s, tile=tile, par=par: s.copy(
                    t["v_own"][:, tile, :], t[f"p{2 + par}"][:, 0:DH]),
                    waits=[mm], inc="sA")
                vfree[par] = ve
                vown_ev.append(ve)
            st["lnT_war"] = mm
    st["kq_evac"] = kq_evac
    st["vown_ev"] = vown_ev
    st["vfree"] = vfree


def emit_out_a(c, st):
    t = c.t
    c.op("sync", lambda s: s.dma_start(out=t["x1o"][:, :], in_=t["xr"][:, :]),
         waits=list(st["ffn_done"]), inc="ycw", inc_n=16)
    c.op("sync", lambda s: s.dma_start(
        out=t["kvo"][0:DH * T_OWN].rearrange("(p f) -> p f", p=DH),
        in_=t["kT_own"][:, :]), waits=st["kq_evac"], inc="ycw", inc_n=16)
    c.op("sync", lambda s: s.dma_start(
        out=t["kvo"][DH * T_OWN:2 * DH * T_OWN].rearrange(
            "(tile p f) -> p tile f", p=128, f=DH),
        in_=t["v_own"][:, :, :]), waits=st["vown_ev"], inc="ycw", inc_n=16)
    c.op("sync", lambda s: s.dma_start(
        out=t["kvo"][2 * DH * T_OWN:].rearrange("(p f) -> p f", p=DH),
        in_=t["qT_sb"][0:DH, :]), waits=st["kq_evac"], inc="ycw", inc_n=16)


def emit_attn_q(c, st, ln_cb=None):
    t = c.t
    NCH = c.t_all // 128
    kv_ok = []
    if c.launch in ("B", "B3", "Bdbg"):
        kv_ok.append(c.op("sync", lambda s: s.dma_start(
            out=t["kT_full"][:, :], in_=t["kTf_in"][:, :]),
            waits=[st["stp_full"], st["wT_done"]], inc="kvl", inc_n=16))
        kv_ok.append(c.op("sync", lambda s: s.dma_start(
            out=t["v_aug"][:, :, 0:DH],
            in_=t["vf_in"][:, :].rearrange("p (c f) -> p c f", f=DH)),
            waits=[st["stp_full"], st["wT_done"]], inc="kvl", inc_n=16))
        for half in range(2):
            kv_ok.append(c.op("sync", lambda s, half=half: s.dma_start(
                out=t["qT_sb"][half * DH:(half + 1) * DH, :], in_=t["qT_in"][:, :]),
                waits=[st["stp_full"]], inc="kvl", inc_n=16))
        vfree = [None, None]
        yfree0 = None
        st["ytr_free"] = [None, None]
    else:
        e1 = c.op("sync", lambda s: s.dma_start(
            out=t["kvin"][0:DH * T_OWN].rearrange("(p f) -> p f", p=DH),
            in_=t["kT_own"][:, :]), waits=st["kq_evac"], inc="kvl", inc_n=16)
        e2 = c.op("sync", lambda s: s.dma_start(
            out=t["kvin"][DH * T_OWN:].rearrange("(tile p f) -> p tile f", p=128, f=DH),
            in_=t["v_own"][:, :, :]), waits=st["vown_ev"], inc="kvl", inc_n=16)
        cc = c.op("gpsimd", lambda g: g.collective_compute(
            "AllGather", OP.bypass, replica_groups=[list(range(c.ncore))],
            ins=[t["kvin"].ap().opt()], outs=[t["kvout"].ap().opt()]),
            waits=[("kvl", c.cnt["kvl"])], inc="scc", inc_n=16)
        for r in range(c.ncore):
            for half in range(2):
                kv_ok.append(c.op("sync", lambda s, r=r, half=half: s.dma_start(
                    out=t["kT_full"][half * DH:(half + 1) * DH,
                                     r * (T_OWN // 2):(r + 1) * (T_OWN // 2)],
                    in_=t["kvout"][r, 0:DH * T_OWN].rearrange(
                        "(p cp two f) -> two p cp f", p=DH, two=2, f=128)[half]),
                    waits=[cc] + st["ffn_done"], inc="kvl", inc_n=16))
            kv_ok.append(c.op("sync", lambda s, r=r: s.dma_start(
                out=t["v_aug"][:, r * 32:(r + 1) * 32, 0:DH],
                in_=t["kvout"][r, DH * T_OWN:].rearrange(
                    "(tile p f) -> p tile f", p=128, f=DH)),
                waits=[cc], inc="kvl", inc_n=16))
        vfree = st["vfree"]
        yfree0 = vfree[0]
        st["ytr_free"] = [vfree[1], None]
    kvl_full = ("kvl", c.cnt["kvl"])
    vaug_ones = c.op("vector", lambda v: v.memset(t["v_aug"][:, :, DH:DH + 1], 1.0),
                     waits=[kvl_full], inc="sV")
    kv_ok = [kvl_full, vaug_ones]

    yfree = yfree0
    exp_ev = {}
    smm_ev = {}
    ymm_ev = {}
    SB = [0, 1, 5]
    ycp = None
    for g in range(T_OWN // 512):
        for ch in range(NCH + 2):
            if ch < NCH:
                par = SB[ch % 3]
                w = (kv_ok if ch < 3 else []) + [exp_ev.get((g, ch - 3))]
                if ch < 3:
                    w = w + [st.get(("w2evac", par))]
                if par == 5:
                    w = w + [st.get("smT_tr_war")]
                smm_ev[(g, ch)] = c.op("tensor", lambda te, ch=ch, g=g, par=par: te.matmul(
                    t[f"p{par}"][:, :],
                    t["kT_full"][(ch % 2) * DH:(ch % 2) * DH + DH,
                                 (ch // 2) * 128:(ch // 2) * 128 + 128],
                    t["qT_sb"][(ch % 2) * DH:(ch % 2) * DH + DH,
                               g * 512:(g + 1) * 512],
                    start=True, stop=True),
                    waits=w, inc="sT")
            if ch >= 2:
                cy = ch - 2
                ymm_ev[(g, cy)] = c.op("tensor", lambda te, cy=cy: te.matmul(
                    t["p2"][0:DH + 1, :], t["v_aug"][:, cy, :],
                    t["E_sb"][:, cy % 2, :],
                    start=(cy == 0), stop=(cy == NCH - 1)),
                    waits=[exp_ev[(g, cy)]] +
                          ([yfree, st.get("yT_ok")] if cy == 0 else []),
                    inc="sT")
            if ch < NCH:
                exp_ev[(g, ch)] = c.op("scalar", lambda s, ch=ch, par=SB[ch % 3]:
                                       s.activation(
                    t["E_sb"][:, ch % 2, :], t[f"p{par}"][:, :], AF.Exp),
                    waits=[smm_ev[(g, ch)], ymm_ev.get((g, ch - 2))], inc="sA")
        ycp = c.op("scalar", lambda s: s.copy(t["yT"][0:DH + 1, :], t["p2"][0:DH + 1, :]),
                   waits=[ymm_ev[(g, NCH - 1)]], inc="sA")
        yfree = ycp
        for tl in range(4):
            par = tl % 2
            tr = c.op("tensor", lambda te, tl=tl, par=par: te.transpose(
                t[f"p{3 + par}"][:, :128], t["yT"][:, tl * 128:(tl + 1) * 128],
                t["ident_sb"][:, :]),
                waits=[ycp, st["ytr_free"][par],
                       (st.get("gate_war") if par == 1 else None)], inc="sT")
            cpy = c.op("vector", lambda v, tl=tl, par=par: v.tensor_copy(
                t["ytmp"][:, par, :], t[f"p{3 + par}"][:, 0:DH + 1]), waits=[tr], inc="sV")
            st["ytr_free"][par] = cpy
            rcy = c.op("vector", lambda v, par=par: v.reciprocal(
                t["dcol"][:, par:par + 1], t["ytmp"][:, par, DH:DH + 1]),
                waits=[cpy], inc="sV")
            dv = c.op("vector", lambda v, tl=tl, par=par: v.tensor_single_scalar(
                t["ytok"][:, tl, :], t["ytmp"][:, par, 0:DH], t["dcol"][:, par:par + 1],
                OP.mult), waits=[rcy], inc="sV")
            tile = g * 4 + tl
            row0 = tile * 128
            ld = c.op("sync", lambda s, tl=tl, row0=row0: s.dma_start(
                out=t["xt"][:, tl, :], in_=t["xr"][row0:row0 + 128, :]),
                waits=st["xt_war"][tl] + st["xt_bar"], inc=f"xt{tl}", inc_n=16)
            add = None
            for j in range(16):
                add = c.op("vector", lambda v, tl=tl, j=j: v.tensor_add(
                    t["xt"][:, tl, j * DH:(j + 1) * DH],
                    t["xt"][:, tl, j * DH:(j + 1) * DH], t["ytok"][:, tl, :]),
                    waits=[ld, dv] if j == 0 else [],
                    inc="sV" if j == 15 else None)
            wb = c.op("sync", lambda s, tl=tl, row0=row0: s.dma_start(
                out=t["xr"][row0:row0 + 128, :], in_=t["xt"][:, tl, :]),
                waits=[add], inc=f"wb{tl}", inc_n=16)
            st["xt_war"][tl] = [wb]
        if ln_cb is not None:
            st["gate_war2"] = st["ytr_free"][1]
            last_p5 = max(ch for ch in range(NCH) if [0, 1, 5][ch % 3] == 2 or
                          [0, 1, 5][ch % 3] == 5)
            st["p5_exp_war"] = exp_ev[(g, last_p5)]
            ln_cb(g)
    st["xt_bar"] = []
    st["att_done"] = [(f"wb{i}", c.cnt[f"wb{i}"]) for i in range(4)]
    gl = T_OWN // 512 - 1
    for par in (0, 1, 5):
        last_ch = max(ch for ch in range(NCH) if [0, 1, 5][ch % 3] == par)
        st[("w2evac", par)] = exp_ev[(gl, last_ch)]
    st[("w2evac", 2)] = ycp
    st[("w2evac", 3)] = st["ytr_free"][0]
    st[("w2evac", 4)] = st["ytr_free"][1]
    st["gate_war"] = st["ytr_free"][1]
    st["h1_war"] = ("sT", c.cnt["sT"])


def emit_copy_y(c, st):
    t = c.t
    dst = "y" if c.launch == "B3" else "xr2"
    ev = c.op("sync", lambda s, dst=dst: s.dma_start(
        out=t[dst][:, :], in_=t["xr"][:, :]),
        waits=st["att_done"], inc="ycw", inc_n=16)
    st["ydone"] = ev


def emit_final(c, st):
    t = c.t
    if c.launch == "Bdbg":
        fw = list(st.get("ffn_done", []))
        c.op("sync", lambda s: s.dma_start(
            out=t["d_ex"][:, :], in_=t["ex_raw"].reshape([128, B * 4 * H])[:, :]),
            waits=fw, inc="dbg", inc_n=16)
        c.op("sync", lambda s: s.dma_start(
            out=t["d_h1"][:, :], in_=t["h1sb"].reshape([128, B * 32 * 512])[:, :]),
            waits=fw, inc="dbg", inc_n=16)
        c.op("sync", lambda s: s.dma_start(
            out=t["d_ext"][:, :], in_=t["exT"].reshape([128, B * 8 * 512])[:, :]),
            waits=fw, inc="dbg", inc_n=16)
        c.op("sync", lambda s: s.dma_start(out=t["d_idx"][:, :, :], in_=t["idxd"][:, :, :]),
             waits=[("tkw", c.cnt["tkw"])], inc="dbg", inc_n=16)
        c.op("sync", lambda s: s.dma_start(out=t["d_g"][:, :, :], in_=t["gd"][:, :, :]),
             waits=[("tkw", c.cnt["tkw"])], inc="dbg", inc_n=16)
        c.op("sync", lambda s: s.dma_start(out=t["d_lnx"][:, :], in_=t["lnx"][:, :]),
             waits=[("sg", c.cnt["sg"])] + [(f"xt{i}", c.cnt[f"xt{i}"]) for i in range(4)],
             inc="dbg", inc_n=16)
        c.op("sync", lambda s: s.dma_start(out=t["d_sm"][0, :, :], in_=t["smT"][:, :]),
             waits=[("sA", c.cnt["sA"])], inc="dbg", inc_n=16)
    if c.launch in ("full", "B", "Bdbg"):
        c.op("sync", lambda s: s.dma_start(out=t["y"][:, :], in_=t["xr2"][:, :]),
             waits=list(st.get("ffn_done", [])), inc="ycw", inc_n=16)
    if c.stage < 4:
        w = list(st.get("ffn_done", [])) + list(st.get("att_done", []))
        c.op("sync", lambda s: s.dma_start(out=t["y"][:, :], in_=t["xr"][:, :]),
             waits=[x for x in w if x], inc="ycw", inc_n=16)
        if "d_lnx" in t:
            c.op("sync", lambda s: s.dma_start(out=t["d_lnx"][:, :], in_=t["lnx"][:, :]),
                 waits=[("sg", c.cnt["sg"])] + [(f"xt{i}", c.cnt[f"xt{i}"]) for i in range(4)],
                 inc="dbg", inc_n=16)
    allsems = (["stp", "b1s", "lnx", "tkw", "dsc"] +
               [f"w1s{i}" for i in range(3)] + [f"w2s{i}" for i in range(8)] +
               ["b2s", "sg", "scc", "kvl", "ycw", "dbg",
                "xt0", "xt1", "xt2", "xt3", "wb0", "wb1", "wb2", "wb3",
                "sb0", "sb1", "sb2", "sb3"])
    fin = [(sname, c.cnt[sname]) for sname in allsems if c.cnt[sname] > 0]
    c.op("sync", lambda s: s.nop(), waits=fin)
    c.op("gpsimd", lambda g: g.nop(), waits=fin)


_CACHE = {}


def _relayout_kv(kvo_list, ncore):
    t_all = ncore * T_OWN
    kT = [k[0:DH * T_OWN].reshape(DH, T_OWN) for k in kvo_list]
    vv = [k[DH * T_OWN:2 * DH * T_OWN].reshape(T_OWN, DH) for k in kvo_list]
    qT = [k[2 * DH * T_OWN:].reshape(DH, T_OWN) for k in kvo_list]
    kTg = np.concatenate(kT, axis=1)
    vg = np.concatenate(vv, axis=0)
    nch = t_all // 128
    kTf = np.zeros((128, t_all // 2), dtype=kTg.dtype)
    for ch in range(nch):
        half, colblk = ch % 2, ch // 2
        kTf[half * DH:(half + 1) * DH, colblk * 128:(colblk + 1) * 128] = \
            kTg[:, ch * 128:(ch + 1) * 128]
    vf = np.ascontiguousarray(
        vg.reshape(nch, 128, DH).transpose(1, 0, 2).reshape(128, nch * DH))
    return kTf, vf, qT


def kernel(L0_attn_embeddings, L0_attn_Wq, L0_attn_bq, L0_attn_Wk, L0_attn_bk,
           L0_attn_Wv, L0_attn_bv, L0_ffn_Wg, L0_ffn_W1, L0_ffn_b1, L0_ffn_W2,
           L0_ffn_b2):
    from concourse.bass_utils import run_bass_kernel_spmd
    ncore = 8
    x = np.ascontiguousarray(np.asarray(L0_attn_embeddings, dtype=np.float32))
    if "A" not in _CACHE:
        _CACHE["A"] = build(stage=4, ncore=ncore, launch="A")
        _CACHE["B"] = build(stage=4, ncore=ncore, launch="B")
    cA, cB = _CACHE["A"], _CACHE["B"]
    common = dict(
        Wg=np.ascontiguousarray(L0_ffn_Wg, dtype=np.float32),
        W1=np.ascontiguousarray(L0_ffn_W1, dtype=np.float32),
        b1=np.ascontiguousarray(L0_ffn_b1, dtype=np.float32),
        W2=np.ascontiguousarray(L0_ffn_W2, dtype=np.float32),
        b2=np.ascontiguousarray(L0_ffn_b2, dtype=np.float32),
    )
    commonA = dict(common,
                   Wq=np.ascontiguousarray(L0_attn_Wq, dtype=np.float32),
                   bq=np.ascontiguousarray(np.asarray(L0_attn_bq, np.float32).reshape(DH, 1)),
                   Wk=np.ascontiguousarray(L0_attn_Wk, dtype=np.float32),
                   bk=np.ascontiguousarray(np.asarray(L0_attn_bk, np.float32).reshape(DH, 1)),
                   Wv=np.ascontiguousarray(L0_attn_Wv, dtype=np.float32),
                   bv=np.ascontiguousarray(np.asarray(L0_attn_bv, np.float32).reshape(DH, 1)))
    in_maps = [dict(commonA, x0=np.ascontiguousarray(
        x[2 * i:2 * i + 2].reshape(T_OWN, H))) for i in range(ncore)]
    resA = run_bass_kernel_spmd(cA.nc, in_maps, core_ids=list(range(ncore)))
    kvo_list = [np.asarray(r["kvo"]).ravel() for r in resA.results]
    kTf, vf, qTl = _relayout_kv(kvo_list, ncore)
    in_mapsB = [dict(common,
                     x0=np.ascontiguousarray(
                         np.asarray(resA.results[i]["x1o"], np.float32).reshape(T_OWN, H)),
                     kTf_in=np.ascontiguousarray(kTf),
                     vf_in=np.ascontiguousarray(vf),
                     qT_in=np.ascontiguousarray(qTl[i]))
                for i in range(ncore)]
    resB = run_bass_kernel_spmd(cB.nc, in_mapsB, core_ids=list(range(ncore)))
    out = np.concatenate(
        [np.asarray(r["y"], dtype=np.float32).reshape(B, S, H) for r in resB.results],
        axis=0)
    return out


# version 13 (final)
# speedup vs baseline: 1.4693x; 1.0188x over previous
import numpy as np
import ml_dtypes
import concourse.bass as bass
import concourse.mybir as mybir
from concourse.bass_core_rust import IndirectOffsetOnAxis

F32 = mybir.dt.float32
BF16 = mybir.dt.bfloat16
U32 = mybir.dt.uint32
AF = mybir.ActivationFunctionType
OP = mybir.AluOpType
AX = mybir.AxisListType

B = 2
S = 2048
H = 1024
E = 8
KSEL = 512
FF = 4096
DH = 64
T_OWN = B * S
EPS = 1e-5
QSCALE = 1.0 / 32.0
NEG = -1e30
ENGINES = ["sync", "scalar", "vector", "tensor", "gpsimd"]


class Prog:
    def __init__(self, stage=4, ncore=8):
        self.nc = bass.Bass("TRN2", target_bir_lowering=False)
        self.stage = stage
        self.ncore = ncore
        self.t_all = ncore * T_OWN
        self.sem = {}
        self.cnt = {}
        self.t = {}
        self.ops = {e: [] for e in ENGINES}
        self.watermark = {e: {} for e in ENGINES}

    def mksem(self, *names):
        for n in names:
            self.sem[n] = self.nc.alloc_semaphore(n)
            self.cnt[n] = 0

    def op(self, eng, fn, waits=(), inc=None, inc_n=1):
        wl = {}
        wm = self.watermark[eng]
        for ev in waits:
            if ev is None:
                continue
            s, v = ev
            if v is None or wm.get(s, 0) >= v:
                continue
            wl[s] = max(wl.get(s, 0), v)
        for s, v in wl.items():
            wm[s] = v
        evo = None
        if inc is not None:
            self.cnt[inc] += inc_n
            evo = (inc, self.cnt[inc])
        self.ops[eng].append((fn, sorted(wl.items()), inc, inc_n))
        return evo

    def emit(self):
        with self.nc.Block() as blk:
            for eng_name in ENGINES:
                oplist = self.ops[eng_name]
                if not oplist:
                    continue

                def make(oplist=oplist):
                    def run(eng):
                        for fn, waits, inc, inc_n in oplist:
                            for s, v in waits:
                                eng.wait_ge(self.sem[s], v)
                            ins = fn(eng)
                            if inc is not None:
                                ins.then_inc(self.sem[inc], inc_n)
                    return run
                getattr(blk, eng_name)(make())


def build(stage=4, ncore=8, launch="full"):
    c = Prog(stage, ncore)
    c.launch = launch
    nc = c.nc
    t = c.t

    def dram(name, shape, dt, kind="Internal"):
        t[name] = nc.dram_tensor(name, shape, dt, kind=kind)

    dram("x0", [T_OWN, H], F32, "ExternalInput")
    wlist = [("Wg", [E, H]), ("W1", [E, H, FF]), ("b1", [E, FF]),
             ("W2", [E, FF, H]), ("b2", [E, H])]
    if launch in ("full", "A"):
        wlist += [("Wq", [DH, H]), ("bq", [DH, 1]), ("Wk", [DH, H]), ("bk", [DH, 1]),
                  ("Wv", [DH, H]), ("bv", [DH, 1])]
    for n, sh in wlist:
        dram(n, sh, F32, "ExternalInput")
    if launch == "A":
        dram("x1o", [T_OWN, H], F32, "ExternalOutput")
        dram("kvo", [3 * DH * T_OWN], BF16, "ExternalOutput")
        t["xr"] = t["x1o"]
    else:
        dram("y", [T_OWN, H], F32, "ExternalOutput")
    if launch in ("B", "B3", "Bdbg"):
        dram("kTf_in", [128, (c.t_all // 256) * 128], BF16, "ExternalInput")
        dram("vf_in", [128, (c.t_all // 128) * DH], BF16, "ExternalInput")
        dram("qT_in", [DH, T_OWN], BF16, "ExternalInput")
        t["xr"] = t["y"]
        t["xr2"] = t["y"]
    if launch == "full":
        dram("xr2", [T_OWN, H], F32)
        dram("kvin", [2 * DH * T_OWN], BF16)
        dram("kvout", [c.ncore, 2 * DH * T_OWN], BF16)
        dram("xr", [T_OWN, H], F32)
    dram("lnx", [T_OWN, H], F32)
    dram("idxd", [B, E, KSEL], U32)
    dram("gd", [B, E, KSEL], F32)
    t["ident"] = nc.inline_tensor(np.eye(128, dtype=np.float32), "identc")

    if launch == "Bdbg":
        dram("d_ex", [128, B * 4 * H], F32, "ExternalOutput")
        dram("d_h1", [128, B * 32 * 512], U32 if False else BF16, "ExternalOutput")
        dram("d_ext", [128, B * 8 * 512], BF16, "ExternalOutput")
    if c.stage < 4 or launch == "Bdbg":
        dram("d_lnx", [T_OWN, H], F32, "ExternalOutput")
        dram("d_sm", [B, E, S], F32, "ExternalOutput")
        dram("d_idx", [B, E, KSEL], U32, "ExternalOutput")
        dram("d_g", [B, E, KSEL], F32, "ExternalOutput")

    def sb(name, shape, dt):
        t[name] = nc.alloc_sbuf_tensor(name, shape, dt)

    has_attw = launch in ("full", "A")
    has_q = launch in ("full", "B", "B3", "Bdbg")
    sb("ident_sb", [128, 128], F32)
    sb("ones_sb", [1, 128], BF16)
    sb("WgT", [128, 8, E], F32)
    if has_attw:
        sb("WqT", [128, 8, DH], BF16)
        sb("WkT", [128, 8, DH], BF16)
        sb("WvT", [128, 8, DH], BF16)
        sb("bq32", [DH, 1], F32)
        sb("bk_sb", [DH, 1], F32)
        sb("bv_f", [1, DH], F32)
        sb("bv_sb", [1, DH], BF16)
        sb("kT_own", [DH, T_OWN], BF16)
    sb("b2row", [1, 2, H], BF16)
    sb("b1all", [128, E, 32], F32)
    sb("xt", [128, 4, H], F32)
    sb("sq", [128, H], F32)
    sb("stat", [128, 4, 8], F32)
    sb("epsc", [128, 1], F32)
    sb("dcol", [128, 2], F32)
    sb("lnT", [128, 8, 512], BF16)
    sb("lnT32", [128, 8, 128], F32)
    sb("smtok", [128, 4, E], F32)
    sb("idxcol", [128, 2, B, 4], U32)
    sb("ccol", [128, 2, B, 4], F32)
    sb("contrib", [128, 4, 512], F32)

    base = (nc.sbuf_base + 31) & ~31

    def at(name, shape, dt, off):
        t[name] = nc.alloc_sbuf_tensor_at(name, shape, dt, offset=base + off)

    at("wq_f", [DH, H], F32, 0)
    at("wk_f", [DH, H], F32, 4096)
    at("wv_f", [DH, H], F32, 8192)
    at("wg_f", [E, H], F32, 12288)
    at("h1sb", [128, B, 32, 512], BF16, 0)
    kt_bytes = (c.t_all // 256) * 128 * 2
    if has_q:
        at("kT_full", [128, (c.t_all // 256) * 128], BF16, 0)
        at("v_aug", [128, c.t_all // 128, DH + 1], BF16, kt_bytes)
    rA = max(65536, (kt_bytes + (c.t_all // 128) * (DH + 1) * 2) if has_q else 0, 16384)
    rA = (rA + 31) & ~31
    at("ex_raw", [128, B, 4, H], F32, rA)
    at("exT", [128, B, 8, 512], BF16, rA + 32768)
    at("qT_sb", [128, T_OWN], BF16, rA)
    if has_attw:
        at("v_own", [128, T_OWN // 128, DH], BF16, rA + 8192)
    if has_q:
        at("E_sb", [128, 2, 512], BF16, rA + 12288)
        at("yT", [128, 512], F32, rA + 14336)
        at("ytmp", [128, 2, DH + 1], F32, rA + 16384)
        at("ytok", [128, 4, DH], F32, rA + 16960)
    rB = rA + 49152
    at("smT", [128, S], F32, rB)
    at("W1sb", [128, 3, 8, 512], BF16, rB)
    at("smW", [128, S], F32, rB + 24576)
    at("W2sb", [128, 8, 512], BF16, rB + 24576)
    at("Gsb", [128, KSEL], F32, rB + 32768)
    at("Isb", [128, KSEL], U32, rB + 34816)
    need = base + rB + 36864
    assert need <= nc.sbuf_top, (need, nc.sbuf_top, nc.sbuf_base)

    for i in range(8):
        t[f"p{i}"] = nc.alloc_psum_tensor(f"p{i}", [128, 512], F32)

    c.mksem("stp", "b1s", "lnx", "tkw", "dsc",
            *[f"kvk{i}" for i in range(4)], *[f"kvv{i}" for i in range(4)],
            *[f"w1s{i}" for i in range(3)], *[f"w2s{i}" for i in range(8)],
            "b2s", "sg", "scc", "kvl", "ycw", "dbg", "sT", "sV", "sA",
            *[f"xt{i}" for i in range(4)], *[f"wb{i}" for i in range(4)],
            *[f"sb{i}" for i in range(4)])

    st = {"xt_war": [[] for _ in range(4)], "xt_bar": []}
    emit_setup(c, st)
    if launch in ("full", "A"):
        emit_ffn(c, st, 0)
        emit_attn_kv(c, st)
    if launch == "A":
        emit_out_a(c, st)
    if launch in ("full", "B", "B3", "Bdbg"):
        if launch == "B3":
            emit_attn_q(c, st)
        else:
            def ln_cb(g, c=c, st=st):
                ev = emit_ln_group(c, st, "xr", g // 4, g % 4,
                                   want_lnx=True, gate=True)
                st["smT_last_ev"] = ev["smT_last"]
            emit_attn_q(c, st, ln_cb=ln_cb)
            if launch == "full":
                emit_copy_y(c, st)
            st["skip_ln"] = True
            emit_ffn(c, st, 1)
    emit_final(c, st)
    c.emit()
    return c


def emit_setup(c, st):
    t = c.t
    evs = {}
    has_attw = c.launch in ("full", "A")
    if c.launch in ("full", "A"):
        c.op("sync", lambda s: s.dma_start(out=t["xr"][:, :], in_=t["x0"][:, :]),
             inc="stp", inc_n=16)
    wl = [("Wg", "wg_f")]
    if has_attw:
        wl = [("Wq", "wq_f"), ("Wk", "wk_f"), ("Wv", "wv_f")] + wl
    for w, f in wl:
        evs[f] = c.op("sync", lambda s, a=w, b=f: s.dma_start(
            out=t[b][:, :], in_=t[a][:, :]), inc="stp", inc_n=16)
    if has_attw:
        evs["bq"] = c.op("sync", lambda s: s.dma_start(
            out=t["bq32"][:, :], in_=t["bq"][:, :]), inc="stp", inc_n=16)
        c.op("sync", lambda s: s.dma_start(out=t["bk_sb"][:, :], in_=t["bk"][:, :]),
             inc="stp", inc_n=16)
        evs["bv"] = c.op("sync", lambda s: s.dma_start(
            out=t["bv_f"][:1, :], in_=t["bv"][:, :].rearrange("a o -> o a")),
            inc="stp", inc_n=16)
    evs["id"] = c.op("sync", lambda s: s.dma_start(
        out=t["ident_sb"][:, :], in_=t["ident"][:, :]), inc="stp", inc_n=16)
    for e in range(E):
        for fc in range(32):
            st["b1_ld"] = c.op("scalar", lambda s, e=e, fc=fc: s.dma_start(
                out=t["b1all"][:, e, fc:fc + 1],
                in_=t["b1"][e, fc * 128:(fc + 1) * 128].rearrange("(p o) -> p o", o=1)),
                inc="b1s", inc_n=16)
    stp_full = ("stp", c.cnt["stp"])
    st["stp_full"] = stp_full
    st["xt_bar"] = [stp_full]
    st["b1_ld"] = ("b1s", c.cnt["b1s"])
    if has_attw:
        st["bq_ok"] = c.op("vector", lambda v: v.tensor_single_scalar(
            t["bq32"][:, :], t["bq32"][:, :], QSCALE, OP.mult),
            waits=[stp_full], inc="sV")
        st["bv_ok"] = c.op("vector", lambda v: v.tensor_copy(
            t["bv_sb"][:1, :], t["bv_f"][:1, :]), waits=[stp_full], inc="sV")
    st["ones_ok"] = c.op("vector", lambda v: v.memset(t["ones_sb"][:1, :], 1.0), inc="sV")
    st["eps_ok"] = c.op("vector", lambda v: v.memset(t["epsc"][:, :], EPS), inc="sV")
    if "yT" in t:
        st["yT_ok"] = c.op("vector", lambda v: v.memset(t["yT"][:, :], 0.0), inc="sV")
    st["smT_init"] = c.op("vector", lambda v: v.memset(t["smT"][:, :], 0.0), inc="sV")

    st["p67_free"] = [None, None]
    st["p67i"] = 0
    pfree = st["p67_free"]
    wT = []
    wtl = [("wg_f", "WgT", E)]
    if has_attw:
        wtl = [("wq_f", "WqT", DH), ("wk_f", "WkT", DH), ("wv_f", "WvT", DH)] + wtl
    for mat, dstn, p in wtl:
        np_in = DH if mat != "wg_f" else E
        for hc in range(8):
            sl = st["p67i"] % 2
            st["p67i"] += 1
            tr = c.op("tensor", lambda te, m=mat, hc=hc, sl=sl, q=np_in: te.transpose(
                t[f"p{6 + sl}"][:, :q], t[m][:, hc * 128:(hc + 1) * 128],
                t["ident_sb"][:q, :q]),
                waits=[stp_full, pfree[sl]], inc="sT")
            pfree[sl] = c.op("scalar", lambda s, d=dstn, hc=hc, sl=sl, q=p:
                             s.copy(t[d][:, hc, :], t[f"p{6 + sl}"][:, :q]),
                             waits=[tr], inc="sA")
            wT.append(pfree[sl])
    st["wT_done"] = wT[-1]
    st["scat_last"] = {0: None, 1: None}
    st["gate_war"] = None
    st["smT_tr_war"] = None
    st["smT_war"] = None
    st["lnx_war"] = None


def emit_ln_group(c, st, src_name, b, grp, want_lnx, gate):
    t = c.t
    out = {}
    row0 = b * S + grp * 512
    ld, red, sqe = [], [], []
    for tl in range(4):
        w = list(st["xt_war"][tl]) + list(st["xt_bar"])
        ld.append(c.op("sync", lambda s, tl=tl: s.dma_start(
            out=t["xt"][:, tl, :],
            in_=t[src_name][row0 + tl * 128: row0 + (tl + 1) * 128, :]),
            waits=w, inc=f"xt{tl}", inc_n=16))
    for tl in range(4):
        red.append(c.op("vector", lambda v, tl=tl: v.reduce_sum(
            t["stat"][:, tl, 0:1], t["xt"][:, tl, :], axis=AX.X),
            waits=[ld[tl]], inc="sV"))
    for tl in range(4):
        sqe.append(c.op("scalar", lambda s, tl=tl: s.activation(
            t["sq"][:, :], t["xt"][:, tl, :], AF.Square,
            accum_out=t["stat"][:, tl, 1:2]),
            waits=[ld[tl], st.get("sq_war")], inc="sA"))
        st["sq_war"] = sqe[-1]
    var_ev = []
    for tl in range(4):
        m = c.op("vector", lambda v, tl=tl: v.tensor_single_scalar(
            t["stat"][:, tl, 2:3], t["stat"][:, tl, 0:1], 1.0 / H, OP.mult),
            waits=[red[tl]], inc="sV")
        msq = c.op("vector", lambda v, tl=tl: v.tensor_mul(
            t["stat"][:, tl, 3:4], t["stat"][:, tl, 2:3], t["stat"][:, tl, 2:3]),
            waits=[m], inc="sV")
        var_ev.append(c.op("vector", lambda v, tl=tl: v.scalar_tensor_tensor(
            t["stat"][:, tl, 4:5], t["stat"][:, tl, 1:2], 1.0 / H, t["stat"][:, tl, 3:4],
            op0=OP.mult, op1=OP.subtract), waits=[msq, sqe[tl]], inc="sV"))
    sd_ev = [c.op("scalar", lambda s, tl=tl: s.activation(
        t["stat"][:, tl, 5:6], t["stat"][:, tl, 4:5], AF.Sqrt, bias=t["epsc"][:, 0:1]),
        waits=[var_ev[tl], st["eps_ok"]], inc="sA") for tl in range(4)]
    ap_ev = []
    for tl in range(4):
        rc = c.op("vector", lambda v, tl=tl: v.reciprocal(
            t["stat"][:, tl, 6:7], t["stat"][:, tl, 5:6]), waits=[sd_ev[tl]], inc="sV")
        ap_ev.append(c.op("vector", lambda v, tl=tl: v.tensor_scalar(
            t["xt"][:, tl, :], t["xt"][:, tl, :], t["stat"][:, tl, 2:3],
            t["stat"][:, tl, 6:7], op0=OP.subtract, op1=OP.mult),
            waits=[rc], inc="sV"))
    out["apply"] = ap_ev
    lnx_w = [None] * 4
    if want_lnx:
        for tl in range(4):
            lnx_w[tl] = c.op("sync", lambda s, tl=tl: s.dma_start(
                out=t["lnx"][row0 + tl * 128: row0 + (tl + 1) * 128, :],
                in_=t["xt"][:, tl, :]),
                waits=[ap_ev[tl], st["lnx_war"]], inc=f"xt{tl}", inc_n=16)
        out["lnx_w"] = lnx_w
    evac = {}
    tr_last = {}
    pfree = st["p67_free"]
    for tl in range(4):
        for hc in range(8):
            sl = st["p67i"] % 2
            st["p67i"] += 1
            tr = c.op("tensor", lambda te, tl=tl, hc=hc, sl=sl: te.transpose(
                t[f"p{6 + sl}"][:, :128], t["xt"][:, tl, hc * 128:(hc + 1) * 128],
                t["ident_sb"][:, :]),
                waits=[ap_ev[tl], pfree[sl], st["wT_done"],
                       st.get(("w2evac", 6 + sl))], inc="sT")
            ev1 = c.op("scalar", lambda s, tl=tl, hc=hc, sl=sl: s.copy(
                t["lnT"][:, hc, tl * 128:(tl + 1) * 128], t[f"p{6 + sl}"][:, :128]),
                waits=[tr, st.get("lnT_war")], inc="sA")
            if gate:
                ev1 = c.op("scalar", lambda s, tl=tl, hc=hc, sl=sl: s.copy(
                    t["lnT32"][:, hc, :], t[f"p{6 + sl}"][:, :128]),
                    waits=[tr, st.get("gate_mm_war")], inc="sA")
            pfree[sl] = ev1
            evac[(tl, hc)] = ev1
            tr_last[tl] = tr
        if gate:
            tile_in_b = grp * 4 + tl
            gm = None
            for hc in range(8):
                gm = c.op("tensor", lambda te, tl=tl, hc=hc: te.matmul(
                    t["p4"][:, 0:E], t["lnT32"][:, hc, :],
                    t["WgT"][:, hc, :], start=(hc == 0), stop=(hc == 7)),
                    waits=[evac[(tl, hc)], st.get("gate_war"), st.get("gate_war2")],
                    inc="sT" if hc == 7 else None)
            st["gate_mm_war"] = gm
            ex = c.op("scalar", lambda s, tl=tl: s.activation(
                t["smtok"][:, tl, :], t["p4"][:, 0:E], AF.Exp,
                accum_out=t["stat"][:, tl, 7:8]), waits=[gm], inc="sA")
            st["gate_war"] = ex
            rcg = c.op("vector", lambda v, tl=tl: v.reciprocal(
                t["stat"][:, tl, 3:4], t["stat"][:, tl, 7:8]), waits=[ex], inc="sV")
            dv = c.op("vector", lambda v, tl=tl: v.tensor_single_scalar(
                t["smtok"][:, tl, :], t["smtok"][:, tl, :], t["stat"][:, tl, 3:4],
                OP.mult), waits=[rcg], inc="sV")
            str_ = c.op("tensor", lambda te, tl=tl: te.transpose(
                t["p5"][0:E, 0:128], t["smtok"][:, tl, :],
                t["ident_sb"][:, :]),
                waits=[dv, st.get("smT_tr_war"), st.get("p5_exp_war"),
                       st.get(("w2evac", 5))], inc="sT")
            smev = c.op("scalar", lambda s, tl=tl, tb=tile_in_b, b=b: s.copy(
                t["smT"][b * 32:b * 32 + E, tb * 128:(tb + 1) * 128],
                t["p5"][0:E, 0:128]),
                waits=[str_, st.get("smT_war"), st.get("smT_war2"),
                       st["smT_init"]], inc="sA")
            st["smT_tr_war"] = smev
            out["smT_last"] = smev
    st["p67_free"] = pfree
    out["evac"] = evac
    for tl in range(4):
        st["xt_war"][tl] = [tr_last[tl], lnx_w[tl]]
    st["xt_bar"] = []
    return out


def emit_topk(c, st, call):
    t = c.t
    cp = c.op("vector", lambda v: v.tensor_copy(t["smW"][:, :], t["smT"][:, :]),
              waits=[st["smT_last_ev"], st.get("smW_wl2")] + st.get("topk_war", []),
              inc="sV")
    st["smT_war"] = cp
    st["smT_free"] = cp
    prev = cp
    for it in range(KSEL // 8):
        sl = slice(it * 8, it * 8 + 8)
        mx = c.op("vector", lambda v, sl=sl: v.max(out=t["Gsb"][:, sl], in_=t["smW"][:, :]),
                  waits=[prev], inc="sV")
        mi = c.op("vector", lambda v, sl=sl: v.max_index(
            t["Isb"][:, sl], t["Gsb"][:, sl], t["smW"][:, :]), waits=[mx], inc="sV")
        prev = c.op("vector", lambda v, sl=sl: v.match_replace(
            out=t["smW"][:, :], in_to_replace=t["Gsb"][:, sl], in_values=t["smW"][:, :],
            imm_value=NEG), waits=[mi], inc="sV")
    st["smW_free"] = prev
    gs = c.op("vector", lambda v: v.tensor_single_scalar(
        t["Gsb"][:, :], t["Gsb"][:, :], 0.5, OP.mult), waits=[prev], inc="sV")
    for b in range(B):
        iw = c.op("sync", lambda s, b=b: s.dma_start(
            out=t["idxd"][b, :, :], in_=t["Isb"][b * 32:b * 32 + E, :]),
            waits=[prev], inc="tkw", inc_n=16)
        gw = c.op("sync", lambda s, b=b: s.dma_start(
            out=t["gd"][b, :, :], in_=t["Gsb"][b * 32:b * 32 + E, :]),
            waits=[gs], inc="tkw", inc_n=16)
    st["topk_war"] = [iw, gw]
    return iw, gw


def emit_ffn(c, st, call):
    t = c.t
    scd = t["xr" if call == 0 else "xr2"]

    if not st.pop("skip_ln", False):
        for b in range(B):
            for grp in range(4):
                ev = emit_ln_group(c, st, "xr", b, grp, want_lnx=True, gate=True)
                st["smT_last_ev"] = ev["smT_last"]
    emit_topk(c, st, call)
    tkw_full = ("tkw", c.cnt["tkw"])
    lnx_all = [(f"xt{i}", c.cnt[f"xt{i}"]) for i in range(4)]

    if c.stage < 2:
        st["ffn_done"] = [tkw_full]
        return

    w1_free = st.setdefault("w1_free4", [None] * 3)
    w2_free = st.setdefault("w2_free8", [None] * 8)
    geluevs = {}
    scat_last = st["scat_last"]
    contrib_free = st.get("contrib_free", [None] * 4)

    def w1_load(e, fcg):
        slot = fcg % 3
        la = c.op("gpsimd", lambda g, e=e, fcg=fcg, slot=slot: g.dma_start(
            out=t["W1sb"][:, slot, 0:4, :],
            in_=t["W1"][e, :, fcg * 512:(fcg + 1) * 512].rearrange(
                "(hc p) f -> p hc f", p=128)[:, 0:4, :]),
            waits=[w1_free[slot], st.get("smT_free")], inc=f"w1s{slot}", inc_n=16)
        lb = c.op("gpsimd", lambda g, e=e, fcg=fcg, slot=slot: g.dma_start(
            out=t["W1sb"][:, slot, 4:8, :],
            in_=t["W1"][e, :, fcg * 512:(fcg + 1) * 512].rearrange(
                "(hc p) f -> p hc f", p=128)[:, 4:8, :]),
            inc=f"w1s{slot}", inc_n=16)
        st[("w1ld", e, fcg)] = (la, lb)

    def w2_load(e, j):
        slot, hh, fc = j % 8, j // 32, j % 32
        st[("w2ld", e, j)] = st["last_w2ld"] = c.op(
            "gpsimd", lambda g, e=e, hh=hh, fc=fc, slot=slot: g.dma_start(
                out=t["W2sb"][:, slot, :],
                in_=t["W2"][e, fc * 128:(fc + 1) * 128, hh * 512:(hh + 1) * 512]),
            waits=[w2_free[slot], st.get("smW_free")], inc=f"w2s{slot}", inc_n=16)

    def idx_loads(e):
        par = e % 2
        for b in range(B):
            for sc in range(4):
                c.op("scalar", lambda s, b=b, sc=sc, e=e, par=par: s.dma_start(
                    out=t["idxcol"][:, par, b, sc:sc + 1],
                    in_=t["idxd"][b, e, sc * 128:(sc + 1) * 128]
                    .rearrange("(p o) -> p o", o=1)),
                    waits=[tkw_full] + st.get(("idx_free", par), []),
                    inc="dsc", inc_n=16)
                c.op("scalar", lambda s, b=b, sc=sc, e=e, par=par: s.dma_start(
                    out=t["ccol"][:, par, b, sc:sc + 1],
                    in_=t["gd"][b, e, sc * 128:(sc + 1) * 128]
                    .rearrange("(p o) -> p o", o=1)),
                    waits=[tkw_full] + st.get(("idx_free", par), []),
                    inc="dsc", inc_n=16)
        st[("dscf", e)] = ("dsc", c.cnt["dsc"])

    def gathers(e):
        par = e % 2
        prev_sg = ("sg", c.cnt["sg"])
        for b in range(B):
            for sc in range(4):
                w = [st[("dscf", e)], prev_sg, st.get("exT_tr_last"),
                     st["wT_done"]] + lnx_all
                c.op("gpsimd", lambda g, b=b, sc=sc, par=par: g.indirect_dma_start(
                    out=t["ex_raw"][:, b, sc, :], out_offset=None,
                    in_=t["lnx"][:, :],
                    in_offset=IndirectOffsetOnAxis(t["idxcol"][:, par, b, sc:sc + 1], 0),
                    element_offset=b * S * H),
                    waits=w, inc="sg", inc_n=16)
        st[("sgf", e)] = ("sg", c.cnt["sg"])

    def b2_load(e):
        st[("b2ld", e)] = c.op("gpsimd", lambda g, e=e: g.dma_start(
            out=t["b2row"][:1, e % 2, :], in_=t["b2"][e:e + 1, :]),
            waits=[st.get(("b2free", e % 2))], inc="b2s", inc_n=16)

    for fcg in range(3):
        w1_load(0, fcg)
    b2_load(0)
    idx_loads(0)
    gathers(0)
    for j in range(4):
        w2_load(0, j)

    for e in range(E):
        scat_all = [(f"sb{i}", c.cnt[f"sb{i}"]) for i in range(4)]
        par = e % 2
        sg_full = st[("sgf", e)]
        dsc_full = st[("dscf", e)]
        b2l = st[("b2ld", e)]
        pfree = st["p67_free"]
        exT_evac = {}
        tr = None
        for b in range(B):
            for sc in range(4):
                for hc in range(8):
                    sl = st["p67i"] % 2
                    st["p67i"] += 1
                    tr = c.op("tensor", lambda te, b=b, sc=sc, hc=hc, sl=sl: te.transpose(
                        t[f"p{6 + sl}"][:, :128],
                        t["ex_raw"][:, b, sc, hc * 128:(hc + 1) * 128],
                        t["ident_sb"][:, :]),
                        waits=[sg_full, pfree[sl], st.get(("w2evac", 6 + sl))],
                        inc="sT")
                    pfree[sl] = c.op("scalar", lambda s, b=b, sc=sc, hc=hc, sl=sl: s.copy(
                        t["exT"][:, b, hc, sc * 128:(sc + 1) * 128],
                        t[f"p{6 + sl}"][:, :128]),
                        waits=[tr], inc="sA")
                    exT_evac[(b, sc, hc)] = pfree[sl]
        st["exT_tr_last"] = tr
        last_w1 = None
        for fcg in range(8):
            if ("w1ld", e, fcg) not in st:
                w1_load(e, fcg)
            la, lb = st.pop(("w1ld", e, fcg))
            slot = fcg % 3
            for fcl in range(4):
                fc = fcg * 4 + fcl
                for b in range(B):
                    bank = 2 * (fc % 2) + b
                    if fc < 2:
                        bfree = st.get(("w2evac", bank))
                    else:
                        bfree = geluevs[(call, e, fc - 2, b)]
                    mm = None
                    for hc in range(8):
                        w = [exT_evac[(b, 3, hc)]]
                        if hc == 0:
                            w += [la, lb, bfree]
                        mm = c.op("tensor",
                                  lambda te, fcl=fcl, b=b, hc=hc, bank=bank, slot=slot:
                                  te.matmul(t[f"p{bank}"][:, :],
                                            t["W1sb"][:, slot, hc, fcl * 128:(fcl + 1) * 128],
                                            t["exT"][:, b, hc, :],
                                            start=(hc == 0), stop=(hc == 7)),
                                  waits=w, inc="sT" if hc == 7 else None)
                    last_w1 = mm
                    geluevs[(call, e, fc, b)] = c.op(
                        "scalar", lambda s, fc=fc, b=b, bank=bank, e=e: s.activation(
                            t["h1sb"][:, b, fc, :], t[f"p{bank}"][:, :], AF.Gelu_apprx_tanh,
                            bias=t["b1all"][:, e, fc:fc + 1]),
                        waits=[mm, st["b1_ld"], st.get("h1_war")], inc="sA")
            w1_free[slot] = last_w1
        for hh in range(2):
            mmstop = {}
            for fc in range(32):
                j = hh * 32 + fc
                if ("w2ld", e, j) not in st:
                    w2_load(e, j)
                wl2 = st.pop(("w2ld", e, j))
                slot = j % 8
                lastmm = None
                for b in range(B):
                    for sc in range(4):
                        bank = b * 4 + sc
                        if fc == 0:
                            if hh == 0 and bank < 4:
                                bfree = geluevs[(call, e, 30 + (bank >> 1), bank & 1)]
                            else:
                                bfree = st.get(("w2evac", bank))
                            c.op("tensor", lambda te, bank=bank, hh=hh, e=e: te.matmul(
                                t[f"p{bank}"][:, :], t["ones_sb"][:1, :],
                                t["b2row"][:1, e % 2, hh * 512:(hh + 1) * 512],
                                start=True, stop=False),
                                waits=[bfree, b2l, st["ones_ok"],
                                       geluevs[(call, e, 0, 0)]])
                        lastmm = c.op("tensor",
                                      lambda te, b=b, sc=sc, fc=fc, bank=bank, slot=slot:
                                      te.matmul(t[f"p{bank}"][:, :],
                                                t["h1sb"][:, b, fc, sc * 128:(sc + 1) * 128],
                                                t["W2sb"][:, slot, :],
                                                start=False, stop=(fc == 31)),
                                      waits=[wl2, geluevs[(call, e, fc, b)]],
                                      inc="sT" if (fc == 31 or (b == B - 1 and sc == 3))
                                      else None)
                        if fc == 31:
                            mmstop[(b, sc)] = lastmm
                w2_free[slot] = lastmm
            if hh == 0:
                if e + 1 < E:
                    b2_load(e + 1)
                    idx_loads(e + 1)
                    gathers(e + 1)
                    for fcg in range(3):
                        w1_load(e + 1, fcg)
                for j2 in range(32, 36):
                    w2_load(e, j2)
            scat_evs = st.setdefault(("scat_evs", par), [])
            if hh == 0:
                scat_evs.clear()
            for b in range(B):
                for sc in range(4):
                    bank = b * 4 + sc
                    buf = (b * 2 + hh * 4 + sc) % 4
                    ev = c.op("vector", lambda v, b=b, sc=sc, bank=bank, buf=buf, par=par:
                              v.tensor_single_scalar(t["contrib"][:, buf, :],
                                                     t[f"p{bank}"][:, :],
                                                     t["ccol"][:, par, b, sc:sc + 1],
                                                     OP.mult),
                              waits=[mmstop[(b, sc)], dsc_full, contrib_free[buf]],
                              inc="sV")
                    st[("w2evac", bank)] = ev
                    scat = c.op("gpsimd", lambda g, b=b, sc=sc, hh=hh, buf=buf, scd=scd,
                                par=par:
                                g.indirect_dma_start(
                                    out=scd[:, :],
                                    out_offset=IndirectOffsetOnAxis(
                                        t["idxcol"][:, par, b, sc:sc + 1], 0),
                                    in_=t["contrib"][:, buf, :], in_offset=None,
                                    element_offset=b * S * H + hh * 512,
                                    compute_op=OP.add),
                                waits=[ev] + scat_all + [st.get("ydone")],
                                inc=f"sb{buf}", inc_n=16)
                    contrib_free[buf] = scat
                    scat_last[b] = scat
                    scat_evs.append(scat)
            st[("b2free", e % 2)] = mmstop[(1, 3)]
        st[("idx_free", par)] = list(st[("scat_evs", par)])
    st["smW_wl2"] = st.get("last_w2ld")
    st["contrib_free"] = contrib_free
    st["h1_war"] = None
    st["ffn_done"] = [(f"sb{i}", c.cnt[f"sb{i}"]) for i in range(4)]


def emit_attn_kv(c, st):
    t = c.t
    st["xt_bar"] = list(st["ffn_done"])
    st["p0_att"] = st.get(("w2evac", 0))
    st["p1_att"] = st.get(("w2evac", 1))
    vfree = [st.get(("w2evac", 2)), st.get(("w2evac", 3))]
    kq_evac, vown_ev = [], []
    mm = ve = None
    for b in range(B):
        for grp in range(4):
            ev = emit_ln_group(c, st, "xr", b, grp, want_lnx=False, gate=False)
            evac = ev["evac"]
            col0 = b * S + grp * 512
            for name, bank, wtn in [("k", 0, "WkT"), ("q", 1, "WqT")]:
                mm = None
                for hc in range(8):
                    w = [evac[(3, hc)]] + ([st[f"p{bank}_att"]] if hc == 0 else [])
                    mm = c.op("tensor", lambda te, hc=hc, bank=bank, wtn=wtn: te.matmul(
                        t[f"p{bank}"][0:DH, :], t[wtn][:, hc, :], t["lnT"][:, hc, :],
                        start=(hc == 0), stop=(hc == 7)),
                        waits=w, inc="sT" if hc == 7 else None)
                if name == "k":
                    evk = c.op("scalar", lambda s, col0=col0: s.activation(
                        t["kT_own"][:, col0:col0 + 512], t["p0"][0:DH, :], AF.Identity,
                        bias=t["bk_sb"][:, 0:1]), waits=[mm], inc="sA")
                    st["p0_att"] = evk
                    kq_evac.append(evk)
                else:
                    evq = c.op("scalar", lambda s, col0=col0: s.activation(
                        t["qT_sb"][0:DH, col0:col0 + 512], t["p1"][0:DH, :], AF.Identity,
                        bias=t["bq32"][:, 0:1], scale=QSCALE),
                        waits=[mm, st["bq_ok"]], inc="sA")
                    evq = c.op("scalar", lambda s, col0=col0: s.activation(
                        t["qT_sb"][DH:2 * DH, col0:col0 + 512], t["p1"][0:DH, :],
                        AF.Identity, bias=t["bq32"][:, 0:1], scale=QSCALE),
                        waits=[mm, st["bq_ok"]], inc="sA")
                    st["p1_att"] = evq
                    kq_evac.append(evq)
            for tl in range(4):
                tile = col0 // 128 + tl
                par = tl % 2
                c.op("tensor", lambda te, par=par: te.matmul(
                    t[f"p{2 + par}"][:, 0:DH], t["ones_sb"][:1, :], t["bv_sb"][:1, :],
                    start=True, stop=False),
                    waits=[evac[(tl, 7)], vfree[par], st["bv_ok"], st["ones_ok"]])
                mm = None
                for hc in range(8):
                    mm = c.op("tensor", lambda te, tl=tl, hc=hc, par=par: te.matmul(
                        t[f"p{2 + par}"][:, 0:DH], t["lnT"][:, hc, tl * 128:(tl + 1) * 128],
                        t["WvT"][:, hc, :], start=False, stop=(hc == 7)),
                        inc="sT" if hc == 7 else None)
                ve = c.op("scalar", lambda s, tile=tile, par=par: s.copy(
                    t["v_own"][:, tile, :], t[f"p{2 + par}"][:, 0:DH]),
                    waits=[mm], inc="sA")
                vfree[par] = ve
                vown_ev.append(ve)
            st["lnT_war"] = mm
    st["kq_evac"] = kq_evac
    st["vown_ev"] = vown_ev
    st["vfree"] = vfree


def emit_out_a(c, st):
    t = c.t
    c.op("sync", lambda s: s.dma_start(
        out=t["kvo"][0:DH * T_OWN].rearrange("(p f) -> p f", p=DH),
        in_=t["kT_own"][:, :]), waits=st["kq_evac"], inc="ycw", inc_n=16)
    c.op("sync", lambda s: s.dma_start(
        out=t["kvo"][DH * T_OWN:2 * DH * T_OWN].rearrange(
            "(tile p f) -> p tile f", p=128, f=DH),
        in_=t["v_own"][:, :, :]), waits=st["vown_ev"], inc="ycw", inc_n=16)
    c.op("sync", lambda s: s.dma_start(
        out=t["kvo"][2 * DH * T_OWN:].rearrange("(p f) -> p f", p=DH),
        in_=t["qT_sb"][0:DH, :]), waits=st["kq_evac"], inc="ycw", inc_n=16)


def emit_attn_q(c, st, ln_cb=None):
    t = c.t
    NCH = c.t_all // 128
    kv_ok = []
    kslice, vms, qld = {}, {}, {}
    if c.launch in ("B", "B3", "Bdbg"):
        for half in range(2):
            c.op("sync", lambda s, half=half: s.dma_start(
                out=t["qT_sb"][half * DH:(half + 1) * DH, :], in_=t["qT_in"][:, :]),
                waits=[st["stp_full"], st["wT_done"]], inc="kvl", inc_n=16)
        qfull = ("kvl", c.cnt["kvl"])
        qld = {0: qfull, 1: qfull}
        NSLK = 4
        colk = ((c.t_all // 256) * 128) // NSLK
        chv = (c.t_all // 128) // NSLK
        for k in range(NSLK):
            ke = c.op("sync", lambda s, k=k, colk=colk: s.dma_start(
                out=t["kT_full"][:, k * colk:(k + 1) * colk],
                in_=t["kTf_in"][:, k * colk:(k + 1) * colk]),
                waits=[st["stp_full"], st["wT_done"]], inc=f"kvk{k}", inc_n=16)
            ve = c.op("sync", lambda s, k=k, chv=chv: s.dma_start(
                out=t["v_aug"][:, k * chv:(k + 1) * chv, 0:DH],
                in_=t["vf_in"][:, :].rearrange(
                    "p (c f) -> p c f", f=DH)[:, k * chv:(k + 1) * chv, :]),
                waits=[st["stp_full"], st["wT_done"]], inc=f"kvv{k}", inc_n=16)
            kslice[k] = ke
            vms[k] = c.op("vector", lambda v, k=k, chv=chv: v.memset(
                t["v_aug"][:, k * chv:(k + 1) * chv, DH:DH + 1], 1.0),
                waits=[ve], inc="sV")
        vfree = [None, None]
        yfree0 = None
        st["ytr_free"] = [None, None]
    else:
        e1 = c.op("sync", lambda s: s.dma_start(
            out=t["kvin"][0:DH * T_OWN].rearrange("(p f) -> p f", p=DH),
            in_=t["kT_own"][:, :]), waits=st["kq_evac"], inc="kvl", inc_n=16)
        e2 = c.op("sync", lambda s: s.dma_start(
            out=t["kvin"][DH * T_OWN:].rearrange("(tile p f) -> p tile f", p=128, f=DH),
            in_=t["v_own"][:, :, :]), waits=st["vown_ev"], inc="kvl", inc_n=16)
        cc = c.op("gpsimd", lambda g: g.collective_compute(
            "AllGather", OP.bypass, replica_groups=[list(range(c.ncore))],
            ins=[t["kvin"].ap().opt()], outs=[t["kvout"].ap().opt()]),
            waits=[("kvl", c.cnt["kvl"])], inc="scc", inc_n=16)
        for r in range(c.ncore):
            for half in range(2):
                kv_ok.append(c.op("sync", lambda s, r=r, half=half: s.dma_start(
                    out=t["kT_full"][half * DH:(half + 1) * DH,
                                     r * (T_OWN // 2):(r + 1) * (T_OWN // 2)],
                    in_=t["kvout"][r, 0:DH * T_OWN].rearrange(
                        "(p cp two f) -> two p cp f", p=DH, two=2, f=128)[half]),
                    waits=[cc] + st["ffn_done"], inc="kvl", inc_n=16))
            kv_ok.append(c.op("sync", lambda s, r=r: s.dma_start(
                out=t["v_aug"][:, r * 32:(r + 1) * 32, 0:DH],
                in_=t["kvout"][r, DH * T_OWN:].rearrange(
                    "(tile p f) -> p tile f", p=128, f=DH)),
                waits=[cc], inc="kvl", inc_n=16))
        vfree = st["vfree"]
        yfree0 = vfree[0]
        st["ytr_free"] = [vfree[1], None]
    if c.launch == "full":
        kvl_full = ("kvl", c.cnt["kvl"])
        vaug_ones = c.op("vector", lambda v: v.memset(t["v_aug"][:, :, DH:DH + 1], 1.0),
                         waits=[kvl_full], inc="sV")
        kv_ok = [kvl_full, vaug_ones]

    yfree = yfree0
    exp_ev = {}
    smm_ev = {}
    ymm_ev = {}
    SB = [0, 1, 5]
    ycp = None
    for g in range(T_OWN // 512):
        for ch in range(NCH + 2):
            if ch < NCH:
                par = SB[ch % 3]
                w = (kv_ok if ch < 3 else []) + [exp_ev.get((g, ch - 3)),
                                                 kslice.get(ch // (NCH // 4)),
                                                 qld.get(ch % 2)]
                if ch < 3:
                    w = w + [st.get(("w2evac", par))]
                if par == 5:
                    w = w + [st.get("smT_tr_war")]
                smm_ev[(g, ch)] = c.op("tensor", lambda te, ch=ch, g=g, par=par: te.matmul(
                    t[f"p{par}"][:, :],
                    t["kT_full"][(ch % 2) * DH:(ch % 2) * DH + DH,
                                 (ch // 2) * 128:(ch // 2) * 128 + 128],
                    t["qT_sb"][(ch % 2) * DH:(ch % 2) * DH + DH,
                               g * 512:(g + 1) * 512],
                    start=True, stop=True),
                    waits=w, inc="sT")
            if ch >= 2:
                cy = ch - 2
                ymm_ev[(g, cy)] = c.op("tensor", lambda te, cy=cy: te.matmul(
                    t["p2"][0:DH + 1, :], t["v_aug"][:, cy, :],
                    t["E_sb"][:, cy % 2, :],
                    start=(cy == 0), stop=(cy == NCH - 1)),
                    waits=[exp_ev[(g, cy)], vms.get(cy // (NCH // 4))] +
                          ([yfree, st.get("yT_ok")] if cy == 0 else []),
                    inc="sT")
            if ch < NCH:
                exp_ev[(g, ch)] = c.op("scalar", lambda s, ch=ch, par=SB[ch % 3]:
                                       s.activation(
                    t["E_sb"][:, ch % 2, :], t[f"p{par}"][:, :], AF.Exp),
                    waits=[smm_ev[(g, ch)], ymm_ev.get((g, ch - 2))], inc="sA")
        ycp = c.op("scalar", lambda s: s.copy(t["yT"][0:DH + 1, :], t["p2"][0:DH + 1, :]),
                   waits=[ymm_ev[(g, NCH - 1)]], inc="sA")
        yfree = ycp
        for tl in range(4):
            par = tl % 2
            tr = c.op("tensor", lambda te, tl=tl, par=par: te.transpose(
                t[f"p{3 + par}"][:, :128], t["yT"][:, tl * 128:(tl + 1) * 128],
                t["ident_sb"][:, :]),
                waits=[ycp, st["ytr_free"][par],
                       (st.get("gate_war") if par == 1 else None)], inc="sT")
            cpy = c.op("vector", lambda v, tl=tl, par=par: v.tensor_copy(
                t["ytmp"][:, par, :], t[f"p{3 + par}"][:, 0:DH + 1]), waits=[tr], inc="sV")
            st["ytr_free"][par] = cpy
            rcy = c.op("vector", lambda v, par=par: v.reciprocal(
                t["dcol"][:, par:par + 1], t["ytmp"][:, par, DH:DH + 1]),
                waits=[cpy], inc="sV")
            dv = c.op("vector", lambda v, tl=tl, par=par: v.tensor_single_scalar(
                t["ytok"][:, tl, :], t["ytmp"][:, par, 0:DH], t["dcol"][:, par:par + 1],
                OP.mult), waits=[rcy], inc="sV")
            tile = g * 4 + tl
            row0 = tile * 128
            xsrc = "xr" if c.launch == "full" else "x0"
            ld = c.op("sync", lambda s, tl=tl, row0=row0, xsrc=xsrc: s.dma_start(
                out=t["xt"][:, tl, :], in_=t[xsrc][row0:row0 + 128, :]),
                waits=st["xt_war"][tl] + st["xt_bar"], inc=f"xt{tl}", inc_n=16)
            add = None
            for j in range(16):
                add = c.op("vector", lambda v, tl=tl, j=j: v.tensor_add(
                    t["xt"][:, tl, j * DH:(j + 1) * DH],
                    t["xt"][:, tl, j * DH:(j + 1) * DH], t["ytok"][:, tl, :]),
                    waits=[ld, dv] if j == 0 else [],
                    inc="sV" if j == 15 else None)
            wb = c.op("sync", lambda s, tl=tl, row0=row0: s.dma_start(
                out=t["xr"][row0:row0 + 128, :], in_=t["xt"][:, tl, :]),
                waits=[add], inc=f"wb{tl}", inc_n=16)
            st["xt_war"][tl] = [wb]
        if ln_cb is not None:
            st["gate_war2"] = st["ytr_free"][1]
            last_p5 = max(ch for ch in range(NCH) if [0, 1, 5][ch % 3] == 2 or
                          [0, 1, 5][ch % 3] == 5)
            st["p5_exp_war"] = exp_ev[(g, last_p5)]
            ln_cb(g)
    st["xt_bar"] = []
    st["att_done"] = [(f"wb{i}", c.cnt[f"wb{i}"]) for i in range(4)]
    gl = T_OWN // 512 - 1
    for par in (0, 1, 5):
        last_ch = max(ch for ch in range(NCH) if [0, 1, 5][ch % 3] == par)
        st[("w2evac", par)] = exp_ev[(gl, last_ch)]
    st[("w2evac", 2)] = ycp
    st[("w2evac", 3)] = st["ytr_free"][0]
    st[("w2evac", 4)] = st["ytr_free"][1]
    st["gate_war"] = st["ytr_free"][1]
    st["h1_war"] = ("sT", c.cnt["sT"])


def emit_copy_y(c, st):
    t = c.t
    dst = "y" if c.launch == "B3" else "xr2"
    ev = c.op("sync", lambda s, dst=dst: s.dma_start(
        out=t[dst][:, :], in_=t["xr"][:, :]),
        waits=st["att_done"], inc="ycw", inc_n=16)
    st["ydone"] = ev


def emit_final(c, st):
    t = c.t
    if c.launch == "Bdbg":
        fw = list(st.get("ffn_done", []))
        c.op("sync", lambda s: s.dma_start(
            out=t["d_ex"][:, :], in_=t["ex_raw"].reshape([128, B * 4 * H])[:, :]),
            waits=fw, inc="dbg", inc_n=16)
        c.op("sync", lambda s: s.dma_start(
            out=t["d_h1"][:, :], in_=t["h1sb"].reshape([128, B * 32 * 512])[:, :]),
            waits=fw, inc="dbg", inc_n=16)
        c.op("sync", lambda s: s.dma_start(
            out=t["d_ext"][:, :], in_=t["exT"].reshape([128, B * 8 * 512])[:, :]),
            waits=fw, inc="dbg", inc_n=16)
        c.op("sync", lambda s: s.dma_start(out=t["d_idx"][:, :, :], in_=t["idxd"][:, :, :]),
             waits=[("tkw", c.cnt["tkw"])], inc="dbg", inc_n=16)
        c.op("sync", lambda s: s.dma_start(out=t["d_g"][:, :, :], in_=t["gd"][:, :, :]),
             waits=[("tkw", c.cnt["tkw"])], inc="dbg", inc_n=16)
        c.op("sync", lambda s: s.dma_start(out=t["d_lnx"][:, :], in_=t["lnx"][:, :]),
             waits=[("sg", c.cnt["sg"])] + [(f"xt{i}", c.cnt[f"xt{i}"]) for i in range(4)],
             inc="dbg", inc_n=16)
        c.op("sync", lambda s: s.dma_start(out=t["d_sm"][0, :, :], in_=t["smT"][:, :]),
             waits=[("sA", c.cnt["sA"])], inc="dbg", inc_n=16)
    if c.launch == "full":
        c.op("sync", lambda s: s.dma_start(out=t["y"][:, :], in_=t["xr2"][:, :]),
             waits=list(st.get("ffn_done", [])), inc="ycw", inc_n=16)
    if c.stage < 4:
        w = list(st.get("ffn_done", [])) + list(st.get("att_done", []))
        c.op("sync", lambda s: s.dma_start(out=t["y"][:, :], in_=t["xr"][:, :]),
             waits=[x for x in w if x], inc="ycw", inc_n=16)
        if "d_lnx" in t:
            c.op("sync", lambda s: s.dma_start(out=t["d_lnx"][:, :], in_=t["lnx"][:, :]),
                 waits=[("sg", c.cnt["sg"])] + [(f"xt{i}", c.cnt[f"xt{i}"]) for i in range(4)],
                 inc="dbg", inc_n=16)
    allsems = (["stp", "b1s", "lnx", "tkw", "dsc"] +
               [f"kvk{i}" for i in range(4)] + [f"kvv{i}" for i in range(4)] +
               [f"w1s{i}" for i in range(3)] + [f"w2s{i}" for i in range(8)] +
               ["b2s", "sg", "scc", "kvl", "ycw", "dbg",
                "xt0", "xt1", "xt2", "xt3", "wb0", "wb1", "wb2", "wb3",
                "sb0", "sb1", "sb2", "sb3"])
    fin = [(sname, c.cnt[sname]) for sname in allsems if c.cnt[sname] > 0]
    c.op("sync", lambda s: s.nop(), waits=fin)
    c.op("gpsimd", lambda g: g.nop(), waits=fin)


_CACHE = {}


def _relayout_kv(kvo_list, ncore):
    t_all = ncore * T_OWN
    kT = [k[0:DH * T_OWN].reshape(DH, T_OWN) for k in kvo_list]
    vv = [k[DH * T_OWN:2 * DH * T_OWN].reshape(T_OWN, DH) for k in kvo_list]
    qT = [k[2 * DH * T_OWN:].reshape(DH, T_OWN) for k in kvo_list]
    kTg = np.concatenate(kT, axis=1)
    vg = np.concatenate(vv, axis=0)
    nch = t_all // 128
    kTf = np.zeros((128, t_all // 2), dtype=kTg.dtype)
    for ch in range(nch):
        half, colblk = ch % 2, ch // 2
        kTf[half * DH:(half + 1) * DH, colblk * 128:(colblk + 1) * 128] = \
            kTg[:, ch * 128:(ch + 1) * 128]
    vf = np.ascontiguousarray(
        vg.reshape(nch, 128, DH).transpose(1, 0, 2).reshape(128, nch * DH))
    return kTf, vf, qT


def kernel(L0_attn_embeddings, L0_attn_Wq, L0_attn_bq, L0_attn_Wk, L0_attn_bk,
           L0_attn_Wv, L0_attn_bv, L0_ffn_Wg, L0_ffn_W1, L0_ffn_b1, L0_ffn_W2,
           L0_ffn_b2):
    from concourse.bass_utils import run_bass_kernel_spmd
    ncore = 8
    x = np.ascontiguousarray(np.asarray(L0_attn_embeddings, dtype=np.float32))
    if "A" not in _CACHE:
        _CACHE["A"] = build(stage=4, ncore=ncore, launch="A")
        _CACHE["B"] = build(stage=4, ncore=ncore, launch="B")
    cA, cB = _CACHE["A"], _CACHE["B"]
    common = dict(
        Wg=np.ascontiguousarray(L0_ffn_Wg, dtype=np.float32),
        W1=np.ascontiguousarray(L0_ffn_W1, dtype=np.float32),
        b1=np.ascontiguousarray(L0_ffn_b1, dtype=np.float32),
        W2=np.ascontiguousarray(L0_ffn_W2, dtype=np.float32),
        b2=np.ascontiguousarray(L0_ffn_b2, dtype=np.float32),
    )
    commonA = dict(common,
                   Wq=np.ascontiguousarray(L0_attn_Wq, dtype=np.float32),
                   bq=np.ascontiguousarray(np.asarray(L0_attn_bq, np.float32).reshape(DH, 1)),
                   Wk=np.ascontiguousarray(L0_attn_Wk, dtype=np.float32),
                   bk=np.ascontiguousarray(np.asarray(L0_attn_bk, np.float32).reshape(DH, 1)),
                   Wv=np.ascontiguousarray(L0_attn_Wv, dtype=np.float32),
                   bv=np.ascontiguousarray(np.asarray(L0_attn_bv, np.float32).reshape(DH, 1)))
    in_maps = [dict(commonA, x0=np.ascontiguousarray(
        x[2 * i:2 * i + 2].reshape(T_OWN, H))) for i in range(ncore)]
    resA = run_bass_kernel_spmd(cA.nc, in_maps, core_ids=list(range(ncore)))
    kvo_list = [np.asarray(r["kvo"]).ravel() for r in resA.results]
    kTf, vf, qTl = _relayout_kv(kvo_list, ncore)
    in_mapsB = [dict(common,
                     x0=np.ascontiguousarray(
                         np.asarray(resA.results[i]["x1o"], np.float32).reshape(T_OWN, H)),
                     kTf_in=np.ascontiguousarray(kTf),
                     vf_in=np.ascontiguousarray(vf),
                     qT_in=np.ascontiguousarray(qTl[i]))
                for i in range(ncore)]
    resB = run_bass_kernel_spmd(cB.nc, in_mapsB, core_ids=list(range(ncore)))
    out = np.concatenate(
        [np.asarray(r["y"], dtype=np.float32).reshape(B, S, H) for r in resB.results],
        axis=0)
    return out


# version 14
# speedup vs baseline: 1.5360x; 1.0454x over previous
import numpy as np
import ml_dtypes
import concourse.bass as bass
import concourse.mybir as mybir
from concourse.bass_core_rust import IndirectOffsetOnAxis

F32 = mybir.dt.float32
BF16 = mybir.dt.bfloat16
U32 = mybir.dt.uint32
AF = mybir.ActivationFunctionType
OP = mybir.AluOpType
AX = mybir.AxisListType

B = 2
S = 2048
H = 1024
E = 8
KSEL = 512
FF = 4096
DH = 64
T_OWN = B * S
EPS = 1e-5
QSCALE = 1.0 / 32.0
NEG = -1e30
ENGINES = ["sync", "scalar", "vector", "tensor", "gpsimd"]


class Prog:
    def __init__(self, stage=4, ncore=8):
        self.nc = bass.Bass("TRN2", target_bir_lowering=False)
        self.stage = stage
        self.ncore = ncore
        self.t_all = ncore * T_OWN
        self.sem = {}
        self.cnt = {}
        self.t = {}
        self.ops = {e: [] for e in ENGINES}
        self.watermark = {e: {} for e in ENGINES}

    def mksem(self, *names):
        for n in names:
            self.sem[n] = self.nc.alloc_semaphore(n)
            self.cnt[n] = 0

    def op(self, eng, fn, waits=(), inc=None, inc_n=1):
        wl = {}
        wm = self.watermark[eng]
        for ev in waits:
            if ev is None:
                continue
            s, v = ev
            if v is None or wm.get(s, 0) >= v:
                continue
            wl[s] = max(wl.get(s, 0), v)
        for s, v in wl.items():
            wm[s] = v
        evo = None
        if inc is not None:
            self.cnt[inc] += inc_n
            evo = (inc, self.cnt[inc])
        self.ops[eng].append((fn, sorted(wl.items()), inc, inc_n))
        return evo

    def emit(self):
        with self.nc.Block() as blk:
            for eng_name in ENGINES:
                oplist = self.ops[eng_name]
                if not oplist:
                    continue

                def make(oplist=oplist):
                    def run(eng):
                        for fn, waits, inc, inc_n in oplist:
                            for s, v in waits:
                                eng.wait_ge(self.sem[s], v)
                            ins = fn(eng)
                            if inc is not None:
                                ins.then_inc(self.sem[inc], inc_n)
                    return run
                getattr(blk, eng_name)(make())


def build(stage=4, ncore=8, launch="full"):
    c = Prog(stage, ncore)
    c.launch = launch
    nc = c.nc
    t = c.t

    def dram(name, shape, dt, kind="Internal"):
        t[name] = nc.dram_tensor(name, shape, dt, kind=kind)

    dram("x0", [T_OWN, H], F32, "ExternalInput")
    wlist = [("Wg", [E, H]), ("W1", [E, H, FF]), ("b1", [E, FF]),
             ("W2", [E, FF, H]), ("b2", [E, H])]
    if launch in ("full", "A"):
        wlist += [("Wq", [DH, H]), ("bq", [DH, 1]), ("Wk", [DH, H]), ("bk", [DH, 1]),
                  ("Wv", [DH, H]), ("bv", [DH, 1])]
    for n, sh in wlist:
        dram(n, sh, F32, "ExternalInput")
    if launch == "A":
        dram("x1o", [T_OWN, H], F32, "ExternalOutput")
        dram("kvo", [3 * DH * T_OWN], BF16, "ExternalOutput")
        t["xr"] = t["x1o"]
    else:
        dram("y", [T_OWN, H], F32, "ExternalOutput")
    if launch in ("B", "B3", "Bdbg"):
        dram("kTf_in", [128, (c.t_all // 256) * 128], BF16, "ExternalInput")
        dram("vf_in", [128, (c.t_all // 128) * DH], BF16, "ExternalInput")
        dram("qT_in", [DH, T_OWN], BF16, "ExternalInput")
        t["xr"] = t["y"]
        t["xr2"] = t["y"]
    if launch == "full":
        dram("xr2", [T_OWN, H], F32)
        dram("kvin", [2 * DH * T_OWN], BF16)
        dram("kvout", [c.ncore, 2 * DH * T_OWN], BF16)
        dram("xr", [T_OWN, H], F32)
    dram("lnx", [T_OWN, H], F32)
    dram("idxd", [B, E, KSEL], U32)
    dram("gd", [B, E, KSEL], F32)
    t["ident"] = nc.inline_tensor(np.eye(128, dtype=np.float32), "identc")

    if launch == "Bdbg":
        dram("d_ex", [128, B * 4 * H], F32, "ExternalOutput")
        dram("d_h1", [128, B * 32 * 512], U32 if False else BF16, "ExternalOutput")
        dram("d_ext", [128, B * 8 * 512], BF16, "ExternalOutput")
    if c.stage < 4 or launch == "Bdbg":
        dram("d_lnx", [T_OWN, H], F32, "ExternalOutput")
        dram("d_sm", [B, E, S], F32, "ExternalOutput")
        dram("d_idx", [B, E, KSEL], U32, "ExternalOutput")
        dram("d_g", [B, E, KSEL], F32, "ExternalOutput")

    def sb(name, shape, dt):
        t[name] = nc.alloc_sbuf_tensor(name, shape, dt)

    has_attw = launch in ("full", "A")
    has_q = launch in ("full", "B", "B3", "Bdbg")
    sb("ident_sb", [128, 128], F32)
    sb("ones_sb", [1, 128], BF16)
    sb("WgT", [128, 8, E], F32)
    if has_attw:
        sb("WqT", [128, 8, DH], BF16)
        sb("WkT", [128, 8, DH], BF16)
        sb("WvT", [128, 8, DH], BF16)
        sb("bq32", [DH, 1], F32)
        sb("bk_sb", [DH, 1], F32)
        sb("bv_f", [1, DH], F32)
        sb("bv_sb", [1, DH], BF16)
        sb("kT_own", [DH, T_OWN], BF16)
    sb("b2row", [1, 2, H], BF16)
    sb("b1all", [128, E, 32], F32)
    sb("xt", [128, 4, H], F32)
    sb("sq", [128, H], F32)
    sb("stat", [128, 4, 8], F32)
    sb("epsc", [128, 1], F32)
    sb("dcol", [128, 2], F32)
    sb("lnT", [128, 8, 512], BF16)
    sb("lnT32", [128, 8, 128], F32)
    sb("smtok", [128, 4, E], F32)
    sb("idxcol", [128, 2, B, 4], U32)
    sb("ccol", [128, 2, B, 4], F32)
    sb("contrib", [128, 4, 512], F32)

    base = (nc.sbuf_base + 31) & ~31

    def at(name, shape, dt, off):
        t[name] = nc.alloc_sbuf_tensor_at(name, shape, dt, offset=base + off)

    at("wq_f", [DH, H], F32, 0)
    at("wk_f", [DH, H], F32, 4096)
    at("wv_f", [DH, H], F32, 8192)
    at("wg_f", [E, H], F32, 12288)
    at("h1sb", [128, B, 32, 512], BF16, 0)
    kt_bytes = (c.t_all // 256) * 128 * 2
    if has_q:
        at("kT_full", [128, (c.t_all // 256) * 128], BF16, 0)
        at("v_aug", [128, c.t_all // 128, DH + 1], BF16, kt_bytes)
    rA = max(65536, (kt_bytes + (c.t_all // 128) * (DH + 1) * 2) if has_q else 0, 16384)
    rA = (rA + 31) & ~31
    at("ex_raw", [128, B, 4, H], F32, rA)
    at("exT", [128, B, 8, 512], BF16, rA + 32768)
    at("qT_sb", [128, T_OWN], BF16, rA)
    if has_attw:
        at("v_own", [128, T_OWN // 128, DH], BF16, rA + 8192)
    if has_q:
        at("E_sb", [128, 2, 512], BF16, rA + 12288)
        at("yT", [128, 512], F32, rA + 14336)
        at("ytmp", [128, 2, DH + 1], F32, rA + 16384)
        at("ytok", [128, 4, DH], F32, rA + 16960)
    rB = rA + 49152
    at("smT", [128, S], F32, rB)
    at("W1sb", [128, 3, 8, 512], BF16, rB)
    at("smW", [128, S], F32, rB + 24576)
    at("W2sb", [128, 8, 512], BF16, rB + 24576)
    at("Gsb", [128, KSEL], F32, rB + 32768)
    at("Isb", [128, KSEL], U32, rB + 34816)
    need = base + rB + 36864
    assert need <= nc.sbuf_top, (need, nc.sbuf_top, nc.sbuf_base)

    for i in range(8):
        t[f"p{i}"] = nc.alloc_psum_tensor(f"p{i}", [128, 512], F32)

    c.mksem("stp", "b1s", "lnx", "tkw", "dsc",
            *[f"kvk{i}" for i in range(4)], *[f"kvv{i}" for i in range(4)],
            *[f"w1s{i}" for i in range(3)], *[f"w2s{i}" for i in range(8)],
            "b2s", "sg", "scc", "kvl", "ycw", "dbg", "sT", "sV", "sA",
            *[f"xt{i}" for i in range(4)], *[f"wb{i}" for i in range(4)],
            *[f"sb{i}" for i in range(4)])

    st = {"xt_war": [[] for _ in range(4)], "xt_bar": []}
    emit_setup(c, st)
    if launch in ("full", "A"):
        emit_ffn(c, st, 0)
        emit_attn_kv(c, st)
    if launch == "A":
        emit_out_a(c, st)
    if launch in ("full", "B", "B3", "Bdbg"):
        if launch == "B3":
            emit_attn_q(c, st)
        else:
            def ln_cb(g, c=c, st=st):
                ev = emit_ln_group(c, st, "xr", g // 4, g % 4,
                                   want_lnx=True, gate=True)
                st["smT_last_ev"] = ev["smT_last"]
            emit_attn_q(c, st, ln_cb=ln_cb)
            if launch == "full":
                emit_copy_y(c, st)
            st["skip_ln"] = True
            emit_ffn(c, st, 1)
    emit_final(c, st)
    c.emit()
    return c


def emit_setup(c, st):
    t = c.t
    evs = {}
    has_attw = c.launch in ("full", "A")
    if c.launch in ("full", "A"):
        c.op("sync", lambda s: s.dma_start(out=t["xr"][:, :], in_=t["x0"][:, :]),
             inc="stp", inc_n=16)
    wl = [("Wg", "wg_f")]
    if has_attw:
        wl = [("Wq", "wq_f"), ("Wk", "wk_f"), ("Wv", "wv_f")] + wl
    for w, f in wl:
        evs[f] = c.op("sync", lambda s, a=w, b=f: s.dma_start(
            out=t[b][:, :], in_=t[a][:, :]), inc="stp", inc_n=16)
    if has_attw:
        evs["bq"] = c.op("sync", lambda s: s.dma_start(
            out=t["bq32"][:, :], in_=t["bq"][:, :]), inc="stp", inc_n=16)
        c.op("sync", lambda s: s.dma_start(out=t["bk_sb"][:, :], in_=t["bk"][:, :]),
             inc="stp", inc_n=16)
        evs["bv"] = c.op("sync", lambda s: s.dma_start(
            out=t["bv_f"][:1, :], in_=t["bv"][:, :].rearrange("a o -> o a")),
            inc="stp", inc_n=16)
    evs["id"] = c.op("sync", lambda s: s.dma_start(
        out=t["ident_sb"][:, :], in_=t["ident"][:, :]), inc="stp", inc_n=16)
    for e in range(E):
        for fc in range(32):
            st["b1_ld"] = c.op("gpsimd", lambda g, e=e, fc=fc: g.dma_start(
                out=t["b1all"][:, e, fc:fc + 1],
                in_=t["b1"][e, fc * 128:(fc + 1) * 128].rearrange("(p o) -> p o", o=1)),
                inc="b1s", inc_n=16)
    stp_full = ("stp", c.cnt["stp"])
    st["stp_full"] = stp_full
    st["xt_bar"] = [stp_full]
    st["b1_ld"] = ("b1s", c.cnt["b1s"])
    if has_attw:
        st["bq_ok"] = c.op("vector", lambda v: v.tensor_single_scalar(
            t["bq32"][:, :], t["bq32"][:, :], QSCALE, OP.mult),
            waits=[stp_full], inc="sV")
        st["bv_ok"] = c.op("vector", lambda v: v.tensor_copy(
            t["bv_sb"][:1, :], t["bv_f"][:1, :]), waits=[stp_full], inc="sV")
    st["ones_ok"] = c.op("vector", lambda v: v.memset(t["ones_sb"][:1, :], 1.0), inc="sV")
    st["eps_ok"] = c.op("vector", lambda v: v.memset(t["epsc"][:, :], EPS), inc="sV")
    if "yT" in t:
        st["yT_ok"] = c.op("vector", lambda v: v.memset(t["yT"][:, :], 0.0), inc="sV")
    st["smT_init"] = c.op("vector", lambda v: v.memset(t["smT"][:, :], 0.0), inc="sV")

    st["p67_free"] = [None, None]
    st["p67i"] = 0
    pfree = st["p67_free"]
    wT = []
    wtl = [("wg_f", "WgT", E)]
    if has_attw:
        wtl = [("wq_f", "WqT", DH), ("wk_f", "WkT", DH), ("wv_f", "WvT", DH)] + wtl
    for mat, dstn, p in wtl:
        np_in = DH if mat != "wg_f" else E
        for hc in range(8):
            sl = st["p67i"] % 2
            st["p67i"] += 1
            tr = c.op("tensor", lambda te, m=mat, hc=hc, sl=sl, q=np_in: te.transpose(
                t[f"p{6 + sl}"][:, :q], t[m][:, hc * 128:(hc + 1) * 128],
                t["ident_sb"][:q, :q]),
                waits=[stp_full, pfree[sl]], inc="sT")
            pfree[sl] = c.op("scalar", lambda s, d=dstn, hc=hc, sl=sl, q=p:
                             s.copy(t[d][:, hc, :], t[f"p{6 + sl}"][:, :q]),
                             waits=[tr], inc="sA")
            wT.append(pfree[sl])
    st["wT_done"] = wT[-1]
    st["scat_last"] = {0: None, 1: None}
    st["gate_war"] = None
    st["smT_tr_war"] = None
    st["smT_war"] = None
    st["lnx_war"] = None


def emit_ln_group(c, st, src_name, b, grp, want_lnx, gate):
    t = c.t
    out = {}
    row0 = b * S + grp * 512
    ld, red, sqe = [], [], []
    for tl in range(4):
        w = list(st["xt_war"][tl]) + list(st["xt_bar"])
        ld.append(c.op("sync", lambda s, tl=tl: s.dma_start(
            out=t["xt"][:, tl, :],
            in_=t[src_name][row0 + tl * 128: row0 + (tl + 1) * 128, :]),
            waits=w, inc=f"xt{tl}", inc_n=16))
    for tl in range(4):
        red.append(c.op("vector", lambda v, tl=tl: v.reduce_sum(
            t["stat"][:, tl, 0:1], t["xt"][:, tl, :], axis=AX.X),
            waits=[ld[tl]], inc="sV"))
    for tl in range(4):
        sqe.append(c.op("scalar", lambda s, tl=tl: s.activation(
            t["sq"][:, :], t["xt"][:, tl, :], AF.Square,
            accum_out=t["stat"][:, tl, 1:2]),
            waits=[ld[tl], st.get("sq_war")], inc="sA"))
        st["sq_war"] = sqe[-1]
    var_ev = []
    for tl in range(4):
        m = c.op("vector", lambda v, tl=tl: v.tensor_single_scalar(
            t["stat"][:, tl, 2:3], t["stat"][:, tl, 0:1], 1.0 / H, OP.mult),
            waits=[red[tl]], inc="sV")
        msq = c.op("vector", lambda v, tl=tl: v.tensor_mul(
            t["stat"][:, tl, 3:4], t["stat"][:, tl, 2:3], t["stat"][:, tl, 2:3]),
            waits=[m], inc="sV")
        var_ev.append(c.op("vector", lambda v, tl=tl: v.scalar_tensor_tensor(
            t["stat"][:, tl, 4:5], t["stat"][:, tl, 1:2], 1.0 / H, t["stat"][:, tl, 3:4],
            op0=OP.mult, op1=OP.subtract), waits=[msq, sqe[tl]], inc="sV"))
    sd_ev = [c.op("scalar", lambda s, tl=tl: s.activation(
        t["stat"][:, tl, 5:6], t["stat"][:, tl, 4:5], AF.Sqrt, bias=t["epsc"][:, 0:1]),
        waits=[var_ev[tl], st["eps_ok"]], inc="sA") for tl in range(4)]
    ap_ev = []
    for tl in range(4):
        rc = c.op("vector", lambda v, tl=tl: v.reciprocal(
            t["stat"][:, tl, 6:7], t["stat"][:, tl, 5:6]), waits=[sd_ev[tl]], inc="sV")
        ap_ev.append(c.op("vector", lambda v, tl=tl: v.tensor_scalar(
            t["xt"][:, tl, :], t["xt"][:, tl, :], t["stat"][:, tl, 2:3],
            t["stat"][:, tl, 6:7], op0=OP.subtract, op1=OP.mult),
            waits=[rc], inc="sV"))
    out["apply"] = ap_ev
    lnx_w = [None] * 4
    if want_lnx:
        for tl in range(4):
            lnx_w[tl] = c.op("sync", lambda s, tl=tl: s.dma_start(
                out=t["lnx"][row0 + tl * 128: row0 + (tl + 1) * 128, :],
                in_=t["xt"][:, tl, :]),
                waits=[ap_ev[tl], st["lnx_war"]], inc=f"xt{tl}", inc_n=16)
        out["lnx_w"] = lnx_w
    evac = {}
    tr_last = {}
    pfree = st["p67_free"]
    for tl in range(4):
        for hc in range(8):
            sl = st["p67i"] % 2
            st["p67i"] += 1
            tr = c.op("tensor", lambda te, tl=tl, hc=hc, sl=sl: te.transpose(
                t[f"p{6 + sl}"][:, :128], t["xt"][:, tl, hc * 128:(hc + 1) * 128],
                t["ident_sb"][:, :]),
                waits=[ap_ev[tl], pfree[sl], st["wT_done"],
                       st.get(("w2evac", 6 + sl))], inc="sT")
            ev1 = c.op("scalar", lambda s, tl=tl, hc=hc, sl=sl: s.copy(
                t["lnT"][:, hc, tl * 128:(tl + 1) * 128], t[f"p{6 + sl}"][:, :128]),
                waits=[tr, st.get("lnT_war")], inc="sA")
            if gate:
                ev1 = c.op("scalar", lambda s, tl=tl, hc=hc, sl=sl: s.copy(
                    t["lnT32"][:, hc, :], t[f"p{6 + sl}"][:, :128]),
                    waits=[tr, st.get("gate_mm_war")], inc="sA")
            pfree[sl] = ev1
            evac[(tl, hc)] = ev1
            tr_last[tl] = tr
        if gate:
            tile_in_b = grp * 4 + tl
            gm = None
            for hc in range(8):
                gm = c.op("tensor", lambda te, tl=tl, hc=hc: te.matmul(
                    t["p4"][:, 0:E], t["lnT32"][:, hc, :],
                    t["WgT"][:, hc, :], start=(hc == 0), stop=(hc == 7)),
                    waits=[evac[(tl, hc)], st.get("gate_war"), st.get("gate_war2")],
                    inc="sT" if hc == 7 else None)
            st["gate_mm_war"] = gm
            ex = c.op("scalar", lambda s, tl=tl: s.activation(
                t["smtok"][:, tl, :], t["p4"][:, 0:E], AF.Exp,
                accum_out=t["stat"][:, tl, 7:8]), waits=[gm], inc="sA")
            st["gate_war"] = ex
            rcg = c.op("vector", lambda v, tl=tl: v.reciprocal(
                t["stat"][:, tl, 3:4], t["stat"][:, tl, 7:8]), waits=[ex], inc="sV")
            dv = c.op("vector", lambda v, tl=tl: v.tensor_single_scalar(
                t["smtok"][:, tl, :], t["smtok"][:, tl, :], t["stat"][:, tl, 3:4],
                OP.mult), waits=[rcg], inc="sV")
            str_ = c.op("tensor", lambda te, tl=tl: te.transpose(
                t["p5"][0:E, 0:128], t["smtok"][:, tl, :],
                t["ident_sb"][:, :]),
                waits=[dv, st.get("smT_tr_war"), st.get("p5_exp_war"),
                       st.get(("w2evac", 5))], inc="sT")
            smev = c.op("scalar", lambda s, tl=tl, tb=tile_in_b, b=b: s.copy(
                t["smT"][b * 32:b * 32 + E, tb * 128:(tb + 1) * 128],
                t["p5"][0:E, 0:128]),
                waits=[str_, st.get("smT_war"), st.get("smT_war2"),
                       st["smT_init"]], inc="sA")
            st["smT_tr_war"] = smev
            out["smT_last"] = smev
    st["p67_free"] = pfree
    out["evac"] = evac
    for tl in range(4):
        st["xt_war"][tl] = [tr_last[tl], lnx_w[tl]]
    st["xt_bar"] = []
    return out


def emit_topk(c, st, call):
    t = c.t
    cp = c.op("vector", lambda v: v.tensor_copy(t["smW"][:, :], t["smT"][:, :]),
              waits=[st["smT_last_ev"], st.get("smW_wl2")] + st.get("topk_war", []),
              inc="sV")
    st["smT_war"] = cp
    st["smT_free"] = cp
    prev = cp
    for it in range(KSEL // 8):
        sl = slice(it * 8, it * 8 + 8)
        mx = c.op("vector", lambda v, sl=sl: v.max(out=t["Gsb"][:, sl], in_=t["smW"][:, :]),
                  waits=[prev], inc="sV")
        mi = c.op("vector", lambda v, sl=sl: v.max_index(
            t["Isb"][:, sl], t["Gsb"][:, sl], t["smW"][:, :]), waits=[mx], inc="sV")
        prev = c.op("vector", lambda v, sl=sl: v.match_replace(
            out=t["smW"][:, :], in_to_replace=t["Gsb"][:, sl], in_values=t["smW"][:, :],
            imm_value=NEG), waits=[mi], inc="sV")
    st["smW_free"] = prev
    gs = c.op("vector", lambda v: v.tensor_single_scalar(
        t["Gsb"][:, :], t["Gsb"][:, :], 0.5, OP.mult), waits=[prev], inc="sV")
    for b in range(B):
        iw = c.op("sync", lambda s, b=b: s.dma_start(
            out=t["idxd"][b, :, :], in_=t["Isb"][b * 32:b * 32 + E, :]),
            waits=[prev], inc="tkw", inc_n=16)
        gw = c.op("sync", lambda s, b=b: s.dma_start(
            out=t["gd"][b, :, :], in_=t["Gsb"][b * 32:b * 32 + E, :]),
            waits=[gs], inc="tkw", inc_n=16)
    st["topk_war"] = [iw, gw]
    return iw, gw


def emit_ffn(c, st, call):
    t = c.t
    scd = t["xr" if call == 0 else "xr2"]

    if not st.pop("skip_ln", False):
        for b in range(B):
            for grp in range(4):
                ev = emit_ln_group(c, st, "xr", b, grp, want_lnx=True, gate=True)
                st["smT_last_ev"] = ev["smT_last"]
    emit_topk(c, st, call)
    tkw_full = ("tkw", c.cnt["tkw"])
    lnx_all = [(f"xt{i}", c.cnt[f"xt{i}"]) for i in range(4)]

    if c.stage < 2:
        st["ffn_done"] = [tkw_full]
        return

    w1_free = st.setdefault("w1_free4", [None] * 3)
    w2_free = st.setdefault("w2_free8", [None] * 8)
    geluevs = {}
    scat_last = st["scat_last"]
    contrib_free = st.get("contrib_free", [None] * 4)

    def w1_load(e, fcg):
        slot = fcg % 3
        la = c.op("gpsimd", lambda g, e=e, fcg=fcg, slot=slot: g.dma_start(
            out=t["W1sb"][:, slot, 0:4, :],
            in_=t["W1"][e, :, fcg * 512:(fcg + 1) * 512].rearrange(
                "(hc p) f -> p hc f", p=128)[:, 0:4, :]),
            waits=[w1_free[slot], st.get("smT_free")], inc=f"w1s{slot}", inc_n=16)
        lb = c.op("gpsimd", lambda g, e=e, fcg=fcg, slot=slot: g.dma_start(
            out=t["W1sb"][:, slot, 4:8, :],
            in_=t["W1"][e, :, fcg * 512:(fcg + 1) * 512].rearrange(
                "(hc p) f -> p hc f", p=128)[:, 4:8, :]),
            inc=f"w1s{slot}", inc_n=16)
        st[("w1ld", e, fcg)] = (la, lb)

    def w2_load(e, j):
        slot, hh, fc = j % 8, j // 32, j % 32
        st[("w2ld", e, j)] = st["last_w2ld"] = c.op(
            "gpsimd", lambda g, e=e, hh=hh, fc=fc, slot=slot: g.dma_start(
                out=t["W2sb"][:, slot, :],
                in_=t["W2"][e, fc * 128:(fc + 1) * 128, hh * 512:(hh + 1) * 512]),
            waits=[w2_free[slot], st.get("smW_free")], inc=f"w2s{slot}", inc_n=16)

    def idx_loads(e):
        par = e % 2
        for b in range(B):
            for sc in range(4):
                c.op("scalar", lambda s, b=b, sc=sc, e=e, par=par: s.dma_start(
                    out=t["idxcol"][:, par, b, sc:sc + 1],
                    in_=t["idxd"][b, e, sc * 128:(sc + 1) * 128]
                    .rearrange("(p o) -> p o", o=1)),
                    waits=[tkw_full] + st.get(("idx_free", par), []),
                    inc="dsc", inc_n=16)
                c.op("scalar", lambda s, b=b, sc=sc, e=e, par=par: s.dma_start(
                    out=t["ccol"][:, par, b, sc:sc + 1],
                    in_=t["gd"][b, e, sc * 128:(sc + 1) * 128]
                    .rearrange("(p o) -> p o", o=1)),
                    waits=[tkw_full] + st.get(("idx_free", par), []),
                    inc="dsc", inc_n=16)
        st[("dscf", e)] = ("dsc", c.cnt["dsc"])

    def gathers(e):
        par = e % 2
        prev_sg = ("sg", c.cnt["sg"])
        for b in range(B):
            for sc in range(4):
                w = [st[("dscf", e)], prev_sg, st.get("exT_tr_last"),
                     st["wT_done"]] + lnx_all
                c.op("gpsimd", lambda g, b=b, sc=sc, par=par: g.indirect_dma_start(
                    out=t["ex_raw"][:, b, sc, :], out_offset=None,
                    in_=t["lnx"][:, :],
                    in_offset=IndirectOffsetOnAxis(t["idxcol"][:, par, b, sc:sc + 1], 0),
                    element_offset=b * S * H),
                    waits=w, inc="sg", inc_n=16)
        st[("sgf", e)] = ("sg", c.cnt["sg"])

    def b2_load(e):
        st[("b2ld", e)] = c.op("gpsimd", lambda g, e=e: g.dma_start(
            out=t["b2row"][:1, e % 2, :], in_=t["b2"][e:e + 1, :]),
            waits=[st.get(("b2free", e % 2))], inc="b2s", inc_n=16)

    for fcg in range(3):
        w1_load(0, fcg)
    b2_load(0)
    idx_loads(0)
    gathers(0)
    for j in range(4):
        w2_load(0, j)

    for e in range(E):
        scat_all = [(f"sb{i}", c.cnt[f"sb{i}"]) for i in range(4)]
        par = e % 2
        sg_full = st[("sgf", e)]
        dsc_full = st[("dscf", e)]
        b2l = st[("b2ld", e)]
        pfree = st["p67_free"]
        exT_evac = {}
        tr = None
        for b in range(B):
            for sc in range(4):
                for hc in range(8):
                    sl = st["p67i"] % 2
                    st["p67i"] += 1
                    tr = c.op("tensor", lambda te, b=b, sc=sc, hc=hc, sl=sl: te.transpose(
                        t[f"p{6 + sl}"][:, :128],
                        t["ex_raw"][:, b, sc, hc * 128:(hc + 1) * 128],
                        t["ident_sb"][:, :]),
                        waits=[sg_full, pfree[sl], st.get(("w2evac", 6 + sl))],
                        inc="sT")
                    pfree[sl] = c.op("scalar", lambda s, b=b, sc=sc, hc=hc, sl=sl: s.copy(
                        t["exT"][:, b, hc, sc * 128:(sc + 1) * 128],
                        t[f"p{6 + sl}"][:, :128]),
                        waits=[tr], inc="sA")
                    exT_evac[(b, sc, hc)] = pfree[sl]
        st["exT_tr_last"] = tr
        last_w1 = None
        for fcg in range(8):
            if ("w1ld", e, fcg) not in st:
                w1_load(e, fcg)
            la, lb = st.pop(("w1ld", e, fcg))
            slot = fcg % 3
            for fcl in range(4):
                fc = fcg * 4 + fcl
                for b in range(B):
                    bank = 2 * (fc % 2) + b
                    if fc < 2:
                        bfree = st.get(("w2evac", bank))
                    else:
                        bfree = geluevs[(call, e, fc - 2, b)]
                    mm = None
                    for hc in range(8):
                        w = [exT_evac[(b, 3, hc)]]
                        if hc == 0:
                            w += [la, lb, bfree]
                        mm = c.op("tensor",
                                  lambda te, fcl=fcl, b=b, hc=hc, bank=bank, slot=slot:
                                  te.matmul(t[f"p{bank}"][:, :],
                                            t["W1sb"][:, slot, hc, fcl * 128:(fcl + 1) * 128],
                                            t["exT"][:, b, hc, :],
                                            start=(hc == 0), stop=(hc == 7)),
                                  waits=w, inc="sT" if hc == 7 else None)
                    last_w1 = mm
                    geluevs[(call, e, fc, b)] = c.op(
                        "scalar", lambda s, fc=fc, b=b, bank=bank, e=e: s.activation(
                            t["h1sb"][:, b, fc, :], t[f"p{bank}"][:, :], AF.Gelu_apprx_tanh,
                            bias=t["b1all"][:, e, fc:fc + 1]),
                        waits=[mm, st["b1_ld"], st.get("h1_war")], inc="sA")
            w1_free[slot] = last_w1
        for hh in range(2):
            mmstop = {}
            for fc in range(32):
                j = hh * 32 + fc
                if ("w2ld", e, j) not in st:
                    w2_load(e, j)
                wl2 = st.pop(("w2ld", e, j))
                slot = j % 8
                lastmm = None
                for b in range(B):
                    for sc in range(4):
                        bank = b * 4 + sc
                        if fc == 0:
                            if hh == 0 and bank < 4:
                                bfree = geluevs[(call, e, 30 + (bank >> 1), bank & 1)]
                            else:
                                bfree = st.get(("w2evac", bank))
                            c.op("tensor", lambda te, bank=bank, hh=hh, e=e: te.matmul(
                                t[f"p{bank}"][:, :], t["ones_sb"][:1, :],
                                t["b2row"][:1, e % 2, hh * 512:(hh + 1) * 512],
                                start=True, stop=False),
                                waits=[bfree, b2l, st["ones_ok"],
                                       geluevs[(call, e, 0, 0)]])
                        lastmm = c.op("tensor",
                                      lambda te, b=b, sc=sc, fc=fc, bank=bank, slot=slot:
                                      te.matmul(t[f"p{bank}"][:, :],
                                                t["h1sb"][:, b, fc, sc * 128:(sc + 1) * 128],
                                                t["W2sb"][:, slot, :],
                                                start=False, stop=(fc == 31)),
                                      waits=[wl2, geluevs[(call, e, fc, b)]],
                                      inc="sT" if (fc == 31 or (b == B - 1 and sc == 3))
                                      else None)
                        if fc == 31:
                            mmstop[(b, sc)] = lastmm
                w2_free[slot] = lastmm
            if hh == 0:
                for j2 in range(32, 36):
                    w2_load(e, j2)
                if e + 1 < E:
                    for fcg in range(3):
                        w1_load(e + 1, fcg)
                    b2_load(e + 1)
                    idx_loads(e + 1)
                    gathers(e + 1)
            scat_evs = st.setdefault(("scat_evs", par), [])
            if hh == 0:
                scat_evs.clear()
            for b in range(B):
                for sc in range(4):
                    bank = b * 4 + sc
                    buf = (b * 2 + hh * 4 + sc) % 4
                    ev = c.op("vector", lambda v, b=b, sc=sc, bank=bank, buf=buf, par=par:
                              v.tensor_single_scalar(t["contrib"][:, buf, :],
                                                     t[f"p{bank}"][:, :],
                                                     t["ccol"][:, par, b, sc:sc + 1],
                                                     OP.mult),
                              waits=[mmstop[(b, sc)], dsc_full, contrib_free[buf]],
                              inc="sV")
                    st[("w2evac", bank)] = ev
                    scat = c.op("gpsimd", lambda g, b=b, sc=sc, hh=hh, buf=buf, scd=scd,
                                par=par:
                                g.indirect_dma_start(
                                    out=scd[:, :],
                                    out_offset=IndirectOffsetOnAxis(
                                        t["idxcol"][:, par, b, sc:sc + 1], 0),
                                    in_=t["contrib"][:, buf, :], in_offset=None,
                                    element_offset=b * S * H + hh * 512,
                                    compute_op=OP.add),
                                waits=[ev] + scat_all + [st.get("ydone")],
                                inc=f"sb{buf}", inc_n=16)
                    contrib_free[buf] = scat
                    scat_last[b] = scat
                    scat_evs.append(scat)
            st[("b2free", e % 2)] = mmstop[(1, 3)]
        st[("idx_free", par)] = list(st[("scat_evs", par)])
    st["smW_wl2"] = st.get("last_w2ld")
    st["contrib_free"] = contrib_free
    st["h1_war"] = None
    st["ffn_done"] = [(f"sb{i}", c.cnt[f"sb{i}"]) for i in range(4)]


def emit_attn_kv(c, st):
    t = c.t
    st["xt_bar"] = list(st["ffn_done"])
    st["p0_att"] = st.get(("w2evac", 0))
    st["p1_att"] = st.get(("w2evac", 1))
    vfree = [st.get(("w2evac", 2)), st.get(("w2evac", 3))]
    kq_evac, vown_ev = [], []
    mm = ve = None
    for b in range(B):
        for grp in range(4):
            ev = emit_ln_group(c, st, "xr", b, grp, want_lnx=False, gate=False)
            evac = ev["evac"]
            col0 = b * S + grp * 512
            for name, bank, wtn in [("k", 0, "WkT"), ("q", 1, "WqT")]:
                mm = None
                for hc in range(8):
                    w = [evac[(3, hc)]] + ([st[f"p{bank}_att"]] if hc == 0 else [])
                    mm = c.op("tensor", lambda te, hc=hc, bank=bank, wtn=wtn: te.matmul(
                        t[f"p{bank}"][0:DH, :], t[wtn][:, hc, :], t["lnT"][:, hc, :],
                        start=(hc == 0), stop=(hc == 7)),
                        waits=w, inc="sT" if hc == 7 else None)
                if name == "k":
                    evk = c.op("scalar", lambda s, col0=col0: s.activation(
                        t["kT_own"][:, col0:col0 + 512], t["p0"][0:DH, :], AF.Identity,
                        bias=t["bk_sb"][:, 0:1]), waits=[mm], inc="sA")
                    st["p0_att"] = evk
                    kq_evac.append(evk)
                else:
                    evq = c.op("scalar", lambda s, col0=col0: s.activation(
                        t["qT_sb"][0:DH, col0:col0 + 512], t["p1"][0:DH, :], AF.Identity,
                        bias=t["bq32"][:, 0:1], scale=QSCALE),
                        waits=[mm, st["bq_ok"]], inc="sA")
                    evq = c.op("scalar", lambda s, col0=col0: s.activation(
                        t["qT_sb"][DH:2 * DH, col0:col0 + 512], t["p1"][0:DH, :],
                        AF.Identity, bias=t["bq32"][:, 0:1], scale=QSCALE),
                        waits=[mm, st["bq_ok"]], inc="sA")
                    st["p1_att"] = evq
                    kq_evac.append(evq)
            for tl in range(4):
                tile = col0 // 128 + tl
                par = tl % 2
                c.op("tensor", lambda te, par=par: te.matmul(
                    t[f"p{2 + par}"][:, 0:DH], t["ones_sb"][:1, :], t["bv_sb"][:1, :],
                    start=True, stop=False),
                    waits=[evac[(tl, 7)], vfree[par], st["bv_ok"], st["ones_ok"]])
                mm = None
                for hc in range(8):
                    mm = c.op("tensor", lambda te, tl=tl, hc=hc, par=par: te.matmul(
                        t[f"p{2 + par}"][:, 0:DH], t["lnT"][:, hc, tl * 128:(tl + 1) * 128],
                        t["WvT"][:, hc, :], start=False, stop=(hc == 7)),
                        inc="sT" if hc == 7 else None)
                ve = c.op("scalar", lambda s, tile=tile, par=par: s.copy(
                    t["v_own"][:, tile, :], t[f"p{2 + par}"][:, 0:DH]),
                    waits=[mm], inc="sA")
                vfree[par] = ve
                vown_ev.append(ve)
            st["lnT_war"] = mm
    st["kq_evac"] = kq_evac
    st["vown_ev"] = vown_ev
    st["vfree"] = vfree


def emit_out_a(c, st):
    t = c.t
    c.op("sync", lambda s: s.dma_start(
        out=t["kvo"][0:DH * T_OWN].rearrange("(p f) -> p f", p=DH),
        in_=t["kT_own"][:, :]), waits=st["kq_evac"], inc="ycw", inc_n=16)
    c.op("sync", lambda s: s.dma_start(
        out=t["kvo"][DH * T_OWN:2 * DH * T_OWN].rearrange(
            "(tile p f) -> p tile f", p=128, f=DH),
        in_=t["v_own"][:, :, :]), waits=st["vown_ev"], inc="ycw", inc_n=16)
    c.op("sync", lambda s: s.dma_start(
        out=t["kvo"][2 * DH * T_OWN:].rearrange("(p f) -> p f", p=DH),
        in_=t["qT_sb"][0:DH, :]), waits=st["kq_evac"], inc="ycw", inc_n=16)


def emit_attn_q(c, st, ln_cb=None):
    t = c.t
    NCH = c.t_all // 128
    kv_ok = []
    kslice, vms, qld = {}, {}, {}
    if c.launch in ("B", "B3", "Bdbg"):
        for half in range(2):
            c.op("sync", lambda s, half=half: s.dma_start(
                out=t["qT_sb"][half * DH:(half + 1) * DH, :], in_=t["qT_in"][:, :]),
                waits=[st["stp_full"], st["wT_done"]], inc="kvl", inc_n=16)
        qfull = ("kvl", c.cnt["kvl"])
        qld = {0: qfull, 1: qfull}
        NSLK = 4
        colk = ((c.t_all // 256) * 128) // NSLK
        chv = (c.t_all // 128) // NSLK
        for k in range(NSLK):
            ke = c.op("sync", lambda s, k=k, colk=colk: s.dma_start(
                out=t["kT_full"][:, k * colk:(k + 1) * colk],
                in_=t["kTf_in"][:, k * colk:(k + 1) * colk]),
                waits=[st["stp_full"], st["wT_done"]], inc=f"kvk{k}", inc_n=16)
            ve = c.op("sync", lambda s, k=k, chv=chv: s.dma_start(
                out=t["v_aug"][:, k * chv:(k + 1) * chv, 0:DH],
                in_=t["vf_in"][:, :].rearrange(
                    "p (c f) -> p c f", f=DH)[:, k * chv:(k + 1) * chv, :]),
                waits=[st["stp_full"], st["wT_done"]], inc=f"kvv{k}", inc_n=16)
            kslice[k] = ke
            vms[k] = c.op("vector", lambda v, k=k, chv=chv: v.memset(
                t["v_aug"][:, k * chv:(k + 1) * chv, DH:DH + 1], 1.0),
                waits=[ve], inc="sV")
        vfree = [None, None]
        yfree0 = None
        st["ytr_free"] = [None, None]
    else:
        e1 = c.op("sync", lambda s: s.dma_start(
            out=t["kvin"][0:DH * T_OWN].rearrange("(p f) -> p f", p=DH),
            in_=t["kT_own"][:, :]), waits=st["kq_evac"], inc="kvl", inc_n=16)
        e2 = c.op("sync", lambda s: s.dma_start(
            out=t["kvin"][DH * T_OWN:].rearrange("(tile p f) -> p tile f", p=128, f=DH),
            in_=t["v_own"][:, :, :]), waits=st["vown_ev"], inc="kvl", inc_n=16)
        cc = c.op("gpsimd", lambda g: g.collective_compute(
            "AllGather", OP.bypass, replica_groups=[list(range(c.ncore))],
            ins=[t["kvin"].ap().opt()], outs=[t["kvout"].ap().opt()]),
            waits=[("kvl", c.cnt["kvl"])], inc="scc", inc_n=16)
        for r in range(c.ncore):
            for half in range(2):
                kv_ok.append(c.op("sync", lambda s, r=r, half=half: s.dma_start(
                    out=t["kT_full"][half * DH:(half + 1) * DH,
                                     r * (T_OWN // 2):(r + 1) * (T_OWN // 2)],
                    in_=t["kvout"][r, 0:DH * T_OWN].rearrange(
                        "(p cp two f) -> two p cp f", p=DH, two=2, f=128)[half]),
                    waits=[cc] + st["ffn_done"], inc="kvl", inc_n=16))
            kv_ok.append(c.op("sync", lambda s, r=r: s.dma_start(
                out=t["v_aug"][:, r * 32:(r + 1) * 32, 0:DH],
                in_=t["kvout"][r, DH * T_OWN:].rearrange(
                    "(tile p f) -> p tile f", p=128, f=DH)),
                waits=[cc], inc="kvl", inc_n=16))
        vfree = st["vfree"]
        yfree0 = vfree[0]
        st["ytr_free"] = [vfree[1], None]
    if c.launch == "full":
        kvl_full = ("kvl", c.cnt["kvl"])
        vaug_ones = c.op("vector", lambda v: v.memset(t["v_aug"][:, :, DH:DH + 1], 1.0),
                         waits=[kvl_full], inc="sV")
        kv_ok = [kvl_full, vaug_ones]

    yfree = yfree0
    exp_ev = {}
    smm_ev = {}
    ymm_ev = {}
    SB = [0, 1, 5]
    ycp = None
    for g in range(T_OWN // 512):
        for ch in range(NCH + 2):
            if ln_cb is not None and g >= 1 and ch == min(48, NCH // 2):
                st["gate_war2"] = st["ytr_free"][1]
                lp5 = max(cc for cc in range(ch) if SB[cc % 3] == 5)
                st["p5_exp_war"] = exp_ev[(g, lp5)]
                ln_cb(g - 1)
            if ch < NCH:
                par = SB[ch % 3]
                w = (kv_ok if ch < 3 else []) + [exp_ev.get((g, ch - 3)),
                                                 kslice.get(ch // (NCH // 4)),
                                                 qld.get(ch % 2)]
                if ch < 3:
                    w = w + [st.get(("w2evac", par))]
                if par == 5:
                    w = w + [st.get("smT_tr_war")]
                smm_ev[(g, ch)] = c.op("tensor", lambda te, ch=ch, g=g, par=par: te.matmul(
                    t[f"p{par}"][:, :],
                    t["kT_full"][(ch % 2) * DH:(ch % 2) * DH + DH,
                                 (ch // 2) * 128:(ch // 2) * 128 + 128],
                    t["qT_sb"][(ch % 2) * DH:(ch % 2) * DH + DH,
                               g * 512:(g + 1) * 512],
                    start=True, stop=True),
                    waits=w, inc="sT")
            if ch >= 2:
                cy = ch - 2
                ymm_ev[(g, cy)] = c.op("tensor", lambda te, cy=cy: te.matmul(
                    t["p2"][0:DH + 1, :], t["v_aug"][:, cy, :],
                    t["E_sb"][:, cy % 2, :],
                    start=(cy == 0), stop=(cy == NCH - 1)),
                    waits=[exp_ev[(g, cy)], vms.get(cy // (NCH // 4))] +
                          ([yfree, st.get("yT_ok")] if cy == 0 else []),
                    inc="sT")
            if ch < NCH:
                exp_ev[(g, ch)] = c.op("scalar", lambda s, ch=ch, par=SB[ch % 3]:
                                       s.activation(
                    t["E_sb"][:, ch % 2, :], t[f"p{par}"][:, :], AF.Exp),
                    waits=[smm_ev[(g, ch)], ymm_ev.get((g, ch - 2))], inc="sA")
        ycp = c.op("scalar", lambda s: s.copy(t["yT"][0:DH + 1, :], t["p2"][0:DH + 1, :]),
                   waits=[ymm_ev[(g, NCH - 1)]], inc="sA")
        yfree = ycp
        for tl in range(4):
            par = tl % 2
            tr = c.op("tensor", lambda te, tl=tl, par=par: te.transpose(
                t[f"p{3 + par}"][:, :128], t["yT"][:, tl * 128:(tl + 1) * 128],
                t["ident_sb"][:, :]),
                waits=[ycp, st["ytr_free"][par],
                       (st.get("gate_war") if par == 1 else None)], inc="sT")
            cpy = c.op("vector", lambda v, tl=tl, par=par: v.tensor_copy(
                t["ytmp"][:, par, :], t[f"p{3 + par}"][:, 0:DH + 1]), waits=[tr], inc="sV")
            st["ytr_free"][par] = cpy
            rcy = c.op("vector", lambda v, par=par: v.reciprocal(
                t["dcol"][:, par:par + 1], t["ytmp"][:, par, DH:DH + 1]),
                waits=[cpy], inc="sV")
            dv = c.op("vector", lambda v, tl=tl, par=par: v.tensor_single_scalar(
                t["ytok"][:, tl, :], t["ytmp"][:, par, 0:DH], t["dcol"][:, par:par + 1],
                OP.mult), waits=[rcy], inc="sV")
            tile = g * 4 + tl
            row0 = tile * 128
            xsrc = "xr" if c.launch == "full" else "x0"
            ld = c.op("sync", lambda s, tl=tl, row0=row0, xsrc=xsrc: s.dma_start(
                out=t["xt"][:, tl, :], in_=t[xsrc][row0:row0 + 128, :]),
                waits=st["xt_war"][tl] + st["xt_bar"], inc=f"xt{tl}", inc_n=16)
            add = None
            for j in range(16):
                add = c.op("vector", lambda v, tl=tl, j=j: v.tensor_add(
                    t["xt"][:, tl, j * DH:(j + 1) * DH],
                    t["xt"][:, tl, j * DH:(j + 1) * DH], t["ytok"][:, tl, :]),
                    waits=[ld, dv] if j == 0 else [],
                    inc="sV" if j == 15 else None)
            wb = c.op("sync", lambda s, tl=tl, row0=row0: s.dma_start(
                out=t["xr"][row0:row0 + 128, :], in_=t["xt"][:, tl, :]),
                waits=[add], inc=f"wb{tl}", inc_n=16)
            st["xt_war"][tl] = [wb]
    if ln_cb is not None:
        gl = T_OWN // 512 - 1
        st["gate_war2"] = st["ytr_free"][1]
        last_p5 = max(ch for ch in range(NCH) if SB[ch % 3] == 5)
        st["p5_exp_war"] = exp_ev[(gl, last_p5)]
        ln_cb(gl)
    st["xt_bar"] = []
    st["att_done"] = [(f"wb{i}", c.cnt[f"wb{i}"]) for i in range(4)]
    gl = T_OWN // 512 - 1
    for par in (0, 1, 5):
        last_ch = max(ch for ch in range(NCH) if [0, 1, 5][ch % 3] == par)
        st[("w2evac", par)] = exp_ev[(gl, last_ch)]
    st[("w2evac", 2)] = ycp
    st[("w2evac", 3)] = st["ytr_free"][0]
    st[("w2evac", 4)] = st["ytr_free"][1]
    st["gate_war"] = st["ytr_free"][1]
    st["h1_war"] = ("sT", c.cnt["sT"])


def emit_copy_y(c, st):
    t = c.t
    dst = "y" if c.launch == "B3" else "xr2"
    ev = c.op("sync", lambda s, dst=dst: s.dma_start(
        out=t[dst][:, :], in_=t["xr"][:, :]),
        waits=st["att_done"], inc="ycw", inc_n=16)
    st["ydone"] = ev


def emit_final(c, st):
    t = c.t
    if c.launch == "Bdbg":
        fw = list(st.get("ffn_done", []))
        c.op("sync", lambda s: s.dma_start(
            out=t["d_ex"][:, :], in_=t["ex_raw"].reshape([128, B * 4 * H])[:, :]),
            waits=fw, inc="dbg", inc_n=16)
        c.op("sync", lambda s: s.dma_start(
            out=t["d_h1"][:, :], in_=t["h1sb"].reshape([128, B * 32 * 512])[:, :]),
            waits=fw, inc="dbg", inc_n=16)
        c.op("sync", lambda s: s.dma_start(
            out=t["d_ext"][:, :], in_=t["exT"].reshape([128, B * 8 * 512])[:, :]),
            waits=fw, inc="dbg", inc_n=16)
        c.op("sync", lambda s: s.dma_start(out=t["d_idx"][:, :, :], in_=t["idxd"][:, :, :]),
             waits=[("tkw", c.cnt["tkw"])], inc="dbg", inc_n=16)
        c.op("sync", lambda s: s.dma_start(out=t["d_g"][:, :, :], in_=t["gd"][:, :, :]),
             waits=[("tkw", c.cnt["tkw"])], inc="dbg", inc_n=16)
        c.op("sync", lambda s: s.dma_start(out=t["d_lnx"][:, :], in_=t["lnx"][:, :]),
             waits=[("sg", c.cnt["sg"])] + [(f"xt{i}", c.cnt[f"xt{i}"]) for i in range(4)],
             inc="dbg", inc_n=16)
        c.op("sync", lambda s: s.dma_start(out=t["d_sm"][0, :, :], in_=t["smT"][:, :]),
             waits=[("sA", c.cnt["sA"])], inc="dbg", inc_n=16)
    if c.launch == "full":
        c.op("sync", lambda s: s.dma_start(out=t["y"][:, :], in_=t["xr2"][:, :]),
             waits=list(st.get("ffn_done", [])), inc="ycw", inc_n=16)
    if c.stage < 4:
        w = list(st.get("ffn_done", [])) + list(st.get("att_done", []))
        c.op("sync", lambda s: s.dma_start(out=t["y"][:, :], in_=t["xr"][:, :]),
             waits=[x for x in w if x], inc="ycw", inc_n=16)
        if "d_lnx" in t:
            c.op("sync", lambda s: s.dma_start(out=t["d_lnx"][:, :], in_=t["lnx"][:, :]),
                 waits=[("sg", c.cnt["sg"])] + [(f"xt{i}", c.cnt[f"xt{i}"]) for i in range(4)],
                 inc="dbg", inc_n=16)
    allsems = (["stp", "b1s", "lnx", "tkw", "dsc"] +
               [f"kvk{i}" for i in range(4)] + [f"kvv{i}" for i in range(4)] +
               [f"w1s{i}" for i in range(3)] + [f"w2s{i}" for i in range(8)] +
               ["b2s", "sg", "scc", "kvl", "ycw", "dbg",
                "xt0", "xt1", "xt2", "xt3", "wb0", "wb1", "wb2", "wb3",
                "sb0", "sb1", "sb2", "sb3"])
    fin = [(sname, c.cnt[sname]) for sname in allsems if c.cnt[sname] > 0]
    c.op("sync", lambda s: s.nop(), waits=fin)
    c.op("gpsimd", lambda g: g.nop(), waits=fin)


_CACHE = {}


def _relayout_kv(kvo_list, ncore):
    t_all = ncore * T_OWN
    kT = [k[0:DH * T_OWN].reshape(DH, T_OWN) for k in kvo_list]
    vv = [k[DH * T_OWN:2 * DH * T_OWN].reshape(T_OWN, DH) for k in kvo_list]
    qT = [k[2 * DH * T_OWN:].reshape(DH, T_OWN) for k in kvo_list]
    kTg = np.concatenate(kT, axis=1)
    vg = np.concatenate(vv, axis=0)
    nch = t_all // 128
    kTf = np.zeros((128, t_all // 2), dtype=kTg.dtype)
    for ch in range(nch):
        half, colblk = ch % 2, ch // 2
        kTf[half * DH:(half + 1) * DH, colblk * 128:(colblk + 1) * 128] = \
            kTg[:, ch * 128:(ch + 1) * 128]
    vf = np.ascontiguousarray(
        vg.reshape(nch, 128, DH).transpose(1, 0, 2).reshape(128, nch * DH))
    return kTf, vf, qT


def kernel(L0_attn_embeddings, L0_attn_Wq, L0_attn_bq, L0_attn_Wk, L0_attn_bk,
           L0_attn_Wv, L0_attn_bv, L0_ffn_Wg, L0_ffn_W1, L0_ffn_b1, L0_ffn_W2,
           L0_ffn_b2):
    from concourse.bass_utils import run_bass_kernel_spmd
    ncore = 8
    x = np.ascontiguousarray(np.asarray(L0_attn_embeddings, dtype=np.float32))
    if "A" not in _CACHE:
        _CACHE["A"] = build(stage=4, ncore=ncore, launch="A")
        _CACHE["B"] = build(stage=4, ncore=ncore, launch="B")
    cA, cB = _CACHE["A"], _CACHE["B"]
    common = dict(
        Wg=np.ascontiguousarray(L0_ffn_Wg, dtype=np.float32),
        W1=np.ascontiguousarray(L0_ffn_W1, dtype=np.float32),
        b1=np.ascontiguousarray(L0_ffn_b1, dtype=np.float32),
        W2=np.ascontiguousarray(L0_ffn_W2, dtype=np.float32),
        b2=np.ascontiguousarray(L0_ffn_b2, dtype=np.float32),
    )
    commonA = dict(common,
                   Wq=np.ascontiguousarray(L0_attn_Wq, dtype=np.float32),
                   bq=np.ascontiguousarray(np.asarray(L0_attn_bq, np.float32).reshape(DH, 1)),
                   Wk=np.ascontiguousarray(L0_attn_Wk, dtype=np.float32),
                   bk=np.ascontiguousarray(np.asarray(L0_attn_bk, np.float32).reshape(DH, 1)),
                   Wv=np.ascontiguousarray(L0_attn_Wv, dtype=np.float32),
                   bv=np.ascontiguousarray(np.asarray(L0_attn_bv, np.float32).reshape(DH, 1)))
    in_maps = [dict(commonA, x0=np.ascontiguousarray(
        x[2 * i:2 * i + 2].reshape(T_OWN, H))) for i in range(ncore)]
    resA = run_bass_kernel_spmd(cA.nc, in_maps, core_ids=list(range(ncore)))
    kvo_list = [np.asarray(r["kvo"]).ravel() for r in resA.results]
    kTf, vf, qTl = _relayout_kv(kvo_list, ncore)
    in_mapsB = [dict(common,
                     x0=np.ascontiguousarray(
                         np.asarray(resA.results[i]["x1o"], np.float32).reshape(T_OWN, H)),
                     kTf_in=np.ascontiguousarray(kTf),
                     vf_in=np.ascontiguousarray(vf),
                     qT_in=np.ascontiguousarray(qTl[i]))
                for i in range(ncore)]
    resB = run_bass_kernel_spmd(cB.nc, in_mapsB, core_ids=list(range(ncore)))
    out = np.concatenate(
        [np.asarray(r["y"], dtype=np.float32).reshape(B, S, H) for r in resB.results],
        axis=0)
    return out
